# version 01
# baseline (speedup 1.0000x reference)
import numpy as np
import ml_dtypes
from contextlib import ExitStack

import concourse.bass as bass
import concourse.mybir as mybir
from concourse.bass_utils import run_bass_kernel_spmd

F32 = mybir.dt.float32
BF16 = mybir.dt.bfloat16
I32 = mybir.dt.int32
U8 = mybir.dt.uint8
AF = mybir.ActivationFunctionType
ALU = mybir.AluOpType
AX = mybir.AxisListType

B, S, D, H, HD = 2, 8192, 2048, 16, 128
HPC = 4
NKC = D // 128
TC = 2048
TTA = 512
NTA = S // TTA
NQT = S // 128
E, IM, ISH = 8, 1024, 4096
SW, TSW = 2, 1024
INV_SQRT_HD = float(1.0 / np.sqrt(HD))

ENGS = ("tensor", "vector", "scalar", "gpsimd", "sync")
DEBUG_TAPS = []


class Sem:
    def __init__(self, h):
        self.h = h
        self.n = 0

    def inc(self, k=1):
        self.n += k
        return self.n


class Builder:
    SELF_EDGE = {"vector", "scalar", "gpsimd"}
    ASYNC_METHS = {"dma_start", "collective_compute", "indirect_dma_start"}

    def __init__(self, selfsems=None):
        self.plan = {e: [] for e in ENGS}
        self.R = {}
        self.F = {}
        self.selfsems = selfsems or {}

    def op(self, eng, meth, *a, inc=None, k=1, **kw):
        if eng in self.SELF_EDGE and meth not in self.ASYNC_METHS:
            sem = self.selfsems[eng]
            if sem.n > 0:
                self.wait(eng, sem, sem.n)
            if inc is None:
                inc, k = sem, 1
            else:
                assert inc is sem, (eng, meth)
        if inc is None:
            self.plan[eng].append(lambda e, m=meth, aa=a, kk=kw: getattr(e, m)(*aa, **kk))
            return None
        v = inc.inc(k)
        self.plan[eng].append(
            lambda e, m=meth, aa=a, kk=kw, s=inc.h, n=k: getattr(e, m)(*aa, **kk).then_inc(s, n))
        return v

    def dma(self, eng, qsem, **kw):
        self.wait(eng, qsem, qsem.n)
        return self.op(eng, "dma_start", inc=qsem, k=16, **kw)

    def wait(self, eng, sem, val):
        if val and val > 0:
            self.plan[eng].append(lambda e, s=sem.h, v=val: e.wait_ge(s, v))

    def raw(self, eng, fn):
        self.plan[eng].append(fn)

    def wR(self, eng, *names):
        for n in names:
            g = self.R.get(n)
            if g is not None:
                self.wait(eng, g[0], g[1])

    def wF(self, eng, *names):
        for n in names:
            g = self.F.get(n)
            if g is not None:
                self.wait(eng, g[0], g[1])

    def sR(self, name, sem, val):
        self.R[name] = (sem, val)

    def sF(self, name, sem, val):
        self.F[name] = (sem, val)


def build_nc(debug_taps=()):
    nc = bass.Bass("TRN2", target_bir_lowering=False)

    xa = nc.dram_tensor("xa", [NTA, 128, NKC * TTA], BF16, kind="ExternalInput")
    xr = nc.dram_tensor("xr", [128, NKC * TC], BF16, kind="ExternalInput")
    wqkv = nc.dram_tensor("wqkv", [128, NKC * HPC * 3 * 128], BF16, kind="ExternalInput")
    bqkv = nc.dram_tensor("bqkv", [128, HPC * 3], BF16, kind="ExternalInput")
    wout = nc.dram_tensor("wout", [128, NKC * D], BF16, kind="ExternalInput")
    bout = nc.dram_tensor("bout", [128, NKC], BF16, kind="ExternalInput")
    wg = nc.dram_tensor("wg", [128, NKC * E], F32, kind="ExternalInput")
    wgu = nc.dram_tensor("wgu", [E, 16, 128, NKC * 128], F32, kind="ExternalInput")
    wdn = nc.dram_tensor("wdn", [E, 8, 128, 8 * 256], F32, kind="ExternalInput")
    wsgu = nc.dram_tensor("wsgu", [64, 128, NKC * 128], F32, kind="ExternalInput")
    wsdn = nc.dram_tensor("wsdn", [4, 8, 128, 8 * 256], F32, kind="ExternalInput")
    coff = nc.dram_tensor("coff", [1, 1], I32, kind="ExternalInput")
    out_d = nc.dram_tensor("out", [D, TC], F32, kind="ExternalOutput")

    ag_in = nc.dram_tensor("ag_in", [HPC * 128, S], BF16)
    ag_out = nc.dram_tensor("ag_out", [D, S], BF16)
    x1d = nc.dram_tensor("x1d", [NKC, 128, TC], BF16)
    cmb_d = nc.dram_tensor("cmb_d", [8, TC], F32)

    dbg = {}
    for name, shape, dt in debug_taps:
        dbg[name] = nc.dram_tensor(name, shape, dt, kind="ExternalOutput")

    with ExitStack() as es:
        ARENA = 210944
        arena = es.enter_context(nc.sbuf_tensor("arena", [128, ARENA], U8))
        abase = nc.lookup_mloc(arena).addr

        def sb(name, shape, dtype, off):
            per = int(np.prod(shape[1:])) * mybir.dt.size(dtype)
            assert off + per <= ARENA, (name, off, per)
            return nc.alloc_sbuf_tensor_at(name, shape, dtype, offset=abase + off)

        ident_bf = sb("ident_bf", [128, 128], BF16, 0)
        ident_f = sb("ident_f", [128, 128], F32, 256)
        cmask = sb("cmask", [128, 128], F32, 768)
        ones_c_bf = sb("ones_c_bf", [128, 1], BF16, 1280)
        ones_c_f = sb("ones_c_f", [128, 1], F32, 1312)
        ones_r_f = sb("ones_r_f", [1, 128], F32, 1344)
        bq_bf = sb("bq_bf", [128, HPC * 3], BF16, 1856)
        bq_f = sb("bq_f", [128, HPC * 3], F32, 1888)
        bo_bf = sb("bo_bf", [128, NKC], BF16, 1952)
        bo_f = sb("bo_f", [128, NKC], F32, 1984)
        wg_f = sb("wg_f", [128, NKC * E], F32, 2048)
        wg_hi = sb("wg_hi", [128, NKC * E], BF16, 2560)
        wg_lo = sb("wg_lo", [128, NKC * E], BF16, 2816)
        eps12 = sb("eps12", [128, 1], F32, 205824)
        eps5 = sb("eps5", [128, 1], F32, 205856)
        coff_sb = sb("coff_sb", [1, 1], I32, 205888)

        wqkv_sb = sb("wqkv_sb", [128, NKC * 1536], BF16, 3072)
        xt = [sb(f"xt{i}", [128, NKC * TTA], BF16, 52224 + i * 16384) for i in range(2)]
        QT = sb("QT", [128, S], BF16, 84992)
        KT = sb("KT", [128, S], BF16, 101376)
        VT = sb("VT", [128, S], BF16, 117760)
        Vtok = sb("Vtok", [128, S], BF16, 134144)
        avT_sb = sb("avT_sb", [128, S], BF16, 150528)
        p_sb = [sb(f"p_sb{i}", [128, 1024], BF16, 166912 + i * 2048) for i in range(2)]
        pt_sb = [sb(f"pt_sb{i}", [128, 1024], BF16, 171008 + i * 2048) for i in range(2)]
        sq_tmp = [sb(f"sq_tmp{i}", [128, 512], BF16, 175104 + i * 1024) for i in range(2)]
        row_buf = sb("row_buf", [1, 512], F32, 177152)
        braw = sb("braw", [128, NQT], F32, 179200)
        bias_cols = sb("bias_cols", [128, NQT], F32, 179456)
        km_tmp = sb("km_tmp", [128, 1], F32, 179712)
        kmax_col = sb("kmax_col", [128, 1], F32, 179744)
        km_s = sb("km_s", [1, 1], F32, 179776)
        lparts = sb("lparts", [128, 16], F32, 179808)
        lsum = sb("lsum", [128, 1], F32, 179872)
        linv = sb("linv", [128, 1], F32, 179904)
        av_bf = sb("av_bf", [128, 128], BF16, 179936)

        wout_sb = sb("wout_sb", [128, NKC * D], BF16, 3072)
        attT = sb("attT", [128, NKC * 512], BF16, 68608)
        xr_t = sb("xr_t", [128, NKC * 512], BF16, 84992)
        r1 = sb("r1", [128, NKC * TC], BF16, 101376)

        srow = sb("srow", [1, TC], F32, 3072)
        sqrow = sb("sqrow", [1, TC], F32, 11264)
        mean_r = sb("mean_r", [1, TC], F32, 19456)
        rstd_r = sb("rstd_r", [1, TC], F32, 27648)
        mrstd_r = sb("mrstd_r", [1, TC], F32, 35840)
        rstd_b = sb("rstd_b", [128, TC], F32, 44032)
        mrstd_b = sb("mrstd_b", [128, TC], F32, 52224)
        lnt = sb("lnt", [128, 512], F32, 60416)
        x1s = [sb(f"x1s{i}", [128, 512], BF16, 62464 + i * 1024) for i in range(2)]

        lg_sb = sb("lg_sb", [8, TC], F32, 68608)
        ltok = sb("ltok", [128, NKC * 8], F32, 76800)
        m1c = sb("m1c", [128, 1], F32, 77312)
        nm1c = sb("nm1c", [128, 1], F32, 77344)
        m2c = sb("m2c", [128, 1], F32, 77376)
        e2c = sb("e2c", [128, 1], F32, 77408)
        dinvc = sb("dinvc", [128, 1], F32, 77440)
        msk8 = sb("msk8", [128, 8], F32, 77472)
        e1t = sb("e1t", [128, 8], F32, 77504)
        cmb_r = sb("cmb_r", [1, TSW], F32, 201728)

        ffn_acc = sb("ffn_acc", [128, NKC * TSW], F32, 3072)
        x1_sw = sb("x1_sw", [128, NKC * TSW], BF16, 68608)
        wstage = [sb(f"wstage{i}", [128, 2048], F32, 101376 + i * 8192) for i in range(2)]
        w_hi = [sb(f"w_hi{i}", [128, 2048], BF16, 117760 + i * 4096) for i in range(2)]
        w_lo = [sb(f"w_lo{i}", [128, 2048], BF16, 125952 + i * 4096) for i in range(2)]
        silu_g = sb("silu_g", [128, TSW], F32, 134144)
        act_f = sb("act_f", [128, TSW], F32, 138240)
        act_hi = sb("act_hi", [128, 8 * TSW], BF16, 142336)
        act_lo = sb("act_lo", [128, 8 * TSW], BF16, 158720)
        cmb_b = sb("cmb_b", [128, TSW], F32, 175104)
        ost = [sb(f"ost{i}", [128, 512], F32, 179200 + i * 2048) for i in range(2)]
        sq2 = [sb("sq2_0", [128, 512], F32, 183296)] * 2
        s2row = sb("s2row", [1, 512], F32, 185344)
        sq2row = sb("sq2row", [1, 512], F32, 187392)
        mean2_r = sb("mean2_r", [1, 512], F32, 189440)
        rstd2_r = sb("rstd2_r", [1, 512], F32, 191488)
        mrstd2_r = sb("mrstd2_r", [1, 512], F32, 193536)
        rstd2_b = sb("rstd2_b", [128, 512], F32, 195584)
        mrstd2_b = sb("mrstd2_b", [128, 512], F32, 197632)
        t2f = sb("t2f", [128, 512], F32, 199680)

        pa = es.enter_context(nc.psum_tensor("parena", [128, 4096], F32))

        def pbank(k, n=1):
            return pa[:, k * 512:(k + n) * 512]

        sems = {n: Sem(es.enter_context(nc.semaphore(n)))
                for n in ("pe", "act", "dve", "pool", "dsy", "dsc", "dgp", "cc")}
        pe, act, dve, pool = sems["pe"], sems["act"], sems["dve"], sems["pool"]
        dsy, dsc, dgp, cc = sems["dsy"], sems["dsc"], sems["dgp"], sems["cc"]

        bd = Builder({"vector": dve, "scalar": act, "gpsimd": pool})
        op, W, wR, wF, sR, sF = bd.op, bd.wait, bd.wR, bd.wF, bd.sR, bd.sF

        op("gpsimd", "memset", ident_bf[:, :], 0.0)
        op("gpsimd", "affine_select", out=ident_bf[:, :], in_=ident_bf[:, :],
           compare_op=ALU.not_equal, fill=1.0, base=0, pattern=[[-1, 128]], channel_multiplier=1)
        op("gpsimd", "memset", ident_f[:, :], 0.0)
        op("gpsimd", "affine_select", out=ident_f[:, :], in_=ident_f[:, :],
           compare_op=ALU.not_equal, fill=1.0, base=0, pattern=[[-1, 128]], channel_multiplier=1)
        op("gpsimd", "memset", cmask[:, :], 0.0)
        op("gpsimd", "affine_select", out=cmask[:, :], in_=cmask[:, :],
           compare_op=ALU.is_ge, fill=-1e30, base=0, pattern=[[-1, 128]], channel_multiplier=1)
        op("gpsimd", "memset", ones_c_bf[:, :], 1.0)
        op("gpsimd", "memset", ones_c_f[:, :], 1.0)
        op("gpsimd", "memset", ones_r_f[:, :], 1.0)
        op("gpsimd", "memset", eps12[:, :], 1e-12)
        v = op("gpsimd", "memset", eps5[:, :], 1e-5, inc=pool)
        sR("consts", pool, v)

        bd.dma("gpsimd", dgp, out=bq_bf[:, :], in_=bqkv[:, :])
        bd.dma("gpsimd", dgp, out=bo_bf[:, :], in_=bout[:, :])
        bd.dma("gpsimd", dgp, out=wg_f[:, :], in_=wg[:, :])
        v = bd.dma("gpsimd", dgp, out=coff_sb[:, :], in_=coff[:, :])
        sR("smallin", dgp, v)
        W("vector", dgp, v)
        op("vector", "tensor_copy", out=bq_f[:, :], in_=bq_bf[:, :])
        op("vector", "tensor_copy", out=bo_f[:, :], in_=bo_bf[:, :])
        v = op("vector", "tensor_copy", out=wg_hi[:, :], in_=wg_f[:, :], inc=dve)
        sR("wg_hi", dve, v)
        W("gpsimd", dve, v)
        v = op("gpsimd", "tensor_tensor", out=wg_lo[:, :], in0=wg_f[:, :], in1=wg_hi[:, :],
               op=ALU.subtract, inc=pool)
        sR("wg_lo", pool, v)

        v = bd.dma("sync", dsy, out=wqkv_sb[:, :], in_=wqkv[:, :])
        sR("wqkv", dsy, v)

        SC = [("scA0", pbank(0, 2)), ("scA1", pbank(2, 2))]
        PTP = [("ptA0", pbank(4).bitcast(BF16)), ("ptA1", pbank(5).bitcast(BF16))]
        AVP = ("avA", pbank(6)[:, 0:128])
        BIP = ("biA", pbank(6)[:, 128:192])
        AVTP = ("avTA", pbank(7).bitcast(BF16)[:, 0:128])
        Q2P = ("q2A", pbank(7)[:, 0:512])
        KMP = ("kmA", pbank(7)[:, 0:128])
        QKVP = [[(f"qkv{w}{p}", pbank(w * 2 + p)) for p in range(2)] for w in range(3)]

        for h in range(HPC):
            wF("scalar", "xt0")
            v = bd.dma("scalar", dsc, out=xt[0][:, :], in_=xa[0, :, :])
            sR("xt0", dsc, v)
            for tt in range(NTA):
                par = tt % 2
                xtile = xt[par]
                if tt + 1 < NTA:
                    npar = (tt + 1) % 2
                    wF("scalar", f"xt{npar}")
                    v = bd.dma("scalar", dsc, out=xt[npar][:, :], in_=xa[tt + 1, :, :])
                    sR(f"xt{npar}", dsc, v)
                W("tensor", dsc, bd.R[f"xt{par}"][1])
                if h == 0 and tt == 0:
                    W("tensor", dsy, bd.R["wqkv"][1])
                for w in range(3):
                    bname, psum = QKVP[w][par]
                    wF("tensor", bname)
                    for kc in range(NKC):
                        col = kc * 1536 + h * 384 + w * 128
                        v = op("tensor", "matmul", psum[:, :],
                               wqkv_sb[:, col:col + 128], xtile[:, kc * TTA:(kc + 1) * TTA],
                               start=(kc == 0), stop=(kc == NKC - 1),
                               inc=(pe if kc == NKC - 1 else None))
                    sR(bname, pe, v)
                    if w == 2:
                        sF(f"xt{par}", pe, v)
                for w, dst in enumerate((QT, KT, VT)):
                    bname, psum = QKVP[w][par]
                    W("scalar", pe, bd.R[bname][1])
                    if h == 0 and tt == 0 and w == 0:
                        W("scalar", dve, bd.R["wg_hi"][1])
                    v = op("scalar", "activation", dst[:, tt * TTA:(tt + 1) * TTA], psum[:, :],
                           AF.Identity, bias=bq_f[:, h * 3 + w:h * 3 + w + 1], scale=1.0,
                           inc=act)
                    sF(bname, act, v)
                    if tt == NTA - 1:
                        sR("QT KT VT".split()[w], act, v)

            for r in range(8):
                par = r % 2
                pname, ptp = PTP[par]
                W("tensor", act, bd.R["VT"][1])
                wF("tensor", pname)
                for j in range(8):
                    kc = r * 8 + j
                    v = op("tensor", "transpose", ptp[:, j * 128:(j + 1) * 128],
                           VT[:, kc * 128:(kc + 1) * 128], ident_bf[:, :],
                           inc=(pe if j == 7 else None))
                sR(pname, pe, v)
                W("vector", pe, v)
                v = op("vector", "tensor_copy", out=Vtok[:, r * 1024:(r + 1) * 1024],
                       in_=ptp[:, :], inc=dve)
                sF(pname, dve, v)
                if r == 7:
                    sR("Vtok", dve, v)

            for c in range(NKC):
                par = c % 2
                W("scalar", act, bd.R["QT"][1])
                wF("scalar", f"sq{par}")
                v = op("scalar", "activation", sq_tmp[par][:, :], QT[:, c * 512:(c + 1) * 512],
                       AF.Square, bias=0.0, scale=1.0, inc=act)
                sR(f"sq{par}", act, v)
                W("tensor", act, v)
                wF("tensor", "q2A")
                if c == 0:
                    wF("tensor", "avTA")
                if h == 0 and c == 0:
                    W("tensor", pool, bd.R["consts"][1])
                v = op("tensor", "matmul", Q2P[1][0:1, :], ones_c_bf[:, :], sq_tmp[par][:, :],
                       start=True, stop=True, inc=pe)
                sR("q2A", pe, v)
                sF(f"sq{par}", pe, v)
                W("vector", pe, v)
                wF("vector", "row_buf")
                v = op("vector", "tensor_copy", out=row_buf[:, :], in_=Q2P[1][0:1, :], inc=dve)
                sR("row_buf", dve, v)
                sF("q2A", dve, v)
                W("tensor", dve, v)
                for j in range(4):
                    qt = c * 4 + j
                    v = op("tensor", "transpose", BIP[1][:, qt:qt + 1],
                           row_buf[0:1, j * 128:(j + 1) * 128], ident_f[0:1, 0:1],
                           inc=(pe if j == 3 else None))
                sF("row_buf", pe, v)
            sR("biA", pe, v)
            W("vector", act, bd.R["KT"][1])
            for c in range(NKC):
                if c == 0:
                    op("vector", "reduce_max", out=kmax_col[:, :], in_=KT[:, c * 512:(c + 1) * 512],
                       axis=AX.X, apply_absolute_value=True)
                else:
                    op("vector", "reduce_max", out=km_tmp[:, :], in_=KT[:, c * 512:(c + 1) * 512],
                       axis=AX.X, apply_absolute_value=True)
                    lastv = op("vector", "tensor_tensor", out=kmax_col[:, :], in0=kmax_col[:, :],
                               in1=km_tmp[:, :], op=ALU.max,
                               inc=(dve if c == NKC - 1 else None))
            sR("kmax_col", dve, lastv)
            W("tensor", dve, lastv)
            wF("tensor", "kmA")
            v = op("tensor", "transpose", KMP[1][0:1, 0:128], kmax_col[:, :], ident_f[:, :],
                   inc=pe)
            sR("kmA", pe, v)
            W("vector", pe, v)
            op("vector", "reduce_max", out=km_s[:, :], in_=KMP[1][0:1, 0:128], axis=AX.X)
            v = op("vector", "tensor_tensor", out=km_s[:, :], in0=km_s[:, :], in1=km_s[:, :],
                   op=ALU.mult, inc=dve)
            sR("km2", dve, v)
            sF("kmA", dve, v)
            W("tensor", dve, v)
            wF("tensor", "kmA")
            v = op("tensor", "matmul", KMP[1][:, 0:1], ones_r_f[:, :], km_s[:, :],
                   start=True, stop=True, inc=pe)
            sR("km2b", pe, v)
            W("vector", pe, v)
            v = op("vector", "tensor_copy", out=kmax_col[:, :], in_=KMP[1][:, 0:1], inc=dve)
            sR("k2col", dve, v)
            sF("kmA", dve, v)
            W("vector", pe, bd.R["biA"][1])
            op("vector", "tensor_copy", out=braw[:, :], in_=BIP[1][:, :])
            v = op("vector", "tensor_scalar", out=braw[:, :], in0=braw[:, :],
                   scalar1=kmax_col[:, :], scalar2=1.0 / 128.0, op0=ALU.mult, op1=ALU.mult,
                   inc=dve)
            sF("biA", dve, v)
            W("scalar", dve, v)
            op("scalar", "activation", braw[:, :], braw[:, :], AF.Sqrt, bias=eps12[:, :], scale=1.0)
            v = op("scalar", "activation", bias_cols[:, :], braw[:, :], AF.Copy, bias=0.0,
                   scale=-1.0, inc=act)
            sR("bias_cols", act, v)

            W("tensor", act, bd.R["bias_cols"][1])
            W("tensor", dve, bd.R["Vtok"][1])
            for qt in range(NQT):
                ncols = (qt + 1) * 128
                npairs = (ncols + 1023) // 1024
                for kp in range(npairs):
                    par = kp % 2
                    scname, scp = SC[par]
                    cols = min(1024, ncols - kp * 1024)
                    kbase = kp * 1024
                    wF("tensor", scname)
                    nm = (cols + 511) // 512
                    for j in range(nm):
                        n = min(512, cols - j * 512)
                        v = op("tensor", "matmul", scp[:, j * 512:j * 512 + n],
                               QT[:, qt * 128:(qt + 1) * 128],
                               KT[:, kbase + j * 512:kbase + j * 512 + n],
                               start=True, stop=True, inc=(pe if j == nm - 1 else None))
                    sR(scname, pe, v)
                    is_diag = (kp == npairs - 1)
                    if is_diag:
                        W("vector", pe, v)
                        v = op("vector", "tensor_tensor", out=scp[:, cols - 128:cols],
                               in0=scp[:, cols - 128:cols], in1=cmask[:, :], op=ALU.add, inc=dve)
                        sR(scname + "m", dve, v)
                    W("scalar", pe, bd.R[scname][1])
                    if is_diag:
                        W("scalar", dve, bd.R[scname + "m"][1])
                    wF("scalar", f"p{par}")
                    if kp == 0:
                        wF("scalar", "lparts")
                    v = op("scalar", "activation", p_sb[par][:, 0:cols], scp[:, 0:cols],
                           AF.Exp, bias=bias_cols[:, qt:qt + 1], scale=INV_SQRT_HD,
                           accum_out=lparts[:, kp:kp + 1], inc=act)
                    sR(f"p{par}", act, v)
                    sF(scname, act, v)
                    pname, ptp = PTP[par]
                    W("tensor", act, v)
                    wF("tensor", pname)
                    nj = cols // 128
                    for j in range(nj):
                        v = op("tensor", "transpose", ptp[:, j * 128:(j + 1) * 128],
                               p_sb[par][:, j * 128:(j + 1) * 128], ident_bf[:, :],
                               inc=(pe if j == nj - 1 else None))
                    sR(pname, pe, v)
                    sF(f"p{par}", pe, v)
                    W("vector", pe, v)
                    v = op("vector", "tensor_copy", out=pt_sb[par][:, 0:cols],
                           in_=ptp[:, 0:cols], inc=dve)
                    sR(f"pt{par}", dve, v)
                    sF(pname, dve, v)
                    W("tensor", dve, v)
                    if kp == 0:
                        wF("tensor", "avA")
                        wF("tensor", "avTA")
                    for j in range(nj):
                        v = op("tensor", "matmul", AVP[1][:, :],
                               pt_sb[par][:, j * 128:(j + 1) * 128],
                               Vtok[:, (kbase + j * 128):(kbase + (j + 1) * 128)],
                               start=(kp == 0 and j == 0), stop=(is_diag and j == nj - 1),
                               inc=(pe if (is_diag and j == nj - 1) else None))
                    if is_diag:
                        sR("avA", pe, v)
                W("vector", act, bd.R[f"p{(npairs - 1) % 2}"][1])
                op("vector", "reduce_sum", out=lsum[:, :], in_=lparts[:, 0:npairs], axis=AX.X)
                v = op("vector", "reciprocal", out=linv[:, :], in_=lsum[:, :], inc=dve)
                sR("linv", dve, v)
                sF("lparts", dve, v)
                W("scalar", dve, v)
                W("scalar", pe, bd.R["avA"][1])
                wF("scalar", "av_bf")
                v = op("scalar", "activation", av_bf[:, :], AVP[1][:, :], AF.Identity,
                       bias=0.0, scale=linv[:, :], inc=act)
                sR("av_bf", act, v)
                sF("avA", act, v)
                W("tensor", act, v)
                wF("tensor", "avTA")
                v = op("tensor", "transpose", AVTP[1][:, :], av_bf[:, :], ident_bf[:, :], inc=pe)
                sR("avTA", pe, v)
                sF("av_bf", pe, v)
                W("vector", pe, v)
                wF("vector", "avT_sb")
                v = op("vector", "tensor_copy", out=avT_sb[:, qt * 128:(qt + 1) * 128],
                       in_=AVTP[1][:, :], inc=dve)
                sF("avTA", dve, v)
                if qt == NQT - 1:
                    sR("avT_done", dve, v)
            W("gpsimd", dve, bd.R["avT_done"][1])
            v = bd.dma("gpsimd", dgp, out=ag_in[h * 128:(h + 1) * 128, :], in_=avT_sb[:, :])
            sF("avT_sb", dgp, v)
            if h == HPC - 1:
                sR("ag_in", dgp, v)

        PE_END_A = pe.n

        W("gpsimd", dgp, bd.R["ag_in"][1])
        v = op("gpsimd", "collective_compute", "AllGather", ALU.bypass,
               replica_groups=[[0, 1, 2, 3], [4, 5, 6, 7]],
               ins=[ag_in.ap().opt()], outs=[ag_out.ap().opt()], inc=cc)
        sR("ag", cc, v)

        if "dbg_att" in dbg:
            W("gpsimd", cc, bd.R["ag"][1])
            bd.dma("gpsimd", dgp, out=dbg["dbg_att"][:, :], in_=ag_out[:, :])

        POP = [("po0", pbank(0)), ("po1", pbank(1))]

        W("sync", pe, PE_END_A)
        v = bd.dma("sync", dsy, out=wout_sb[:, :], in_=wout[:, :])
        sR("wout", dsy, v)

        for tt in range(4):
            def agdma(e, tt=tt, nprev=None):
                e.wait_ge(dsy.h, nprev)
                with e.register(f"g2_{tt}") as g2:
                    e.reg_load(g2, coff_sb[:1, :1])
                    e.reg_add(g2, g2, tt * 512)
                    e.dma_start(
                        out=attT[:, :].rearrange("p (k t) -> p k t", t=512),
                        in_=bass.AP(ag_out, g2, [[S, 128], [128 * S, NKC], [1, 512]]),
                    ).then_inc(dsy.h, 16)
            W("sync", cc, bd.R["ag"][1])
            W("sync", dgp, bd.R["smallin"][1])
            wF("sync", "attT")
            _np = dsy.n
            bd.raw("sync", lambda e, f=agdma, n=_np: f(e, nprev=n))
            vA = dsy.inc(16)
            sR("attT", dsy, vA)
            wF("sync", "xr_t")
            v = bd.dma("sync", dsy, out=xr_t[:, :],
                   in_=bass.AP(xr, tt * 512, [[NKC * TC, 128], [TC, NKC], [1, 512]]))
            sR("xr_t", dsy, v)
            W("tensor", dsy, bd.R["attT"][1])
            if tt == 0:
                W("tensor", dsy, bd.R["wout"][1])
            for m in range(NKC):
                par = m % 2
                bname, psum = POP[par]
                wF("tensor", bname)
                for kc in range(NKC):
                    v = op("tensor", "matmul", psum[:, :],
                           wout_sb[:, kc * D + m * 128:kc * D + m * 128 + 128],
                           attT[:, kc * 512:(kc + 1) * 512],
                           start=(kc == 0), stop=(kc == NKC - 1),
                           inc=(pe if kc == NKC - 1 else None))
                sR(bname, pe, v)
                if m == NKC - 1:
                    sF("attT", pe, v)
                W("vector", pe, v)
                W("vector", dsy, bd.R["xr_t"][1])
                v = op("vector", "scalar_tensor_tensor",
                       out=r1[:, m * TC + tt * 512:m * TC + tt * 512 + 512],
                       in0=psum[:, :], scalar=bo_f[:, m:m + 1],
                       in1=xr_t[:, m * 512:(m + 1) * 512],
                       op0=ALU.add, op1=ALU.add, inc=dve)
                sF(bname, dve, v)
                if m == NKC - 1:
                    sF("xr_t", dve, v)
                    if tt == 3:
                        sR("r1", dve, v)

        SSUM = ("ssum", pbank(4)[0:1, :])
        SSQ = ("ssq", pbank(5)[0:1, :])
        BCP = ("bcp", pbank(6))
        W("scalar", dve, bd.R["r1"][1])
        for tt in range(4):
            for kc in range(NKC):
                par = kc % 2
                W("tensor", dve, bd.R["r1"][1])
                if kc == 0:
                    wF("tensor", "ssum")
                    wF("tensor", "ssq")
                v = op("tensor", "matmul", SSUM[1][:, :], ones_c_bf[:, :],
                       r1[:, kc * TC + tt * 512:kc * TC + tt * 512 + 512],
                       start=(kc == 0), stop=(kc == NKC - 1),
                       inc=(pe if kc == NKC - 1 else None))
                wF("scalar", f"sqB{par}")
                v2 = op("scalar", "activation", sq_tmp[par][:, :],
                        r1[:, kc * TC + tt * 512:kc * TC + tt * 512 + 512],
                        AF.Square, bias=0.0, scale=1.0, inc=act)
                sR(f"sqB{par}", act, v2)
                W("tensor", act, v2)
                v = op("tensor", "matmul", SSQ[1][:, :], ones_c_bf[:, :], sq_tmp[par][:, :],
                       start=(kc == 0), stop=(kc == NKC - 1), inc=pe)
                sF(f"sqB{par}", pe, v)
            sR("stats", pe, v)
            W("vector", pe, bd.R["stats"][1])
            op("vector", "tensor_copy", out=srow[0:1, tt * 512:(tt + 1) * 512], in_=SSUM[1][:, :])
            v = op("vector", "tensor_copy", out=sqrow[0:1, tt * 512:(tt + 1) * 512],
                   in_=SSQ[1][:, :], inc=dve)
            sR("statrows", dve, v)
            sF("ssum", dve, v)
            sF("ssq", dve, v)
        op("vector", "tensor_scalar", out=mean_r[:, :], in0=srow[:, :],
           scalar1=1.0 / D, scalar2=None, op0=ALU.mult)
        op("vector", "tensor_tensor", out=mrstd_r[:, :], in0=mean_r[:, :], in1=mean_r[:, :],
           op=ALU.mult)
        v = op("vector", "scalar_tensor_tensor", out=rstd_r[:, :], in0=sqrow[:, :],
               scalar=1.0 / D, in1=mrstd_r[:, :], op0=ALU.mult, op1=ALU.subtract, inc=dve)
        W("scalar", dve, v)
        op("scalar", "activation", rstd_r[:, :], rstd_r[:, :], AF.Ln, bias=eps5[0:1, :], scale=1.0)
        v = op("scalar", "activation", rstd_r[:, :], rstd_r[:, :], AF.Exp, bias=0.0,
               scale=-0.5, inc=act)
        sR("rstd_r", act, v)
        W("vector", act, v)
        v = op("vector", "tensor_tensor", out=mrstd_r[:, :], in0=mean_r[:, :], in1=rstd_r[:, :],
               op=ALU.mult, inc=dve)
        sR("mrstd_r", dve, v)
        for tt in range(4):
            W("tensor", dve, bd.R["mrstd_r"][1])
            wF("tensor", "bcp")
            v = op("tensor", "matmul", BCP[1][:, :], ones_r_f[:, :],
                   rstd_r[0:1, tt * 512:(tt + 1) * 512], start=True, stop=True, inc=pe)
            W("vector", pe, v)
            v = op("vector", "tensor_copy", out=rstd_b[:, tt * 512:(tt + 1) * 512],
                   in_=BCP[1][:, :], inc=dve)
            sF("bcp", dve, v)
            wF("tensor", "bcp")
            v = op("tensor", "matmul", BCP[1][:, :], ones_r_f[:, :],
                   mrstd_r[0:1, tt * 512:(tt + 1) * 512], start=True, stop=True, inc=pe)
            W("vector", pe, v)
            v = op("vector", "tensor_copy", out=mrstd_b[:, tt * 512:(tt + 1) * 512],
                   in_=BCP[1][:, :], inc=dve)
            sF("bcp", dve, v)
            if tt == 3:
                sR("bcast1", dve, v)

        LGP = [(f"lg{tt}", pbank(tt)[0:8, :]) for tt in range(4)]
        for tt in range(4):
            for kc in range(NKC):
                par = kc % 2
                W("vector", dve, bd.R["bcast1"][1])
                op("vector", "tensor_tensor", out=lnt[:, :],
                   in0=r1[:, kc * TC + tt * 512:kc * TC + tt * 512 + 512],
                   in1=rstd_b[:, tt * 512:(tt + 1) * 512], op=ALU.mult)
                wF("vector", f"x1s{par}")
                wF("vector", f"x1sD{par}")
                v = op("vector", "tensor_tensor", out=x1s[par][:, :], in0=lnt[:, :],
                       in1=mrstd_b[:, tt * 512:(tt + 1) * 512], op=ALU.subtract, inc=dve)
                sR(f"x1s{par}", dve, v)
                W("scalar", dve, v)
                v = bd.dma("scalar", dsc,
                       out=bass.AP(x1d, kc * 128 * TC + tt * 512, [[TC, 128], [1, 512]]),
                       in_=x1s[par][:, :])
                sF(f"x1sD{par}", dsc, v)
                vx1 = v
                W("tensor", dve, bd.R[f"x1s{par}"][1])
                if tt == 0 and kc == 0:
                    W("tensor", pool, bd.R["wg_lo"][1])
                    wF("tensor", f"lgb{tt}")
                op("tensor", "matmul", LGP[tt][1][:, :], wg_hi[:, kc * 8:(kc + 1) * 8],
                   x1s[par][:, :], start=(kc == 0), stop=False)
                v = op("tensor", "matmul", LGP[tt][1][:, :], wg_lo[:, kc * 8:(kc + 1) * 8],
                       x1s[par][:, :], start=False, stop=(kc == NKC - 1), inc=pe)
                sF(f"x1s{par}", pe, v)
                if kc == NKC - 1:
                    sR(LGP[tt][0], pe, v)
        sR("x1d_all", dsc, vx1)

        TRP = ("trp", pbank(4))
        CMP_ = ("cmp2", pa[0:8, 0:2048])
        for tt in range(4):
            W("vector", pe, bd.R[LGP[tt][0]][1])
            v = op("vector", "tensor_copy", out=lg_sb[:, tt * 512:(tt + 1) * 512],
                   in_=LGP[tt][1][:, :], inc=dve)
            if tt == 3:
                sR("lg", dve, v)
        W("tensor", dve, bd.R["lg"][1])
        wF("tensor", "trp")
        for ti in range(NKC):
            v = op("tensor", "transpose", TRP[1][:, ti * 8:(ti + 1) * 8],
                   lg_sb[0:8, ti * 128:(ti + 1) * 128], ident_f[0:8, 0:8],
                   inc=(pe if ti == NKC - 1 else None))
        sR("trp", pe, v)
        W("vector", pe, v)
        v = op("vector", "tensor_copy", out=ltok[:, :], in_=TRP[1][:, 0:NKC * 8], inc=dve)
        sF("trp", dve, v)
        for ti in range(NKC):
            lt = ltok[:, ti * 8:(ti + 1) * 8]
            op("vector", "reduce_max", out=m1c[:, :], in_=lt, axis=AX.X)
            op("vector", "reduce_max", out=nm1c[:, :], in_=lt, axis=AX.X, negate=True)
            op("vector", "tensor_scalar", out=msk8[:, :], in0=lt, scalar1=m1c[:, :],
               scalar2=None, op0=ALU.is_ge)
            op("vector", "scalar_tensor_tensor", out=msk8[:, :], in0=msk8[:, :],
               scalar=-1e30, in1=lt, op0=ALU.mult, op1=ALU.add)
            v = op("vector", "reduce_max", out=m2c[:, :], in_=msk8[:, :], axis=AX.X, inc=dve)
            W("scalar", dve, v)
            op("scalar", "activation", e1t[:, :], lt, AF.Exp, bias=nm1c[:, :], scale=1.0)
            v = op("scalar", "activation", e2c[:, :], m2c[:, :], AF.Exp, bias=nm1c[:, :],
                   scale=1.0, inc=act)
            W("vector", act, v)
            op("vector", "tensor_scalar", out=e2c[:, :], in0=e2c[:, :], scalar1=1.0,
               scalar2=None, op0=ALU.add)
            op("vector", "reciprocal", out=dinvc[:, :], in_=e2c[:, :])
            op("vector", "tensor_scalar", out=msk8[:, :], in0=lt, scalar1=m2c[:, :],
               scalar2=None, op0=ALU.is_ge)
            v = op("vector", "scalar_tensor_tensor", out=ltok[:, ti * 8:(ti + 1) * 8],
                   in0=e1t[:, :], scalar=dinvc[:, :], in1=msk8[:, :],
                   op0=ALU.mult, op1=ALU.mult, inc=dve)
            sR(f"cmb{ti}", dve, v)
        W("tensor", dve, bd.R[f"cmb{NKC - 1}"][1])
        for ti in range(NKC):
            v = op("tensor", "transpose", CMP_[1][:, ti * 128:(ti + 1) * 128],
                   ltok[:, ti * 8:(ti + 1) * 8], ident_f[:, :],
                   inc=(pe if ti == NKC - 1 else None))
        sR("cmp2", pe, v)
        W("vector", pe, v)
        v = op("vector", "tensor_copy", out=lg_sb[:, :], in_=CMP_[1][:, :], inc=dve)
        sR("cmbT", dve, v)
        sF("cmp2", dve, v)
        W("gpsimd", dve, v)
        v = bd.dma("gpsimd", dgp, out=cmb_d[:, :], in_=lg_sb[:, :])
        sR("cmb_d", dgp, v)

        if "dbg_x1" in dbg:
            W("gpsimd", dsc, bd.R["x1d_all"][1])
            bd.dma("gpsimd", dgp, out=dbg["dbg_x1"][:, :, :], in_=x1d[:, :, :])
        if "dbg_cmb" in dbg:
            W("gpsimd", dgp, bd.R["cmb_d"][1])
            bd.dma("gpsimd", dgp, out=dbg["dbg_cmb"][:, :], in_=cmb_d[:, :])

        GUP = [(f"gu{i}", pbank(i)) for i in range(4)]
        CBP = ("cbp", pbank(6))
        S2SUM = ("s2sum", pbank(4)[0:1, :])
        S2SQ = ("s2sq", pbank(5)[0:1, :])
        gu_rot = [0]
        nstage = [0]

        def stream_weight(src_ap):
            par = nstage[0] % 2
            nstage[0] += 1
            wF("sync", f"wst{par}")
            v = bd.dma("sync", dsy, out=wstage[par][:, :], in_=src_ap)
            sR(f"wst{par}", dsy, v)
            W("vector", dsy, v)
            wF("vector", f"whi{par}")
            v = op("vector", "tensor_copy", out=w_hi[par][:, :], in_=wstage[par][:, :], inc=dve)
            sR(f"whi{par}", dve, v)
            W("gpsimd", dve, v)
            wF("gpsimd", f"wlo{par}")
            v = op("gpsimd", "tensor_tensor", out=w_lo[par][:, :], in0=wstage[par][:, :],
                   in1=w_hi[par][:, :], op=ALU.subtract, inc=pool)
            sR(f"wlo{par}", pool, v)
            sF(f"wst{par}", pool, v)
            return par

        def gate_up_part(par_w, psrc_is_g, jp, tts_act_col, e_cmb):
            for tts in range(2):
                rot = gu_rot[0] % 4
                gu_rot[0] += 1
                bname, psum = GUP[rot]
                wF("tensor", bname)
                W("tensor", dve, bd.R[f"whi{par_w}"][1])
                W("tensor", pool, bd.R[f"wlo{par_w}"][1])
                nmm = 0
                for kc in range(NKC):
                    acol = kc * TSW + tts * 512
                    for wt in (w_hi[par_w], w_lo[par_w]):
                        nmm += 1
                        v = op("tensor", "matmul", psum[:, :],
                               wt[:, kc * 128:(kc + 1) * 128], x1_sw[:, acol:acol + 512],
                               start=(nmm == 1), stop=(nmm == NKC * 2),
                               inc=(pe if nmm == NKC * 2 else None))
                sR(bname, pe, v)
                if tts == 1:
                    sF(f"whi{par_w}", pe, v)
                    sF(f"wlo{par_w}", pe, v)
                if psrc_is_g:
                    W("scalar", pe, v)
                    wF("scalar", f"silu{tts}")
                    v = op("scalar", "activation", silu_g[:, tts * 512:(tts + 1) * 512],
                           psum[:, :], AF.Silu, bias=0.0, scale=1.0, inc=act)
                    sR(f"siluR{tts}", act, v)
                    sF(bname, act, v)
                else:
                    W("vector", pe, v)
                    W("vector", act, bd.R[f"siluR{tts}"][1])
                    if e_cmb is not None:
                        W("vector", dve, bd.R[f"cmb_bR{tts}"][1])
                        op("vector", "tensor_tensor", out=act_f[:, tts * 512:(tts + 1) * 512],
                           in0=psum[:, :], in1=cmb_b[:, tts * 512:(tts + 1) * 512], op=ALU.mult)
                        v = op("vector", "tensor_tensor",
                               out=act_f[:, tts * 512:(tts + 1) * 512],
                               in0=act_f[:, tts * 512:(tts + 1) * 512],
                               in1=silu_g[:, tts * 512:(tts + 1) * 512], op=ALU.mult, inc=dve)
                    else:
                        v = op("vector", "tensor_tensor",
                               out=act_f[:, tts * 512:(tts + 1) * 512], in0=psum[:, :],
                               in1=silu_g[:, tts * 512:(tts + 1) * 512], op=ALU.mult, inc=dve)
                    sF(bname, dve, v)
                    sF(f"silu{tts}", dve, v)
                    acol = tts_act_col + tts * 512
                    wF("vector", "act_hi")
                    v = op("vector", "tensor_copy", out=act_hi[:, acol:acol + 512],
                           in_=act_f[:, tts * 512:(tts + 1) * 512], inc=dve)
                    sR(f"ahi{jp}_{tts}", dve, v)
                    W("gpsimd", dve, v)
                    wF("gpsimd", "act_lo")
                    v = op("gpsimd", "tensor_tensor", out=act_lo[:, acol:acol + 512],
                           in0=act_f[:, tts * 512:(tts + 1) * 512],
                           in1=act_hi[:, acol:acol + 512], op=ALU.subtract, inc=pool)
                    sR(f"alo{jp}_{tts}", pool, v)

        def down_proj(src_fn, first_acc):
            for ds in range(8):
                par = stream_weight(src_fn(ds))
                for d2 in range(2):
                    for tts in range(2):
                        rot = gu_rot[0] % 4
                        gu_rot[0] += 1
                        bname, psum = GUP[rot]
                        wF("tensor", bname)
                        W("tensor", dve, bd.R[f"whi{par}"][1])
                        W("tensor", pool, bd.R[f"wlo{par}"][1])
                        nmm = 0
                        for ic in range(8):
                            wcol = ic * 256 + d2 * 128
                            acol = ic * TSW + tts * 512
                            for wt, at in ((w_hi[par], act_hi), (w_hi[par], act_lo),
                                           (w_lo[par], act_hi)):
                                nmm += 1
                                v = op("tensor", "matmul", psum[:, :],
                                       wt[:, wcol:wcol + 128], at[:, acol:acol + 512],
                                       start=(nmm == 1), stop=(nmm == 24),
                                       inc=(pe if nmm == 24 else None))
                        sR(bname, pe, v)
                        if d2 == 1 and tts == 1:
                            sF(f"whi{par}", pe, v)
                            sF(f"wlo{par}", pe, v)
                        dch = ds * 2 + d2
                        W("vector", pe, v)
                        accap = ffn_acc[:, dch * TSW + tts * 512:dch * TSW + tts * 512 + 512]
                        if first_acc[dch * 2 + tts]:
                            first_acc[dch * 2 + tts] = False
                            v = op("vector", "tensor_copy", out=accap, in_=psum[:, :], inc=dve)
                        else:
                            v = op("vector", "tensor_tensor", out=accap, in0=accap,
                                   in1=psum[:, :], op=ALU.add, inc=dve)
                        sF(bname, dve, v)

        for sw in range(SW):
            wF("sync", "x1_sw")
            W("sync", dsc, bd.R["x1d_all"][1])
            W("sync", dgp, bd.R["cmb_d"][1])
            if True:
                v = bd.dma("sync", dsy,
                       out=x1_sw[:, :].rearrange("p (k t) -> p k t", t=TSW),
                       in_=bass.AP(x1d, sw * TSW, [[TC, 128], [128 * TC, NKC], [1, TSW]]))
            sR("x1_sw", dsy, v)
            W("tensor", dsy, v)

            first_acc = [True] * (NKC * 2)
            for e in range(E):
                wF("gpsimd", "cmb_r")
                vcr = bd.dma("gpsimd", dgp, out=cmb_r[:, :],
                         in_=cmb_d[e:e + 1, sw * TSW:(sw + 1) * TSW])
                sR("cmb_rR", dgp, vcr)
                for tts in range(2):
                    W("tensor", dgp, bd.R["cmb_rR"][1])
                    wF("tensor", "cbp")
                    v = op("tensor", "matmul", CBP[1][:, :], ones_r_f[:, :],
                           cmb_r[0:1, tts * 512:(tts + 1) * 512],
                           start=True, stop=True, inc=pe)
                    sR("cbp", pe, v)
                    sF("cmb_r", pe, v)
                    W("vector", pe, v)
                    wF("vector", f"cmb_b{tts}")
                    v = op("vector", "tensor_copy", out=cmb_b[:, tts * 512:(tts + 1) * 512],
                           in_=CBP[1][:, :], inc=dve)
                    sR(f"cmb_bR{tts}", dve, v)
                    sF("cbp", dve, v)
                for jp in range(8):
                    parg = stream_weight(wgu[e, jp, :, :])
                    gate_up_part(parg, True, jp, None, None)
                    paru = stream_weight(wgu[e, jp + 8, :, :])
                    gate_up_part(paru, False, jp, jp * TSW, e)
                down_proj(lambda ds, e=e: wdn[e, ds, :, :], first_acc)
                sF("act_hi", pe, pe.n)
                sF("act_lo", pe, pe.n)
                sF("silu0", pe, pe.n)
                sF("silu1", pe, pe.n)
                sF("cmb_b0", pe, pe.n)
                sF("cmb_b1", pe, pe.n)
            for ig in range(4):
                for jp in range(8):
                    parg = stream_weight(wsgu[ig * 8 + jp, :, :])
                    gate_up_part(parg, True, jp, None, None)
                    paru = stream_weight(wsgu[32 + ig * 8 + jp, :, :])
                    gate_up_part(paru, False, jp, jp * TSW, None)
                down_proj(lambda ds, ig=ig: wsdn[ig, ds, :, :], first_acc)
                sF("act_hi", pe, pe.n)
                sF("act_lo", pe, pe.n)
                sF("silu0", pe, pe.n)
                sF("silu1", pe, pe.n)

            for dch in range(NKC):
                ap_ = ffn_acc[:, dch * TSW:(dch + 1) * TSW]
                v = op("vector", "tensor_tensor", out=ap_,
                       in0=x1_sw[:, dch * TSW:(dch + 1) * TSW], in1=ap_, op=ALU.add, inc=dve)
                if dch == NKC - 1:
                    sR("r2", dve, v)
                    sF("x1_sw", dve, v)
            for tts in range(2):
                for dch in range(NKC):
                    W("tensor", dve, bd.R["r2"][1])
                    rc = ffn_acc[:, dch * TSW + tts * 512:dch * TSW + tts * 512 + 512]
                    if dch == 0:
                        wF("tensor", "s2sum")
                        wF("tensor", "s2sq")
                    v = op("tensor", "matmul", S2SUM[1][:, :], ones_c_f[:, :], rc,
                           start=(dch == 0), stop=(dch == NKC - 1),
                           inc=(pe if dch == NKC - 1 else None))
                    W("scalar", dve, bd.R["r2"][1])
                    wF("scalar", "sq2_0")
                    v2 = op("scalar", "activation", sq2[0][:, :], rc, AF.Square,
                            bias=0.0, scale=1.0, inc=act)
                    sR("sq2R", act, v2)
                    W("tensor", act, v2)
                    v = op("tensor", "matmul", S2SQ[1][:, :], ones_c_f[:, :], sq2[0][:, :],
                           start=(dch == 0), stop=(dch == NKC - 1), inc=pe)
                    sF("sq2_0", pe, v)
                sR("stats2", pe, v)
                W("vector", pe, bd.R["stats2"][1])
                op("vector", "tensor_copy", out=s2row[:, :], in_=S2SUM[1][:, :])
                v = op("vector", "tensor_copy", out=sq2row[:, :], in_=S2SQ[1][:, :], inc=dve)
                sF("s2sum", dve, v)
                sF("s2sq", dve, v)
                op("vector", "tensor_scalar", out=mean2_r[:, :], in0=s2row[:, :],
                   scalar1=1.0 / D, scalar2=None, op0=ALU.mult)
                op("vector", "tensor_tensor", out=mrstd2_r[:, :], in0=mean2_r[:, :],
                   in1=mean2_r[:, :], op=ALU.mult)
                v = op("vector", "scalar_tensor_tensor", out=rstd2_r[:, :], in0=sq2row[:, :],
                       scalar=1.0 / D, in1=mrstd2_r[:, :], op0=ALU.mult, op1=ALU.subtract,
                       inc=dve)
                W("scalar", dve, v)
                op("scalar", "activation", rstd2_r[:, :], rstd2_r[:, :], AF.Ln, bias=eps5[0:1, :],
                   scale=1.0)
                v = op("scalar", "activation", rstd2_r[:, :], rstd2_r[:, :], AF.Exp, bias=0.0,
                       scale=-0.5, inc=act)
                W("vector", act, v)
                v = op("vector", "tensor_tensor", out=mrstd2_r[:, :], in0=mean2_r[:, :],
                       in1=rstd2_r[:, :], op=ALU.mult, inc=dve)
                sR("mrstd2", dve, v)
                W("tensor", dve, v)
                wF("tensor", "cbp")
                v = op("tensor", "matmul", CBP[1][:, :], ones_r_f[:, :], rstd2_r[0:1, :],
                       start=True, stop=True, inc=pe)
                W("vector", pe, v)
                v = op("vector", "tensor_copy", out=rstd2_b[:, :], in_=CBP[1][:, :], inc=dve)
                sF("cbp", dve, v)
                wF("tensor", "cbp")
                v = op("tensor", "matmul", CBP[1][:, :], ones_r_f[:, :], mrstd2_r[0:1, :],
                       start=True, stop=True, inc=pe)
                W("vector", pe, v)
                v = op("vector", "tensor_copy", out=mrstd2_b[:, :], in_=CBP[1][:, :], inc=dve)
                sF("cbp", dve, v)
                for dch in range(NKC):
                    par = dch % 2
                    rc = ffn_acc[:, dch * TSW + tts * 512:dch * TSW + tts * 512 + 512]
                    op("vector", "tensor_tensor", out=t2f[:, :], in0=rc, in1=rstd2_b[:, :],
                       op=ALU.mult)
                    wF("vector", f"ost{par}")
                    v = op("vector", "tensor_tensor", out=ost[par][:, :], in0=t2f[:, :],
                           in1=mrstd2_b[:, :], op=ALU.subtract, inc=dve)
                    W("scalar", dve, v)
                    v = bd.dma("scalar", dsc,
                           out=out_d[dch * 128:(dch + 1) * 128,
                                     sw * TSW + tts * 512:sw * TSW + tts * 512 + 512],
                           in_=ost[par][:, :])
                    sF(f"ost{par}", dsc, v)

        W("scalar", dsc, dsc.n)
        W("sync", dsy, dsy.n)
        W("gpsimd", dgp, dgp.n)

        with nc.Block() as block:
            @block.tensor
            def _(eng):
                for f in bd.plan["tensor"]:
                    f(eng)

            @block.vector
            def _(eng):
                for f in bd.plan["vector"]:
                    f(eng)

            @block.scalar
            def _(eng):
                for f in bd.plan["scalar"]:
                    f(eng)

            @block.gpsimd
            def _(eng):
                for f in bd.plan["gpsimd"]:
                    f(eng)

            @block.sync
            def _(eng):
                for f in bd.plan["sync"]:
                    f(eng)

    return nc



def _layout_inputs(inputs):
    x = np.asarray(inputs["L0_attn_x"])
    Wqkv = np.asarray(inputs["L0_attn_W_qkv"])
    bqkv = np.asarray(inputs["L0_attn_b_qkv"])
    Wout = np.asarray(inputs["L0_attn_W_out"])
    bout = np.asarray(inputs["L0_attn_b_out"])
    wg = np.asarray(inputs["L0_ffn_w_gate"])
    wgu = np.asarray(inputs["L0_ffn_w_gate_up"])
    wdn = np.asarray(inputs["L0_ffn_w_down"])
    wsgu = np.asarray(inputs["L0_ffn_w_sh_gate_up"])
    wsdn = np.asarray(inputs["L0_ffn_w_sh_down"])

    wout_h = np.ascontiguousarray(
        Wout.reshape(16, 128, 2048).transpose(1, 0, 2).reshape(128, 16 * 2048))
    bout_h = np.ascontiguousarray(bout.reshape(16, 128).T)
    wg_h = np.ascontiguousarray(
        wg.reshape(16, 128, 8).transpose(1, 0, 2).reshape(128, 128))
    wgu_h = np.ascontiguousarray(
        wgu.reshape(8, 16, 128, 16, 128).transpose(0, 3, 2, 1, 4).reshape(8, 16, 128, 2048))
    wdn_h = np.ascontiguousarray(
        wdn.reshape(8, 8, 128, 8, 256).transpose(0, 3, 2, 1, 4).reshape(8, 8, 128, 2048))
    wsgu_h = np.ascontiguousarray(
        wsgu.reshape(16, 128, 64, 128).transpose(2, 1, 0, 3).reshape(64, 128, 2048))
    wsdn_h = np.ascontiguousarray(
        wsdn.reshape(4, 8, 128, 8, 256).transpose(0, 3, 2, 1, 4).reshape(4, 8, 128, 2048))

    Wq5 = Wqkv.reshape(16, 128, 3, 16, 128)
    bq3 = bqkv.reshape(3, 16, 128)

    in_maps = []
    for c in range(8):
        b, hg = c // 4, c % 4
        s = c % 4
        xa_h = np.ascontiguousarray(
            x[b].reshape(NTA, TTA, 16, 128).transpose(0, 3, 2, 1).reshape(NTA, 128, 16 * TTA))
        xr_h = np.ascontiguousarray(
            x[b, s * TC:(s + 1) * TC].reshape(TC, 16, 128).transpose(2, 1, 0).reshape(128, 16 * TC))
        wqkv_h = np.ascontiguousarray(
            Wq5[:, :, :, hg * 4:(hg + 1) * 4, :].transpose(1, 0, 3, 2, 4).reshape(128, 16 * 1536))
        bq_h = np.ascontiguousarray(
            bq3[:, hg * 4:(hg + 1) * 4, :].transpose(2, 1, 0).reshape(128, 12))
        in_maps.append({
            "xa": xa_h, "xr": xr_h, "wqkv": wqkv_h, "bqkv": bq_h,
            "wout": wout_h, "bout": bout_h, "wg": wg_h,
            "wgu": wgu_h, "wdn": wdn_h, "wsgu": wsgu_h, "wsdn": wsdn_h,
            "coff": np.array([[s * TC]], dtype=np.int32),
        })
    return in_maps


def kernel(**inputs):
    in_maps = _layout_inputs(inputs)
    nc = build_nc(DEBUG_TAPS)
    res = run_bass_kernel_spmd(nc, in_maps, core_ids=list(range(8)))
    out = np.empty((B, S, D), np.float32)
    for c in range(8):
        b, s = c // 4, c % 4
        out[b, s * TC:(s + 1) * TC, :] = res.results[c]["out"].T
    if DEBUG_TAPS:
        kernel.last_results = res
    return out


# version 02
# speedup vs baseline: 1.1382x; 1.1382x over previous
import numpy as np
import ml_dtypes
from contextlib import ExitStack

import concourse.bass as bass
import concourse.mybir as mybir
from concourse.bass_utils import run_bass_kernel_spmd

F32 = mybir.dt.float32
BF16 = mybir.dt.bfloat16
I32 = mybir.dt.int32
U8 = mybir.dt.uint8
AF = mybir.ActivationFunctionType
ALU = mybir.AluOpType
AX = mybir.AxisListType

B, S, D, H, HD = 2, 8192, 2048, 16, 128
HPC = 4
NKC = D // 128
TC = 2048
TTA = 512
NTA = S // TTA
NQT = S // 128
E, IM, ISH = 8, 1024, 4096
SW, TSW = 2, 1024
INV_SQRT_HD = float(1.0 / np.sqrt(HD))

ENGS = ("tensor", "vector", "scalar", "gpsimd", "sync")
DEBUG_TAPS = []


class Sem:
    def __init__(self, h):
        self.h = h
        self.n = 0

    def inc(self, k=1):
        self.n += k
        return self.n


class Builder:
    SELF_EDGE = {"vector", "scalar", "gpsimd"}
    ASYNC_METHS = {"dma_start", "collective_compute", "indirect_dma_start"}

    def __init__(self, selfsems=None):
        self.plan = {e: [] for e in ENGS}
        self.R = {}
        self.F = {}
        self.selfsems = selfsems or {}

    def op(self, eng, meth, *a, inc=None, k=1, **kw):
        if eng in self.SELF_EDGE and meth not in self.ASYNC_METHS:
            sem = self.selfsems[eng]
            if sem.n > 0:
                self.wait(eng, sem, sem.n)
            if inc is None:
                inc, k = sem, 1
            else:
                assert inc is sem, (eng, meth)
        if inc is None:
            self.plan[eng].append(lambda e, m=meth, aa=a, kk=kw: getattr(e, m)(*aa, **kk))
            return None
        v = inc.inc(k)
        self.plan[eng].append(
            lambda e, m=meth, aa=a, kk=kw, s=inc.h, n=k: getattr(e, m)(*aa, **kk).then_inc(s, n))
        return v

    def dma(self, eng, qsem, **kw):
        self.wait(eng, qsem, qsem.n)
        return self.op(eng, "dma_start", inc=qsem, k=16, **kw)

    def wait(self, eng, sem, val):
        if val and val > 0:
            self.plan[eng].append(lambda e, s=sem.h, v=val: e.wait_ge(s, v))

    def raw(self, eng, fn):
        self.plan[eng].append(fn)

    def wR(self, eng, *names):
        for n in names:
            g = self.R.get(n)
            if g is not None:
                self.wait(eng, g[0], g[1])

    def wF(self, eng, *names):
        for n in names:
            g = self.F.get(n)
            if g is not None:
                self.wait(eng, g[0], g[1])

    def sR(self, name, sem, val):
        self.R[name] = (sem, val)

    def sF(self, name, sem, val):
        self.F[name] = (sem, val)


def build_nc(debug_taps=()):
    nc = bass.Bass("TRN2", target_bir_lowering=False)

    xa = nc.dram_tensor("xa", [NTA, 128, NKC * TTA], BF16, kind="ExternalInput")
    xr = nc.dram_tensor("xr", [128, NKC * TC], BF16, kind="ExternalInput")
    wqkv = nc.dram_tensor("wqkv", [128, NKC * HPC * 3 * 128], BF16, kind="ExternalInput")
    bqkv = nc.dram_tensor("bqkv", [128, HPC * 3], BF16, kind="ExternalInput")
    wout = nc.dram_tensor("wout", [128, NKC * D], BF16, kind="ExternalInput")
    bout = nc.dram_tensor("bout", [128, NKC], BF16, kind="ExternalInput")
    wg = nc.dram_tensor("wg", [128, NKC * E], F32, kind="ExternalInput")
    wgu = nc.dram_tensor("wgu", [E, 16, 128, NKC * 128], F32, kind="ExternalInput")
    wdn = nc.dram_tensor("wdn", [E, 8, 128, 8 * 256], F32, kind="ExternalInput")
    wsgu = nc.dram_tensor("wsgu", [64, 128, NKC * 128], F32, kind="ExternalInput")
    wsdn = nc.dram_tensor("wsdn", [4, 8, 128, 8 * 256], F32, kind="ExternalInput")
    coff = nc.dram_tensor("coff", [1, 1], I32, kind="ExternalInput")
    out_d = nc.dram_tensor("out", [D, TC], F32, kind="ExternalOutput")

    ag_in = nc.dram_tensor("ag_in", [HPC * 128, S], BF16)
    ag_out = nc.dram_tensor("ag_out", [D, S], BF16)
    x1d = nc.dram_tensor("x1d", [NKC, 128, TC], BF16)
    cmb_d = nc.dram_tensor("cmb_d", [8, TC], F32)

    dbg = {}
    for name, shape, dt in debug_taps:
        dbg[name] = nc.dram_tensor(name, shape, dt, kind="ExternalOutput")

    with ExitStack() as es:
        ARENA = 210944
        arena = es.enter_context(nc.sbuf_tensor("arena", [128, ARENA], U8))
        abase = nc.lookup_mloc(arena).addr

        def sb(name, shape, dtype, off):
            per = int(np.prod(shape[1:])) * mybir.dt.size(dtype)
            assert off + per <= ARENA, (name, off, per)
            return nc.alloc_sbuf_tensor_at(name, shape, dtype, offset=abase + off)

        ident_bf = sb("ident_bf", [128, 128], BF16, 0)
        ident_f = sb("ident_f", [128, 128], F32, 256)
        cmask = sb("cmask", [128, 128], F32, 768)
        ones_c_bf = sb("ones_c_bf", [128, 1], BF16, 1280)
        ones_c_f = sb("ones_c_f", [128, 1], F32, 1312)
        ones_r_f = sb("ones_r_f", [1, 128], F32, 1344)
        bq_bf = sb("bq_bf", [128, HPC * 3], BF16, 1856)
        bq_f = sb("bq_f", [128, HPC * 3], F32, 1888)
        bo_bf = sb("bo_bf", [128, NKC], BF16, 1952)
        bo_f = sb("bo_f", [128, NKC], F32, 1984)
        wg_f = sb("wg_f", [128, NKC * E], F32, 2048)
        wg_hi = sb("wg_hi", [128, NKC * E], BF16, 2560)
        wg_lo = sb("wg_lo", [128, NKC * E], BF16, 2816)
        eps12 = sb("eps12", [128, 1], F32, 205824)
        eps5 = sb("eps5", [128, 1], F32, 205856)
        coff_sb = sb("coff_sb", [1, 1], I32, 205888)

        wqkv_sb = sb("wqkv_sb", [128, NKC * 1536], BF16, 3072)
        xt = [sb(f"xt{i}", [128, NKC * TTA], BF16, 52224 + i * 16384) for i in range(2)]
        QT = sb("QT", [128, S], BF16, 84992)
        KT = sb("KT", [128, S], BF16, 101376)
        VT = sb("VT", [128, S], BF16, 117760)
        Vtok = sb("Vtok", [128, S], BF16, 134144)
        avT_sb = sb("avT_sb", [128, S], BF16, 150528)
        p_sb = [sb(f"p_sb{i}", [128, 1024], BF16, 166912 + i * 2048) for i in range(2)]
        pt_sb = [sb(f"pt_sb{i}", [128, 1024], BF16, 171008 + i * 2048) for i in range(2)]
        sq_tmp = [sb(f"sq_tmp{i}", [128, 512], BF16, 175104 + i * 1024) for i in range(2)]
        row_buf = sb("row_buf", [1, 512], F32, 177152)
        braw = sb("braw", [128, NQT], F32, 179200)
        bias_cols = sb("bias_cols", [128, NQT], F32, 179456)
        km_tmp = sb("km_tmp", [128, 1], F32, 179712)
        kmax_col = sb("kmax_col", [128, 1], F32, 179744)
        km_s = sb("km_s", [1, 1], F32, 179776)
        lparts = sb("lparts", [128, 32], F32, 179808)
        lsum = sb("lsum", [128, 1], F32, 179936)
        linv = sb("linv", [128, 1], F32, 179968)
        av_bf = sb("av_bf", [128, 128], BF16, 180000)

        wout_sb = sb("wout_sb", [128, NKC * D], BF16, 3072)
        attT = sb("attT", [128, NKC * 512], BF16, 68608)
        xr_t = sb("xr_t", [128, NKC * 512], BF16, 84992)
        r1 = sb("r1", [128, NKC * TC], BF16, 101376)

        srow = sb("srow", [1, TC], F32, 3072)
        sqrow = sb("sqrow", [1, TC], F32, 11264)
        mean_r = sb("mean_r", [1, TC], F32, 19456)
        rstd_r = sb("rstd_r", [1, TC], F32, 27648)
        mrstd_r = sb("mrstd_r", [1, TC], F32, 35840)
        rstd_b = sb("rstd_b", [128, TC], F32, 44032)
        mrstd_b = sb("mrstd_b", [128, TC], F32, 52224)
        lnt = sb("lnt", [128, 512], F32, 60416)
        x1s = [sb(f"x1s{i}", [128, 512], BF16, 62464 + i * 1024) for i in range(2)]

        lg_sb = sb("lg_sb", [8, TC], F32, 68608)
        ltok = sb("ltok", [128, NKC * 8], F32, 76800)
        m1c = sb("m1c", [128, 1], F32, 77312)
        nm1c = sb("nm1c", [128, 1], F32, 77344)
        m2c = sb("m2c", [128, 1], F32, 77376)
        e2c = sb("e2c", [128, 1], F32, 77408)
        dinvc = sb("dinvc", [128, 1], F32, 77440)
        msk8 = sb("msk8", [128, 8], F32, 77472)
        e1t = sb("e1t", [128, 8], F32, 77504)
        cmb_r = sb("cmb_r", [1, TSW], F32, 201728)

        ffn_acc = sb("ffn_acc", [128, NKC * TSW], F32, 3072)
        x1_sw = sb("x1_sw", [128, NKC * TSW], BF16, 68608)
        wstage = [sb(f"wstage{i}", [128, 2048], F32, 101376 + i * 8192) for i in range(2)]
        w_hi = [sb(f"w_hi{i}", [128, 2048], BF16, 117760 + i * 4096) for i in range(2)]
        w_lo = [sb(f"w_lo{i}", [128, 2048], BF16, 125952 + i * 4096) for i in range(2)]
        silu_g = sb("silu_g", [128, TSW], F32, 134144)
        act_f = sb("act_f", [128, TSW], F32, 138240)
        act_hi = sb("act_hi", [128, 8 * TSW], BF16, 142336)
        act_lo = sb("act_lo", [128, 8 * TSW], BF16, 158720)
        cmb_b = sb("cmb_b", [128, TSW], F32, 175104)
        ost = [sb(f"ost{i}", [128, 512], F32, 179200 + i * 2048) for i in range(2)]
        sq2 = [sb("sq2_0", [128, 512], F32, 183296)] * 2
        s2row = sb("s2row", [1, 512], F32, 185344)
        sq2row = sb("sq2row", [1, 512], F32, 187392)
        mean2_r = sb("mean2_r", [1, 512], F32, 189440)
        rstd2_r = sb("rstd2_r", [1, 512], F32, 191488)
        mrstd2_r = sb("mrstd2_r", [1, 512], F32, 193536)
        rstd2_b = sb("rstd2_b", [128, 512], F32, 195584)
        mrstd2_b = sb("mrstd2_b", [128, 512], F32, 197632)
        t2f = sb("t2f", [128, 512], F32, 199680)

        pa = es.enter_context(nc.psum_tensor("parena", [128, 4096], F32))

        def pbank(k, n=1):
            return pa[:, k * 512:(k + n) * 512]

        sems = {n: Sem(es.enter_context(nc.semaphore(n)))
                for n in ("pe", "act", "dve", "pool", "dsy", "dsc", "dgp", "cc")}
        pe, act, dve, pool = sems["pe"], sems["act"], sems["dve"], sems["pool"]
        dsy, dsc, dgp, cc = sems["dsy"], sems["dsc"], sems["dgp"], sems["cc"]

        bd = Builder({"vector": dve, "scalar": act, "gpsimd": pool})
        op, W, wR, wF, sR, sF = bd.op, bd.wait, bd.wR, bd.wF, bd.sR, bd.sF

        op("gpsimd", "memset", ident_bf[:, :], 0.0)
        op("gpsimd", "affine_select", out=ident_bf[:, :], in_=ident_bf[:, :],
           compare_op=ALU.not_equal, fill=1.0, base=0, pattern=[[-1, 128]], channel_multiplier=1)
        op("gpsimd", "memset", ident_f[:, :], 0.0)
        op("gpsimd", "affine_select", out=ident_f[:, :], in_=ident_f[:, :],
           compare_op=ALU.not_equal, fill=1.0, base=0, pattern=[[-1, 128]], channel_multiplier=1)
        op("gpsimd", "memset", cmask[:, :], 0.0)
        op("gpsimd", "affine_select", out=cmask[:, :], in_=cmask[:, :],
           compare_op=ALU.is_ge, fill=-1e30, base=0, pattern=[[-1, 128]], channel_multiplier=1)
        op("gpsimd", "memset", ones_c_bf[:, :], 1.0)
        op("gpsimd", "memset", ones_c_f[:, :], 1.0)
        op("gpsimd", "memset", ones_r_f[:, :], 1.0)
        op("gpsimd", "memset", eps12[:, :], 1e-12)
        v = op("gpsimd", "memset", eps5[:, :], 1e-5, inc=pool)
        sR("consts", pool, v)

        bd.dma("gpsimd", dgp, out=bq_bf[:, :], in_=bqkv[:, :])
        bd.dma("gpsimd", dgp, out=bo_bf[:, :], in_=bout[:, :])
        bd.dma("gpsimd", dgp, out=wg_f[:, :], in_=wg[:, :])
        v = bd.dma("gpsimd", dgp, out=coff_sb[:, :], in_=coff[:, :])
        sR("smallin", dgp, v)
        W("vector", dgp, v)
        op("vector", "tensor_copy", out=bq_f[:, :], in_=bq_bf[:, :])
        op("vector", "tensor_copy", out=bo_f[:, :], in_=bo_bf[:, :])
        v = op("vector", "tensor_copy", out=wg_hi[:, :], in_=wg_f[:, :], inc=dve)
        sR("wg_hi", dve, v)
        W("gpsimd", dve, v)
        v = op("gpsimd", "tensor_tensor", out=wg_lo[:, :], in0=wg_f[:, :], in1=wg_hi[:, :],
               op=ALU.subtract, inc=pool)
        sR("wg_lo", pool, v)

        v = bd.dma("sync", dsy, out=wqkv_sb[:, :], in_=wqkv[:, :])
        sR("wqkv", dsy, v)

        SC = [("scA0", pbank(0, 2)), ("scA1", pbank(2, 2))]
        PTP = [("ptA0", pbank(4).bitcast(BF16)), ("ptA1", pbank(5).bitcast(BF16))]
        AVP = ("avA", pbank(6)[:, 0:128])
        BIP = ("biA", pbank(6)[:, 128:192])
        AVTP = ("avTA", pbank(7).bitcast(BF16)[:, 0:128])
        Q2P = ("q2A", pbank(7)[:, 0:512])
        KMP = ("kmA", pbank(7)[:, 0:128])
        QKVP = [[(f"qkv{w}{p}", pbank(w * 2 + p)) for p in range(2)] for w in range(3)]

        for h in range(HPC):
            wF("scalar", "xt0")
            v = bd.dma("scalar", dsc, out=xt[0][:, :], in_=xa[0, :, :])
            sR("xt0", dsc, v)
            for tt in range(NTA):
                par = tt % 2
                xtile = xt[par]
                if tt + 1 < NTA:
                    npar = (tt + 1) % 2
                    wF("scalar", f"xt{npar}")
                    v = bd.dma("scalar", dsc, out=xt[npar][:, :], in_=xa[tt + 1, :, :])
                    sR(f"xt{npar}", dsc, v)
                W("tensor", dsc, bd.R[f"xt{par}"][1])
                if h == 0 and tt == 0:
                    W("tensor", dsy, bd.R["wqkv"][1])
                for w in range(3):
                    bname, psum = QKVP[w][par]
                    wF("tensor", bname)
                    for kc in range(NKC):
                        col = kc * 1536 + h * 384 + w * 128
                        v = op("tensor", "matmul", psum[:, :],
                               wqkv_sb[:, col:col + 128], xtile[:, kc * TTA:(kc + 1) * TTA],
                               start=(kc == 0), stop=(kc == NKC - 1),
                               inc=(pe if kc == NKC - 1 else None))
                    sR(bname, pe, v)
                    if w == 2:
                        sF(f"xt{par}", pe, v)
                for w, dst in enumerate((QT, KT, VT)):
                    bname, psum = QKVP[w][par]
                    W("scalar", pe, bd.R[bname][1])
                    if h == 0 and tt == 0 and w == 0:
                        W("scalar", dve, bd.R["wg_hi"][1])
                    v = op("scalar", "activation", dst[:, tt * TTA:(tt + 1) * TTA], psum[:, :],
                           AF.Identity, bias=bq_f[:, h * 3 + w:h * 3 + w + 1], scale=1.0,
                           inc=act)
                    sF(bname, act, v)
                    if tt == NTA - 1:
                        sR("QT KT VT".split()[w], act, v)

            for r in range(8):
                par = r % 2
                pname, ptp = PTP[par]
                W("tensor", act, bd.R["VT"][1])
                wF("tensor", pname)
                for j in range(8):
                    kc = r * 8 + j
                    v = op("tensor", "transpose", ptp[:, j * 128:(j + 1) * 128],
                           VT[:, kc * 128:(kc + 1) * 128], ident_bf[:, :],
                           inc=(pe if j == 7 else None))
                sR(pname, pe, v)
                W("vector", pe, v)
                v = op("vector", "tensor_copy", out=Vtok[:, r * 1024:(r + 1) * 1024],
                       in_=ptp[:, :], inc=dve)
                sF(pname, dve, v)
                if r == 7:
                    sR("Vtok", dve, v)

            for c in range(NKC):
                par = c % 2
                W("scalar", act, bd.R["QT"][1])
                wF("scalar", f"sq{par}")
                v = op("scalar", "activation", sq_tmp[par][:, :], QT[:, c * 512:(c + 1) * 512],
                       AF.Square, bias=0.0, scale=1.0, inc=act)
                sR(f"sq{par}", act, v)
                W("tensor", act, v)
                wF("tensor", "q2A")
                if c == 0:
                    wF("tensor", "avTA")
                if h == 0 and c == 0:
                    W("tensor", pool, bd.R["consts"][1])
                v = op("tensor", "matmul", Q2P[1][0:1, :], ones_c_bf[:, :], sq_tmp[par][:, :],
                       start=True, stop=True, inc=pe)
                sR("q2A", pe, v)
                sF(f"sq{par}", pe, v)
                W("vector", pe, v)
                wF("vector", "row_buf")
                v = op("vector", "tensor_copy", out=row_buf[:, :], in_=Q2P[1][0:1, :], inc=dve)
                sR("row_buf", dve, v)
                sF("q2A", dve, v)
                W("tensor", dve, v)
                for j in range(4):
                    qt = c * 4 + j
                    v = op("tensor", "transpose", BIP[1][:, qt:qt + 1],
                           row_buf[0:1, j * 128:(j + 1) * 128], ident_f[0:1, 0:1],
                           inc=(pe if j == 3 else None))
                sF("row_buf", pe, v)
            sR("biA", pe, v)
            W("vector", act, bd.R["KT"][1])
            for c in range(NKC):
                if c == 0:
                    op("vector", "reduce_max", out=kmax_col[:, :], in_=KT[:, c * 512:(c + 1) * 512],
                       axis=AX.X, apply_absolute_value=True)
                else:
                    op("vector", "reduce_max", out=km_tmp[:, :], in_=KT[:, c * 512:(c + 1) * 512],
                       axis=AX.X, apply_absolute_value=True)
                    lastv = op("vector", "tensor_tensor", out=kmax_col[:, :], in0=kmax_col[:, :],
                               in1=km_tmp[:, :], op=ALU.max,
                               inc=(dve if c == NKC - 1 else None))
            sR("kmax_col", dve, lastv)
            W("tensor", dve, lastv)
            wF("tensor", "kmA")
            v = op("tensor", "transpose", KMP[1][0:1, 0:128], kmax_col[:, :], ident_f[:, :],
                   inc=pe)
            sR("kmA", pe, v)
            W("vector", pe, v)
            op("vector", "reduce_max", out=km_s[:, :], in_=KMP[1][0:1, 0:128], axis=AX.X)
            v = op("vector", "tensor_tensor", out=km_s[:, :], in0=km_s[:, :], in1=km_s[:, :],
                   op=ALU.mult, inc=dve)
            sR("km2", dve, v)
            sF("kmA", dve, v)
            W("tensor", dve, v)
            wF("tensor", "kmA")
            v = op("tensor", "matmul", KMP[1][:, 0:1], ones_r_f[:, :], km_s[:, :],
                   start=True, stop=True, inc=pe)
            sR("km2b", pe, v)
            W("vector", pe, v)
            v = op("vector", "tensor_copy", out=kmax_col[:, :], in_=KMP[1][:, 0:1], inc=dve)
            sR("k2col", dve, v)
            sF("kmA", dve, v)
            W("vector", pe, bd.R["biA"][1])
            op("vector", "tensor_copy", out=braw[:, :], in_=BIP[1][:, :])
            v = op("vector", "tensor_scalar", out=braw[:, :], in0=braw[:, :],
                   scalar1=kmax_col[:, :], scalar2=1.0 / 128.0, op0=ALU.mult, op1=ALU.mult,
                   inc=dve)
            sF("biA", dve, v)
            W("scalar", dve, v)
            op("scalar", "activation", braw[:, :], braw[:, :], AF.Sqrt, bias=eps12[:, :], scale=1.0)
            v = op("scalar", "activation", bias_cols[:, :], braw[:, :], AF.Copy, bias=0.0,
                   scale=-1.0, inc=act)
            sR("bias_cols", act, v)

            W("tensor", act, bd.R["bias_cols"][1])
            W("tensor", dve, bd.R["Vtok"][1])
            units = []
            for qt in range(NQT):
                npr = ((qt + 1) * 128 + 1023) // 1024
                for kp in range(npr):
                    units.append((qt, kp, npr))
            defer = []

            def emit_trav(qt, kp, npr, par):
                cols = min(1024, (qt + 1) * 128 - kp * 1024)
                kbase = kp * 1024
                is_diag = (kp == npr - 1)
                nj = cols // 128
                pname, ptp = PTP[par]
                W("tensor", act, bd.R[f"p{par}"][1])
                wF("tensor", pname)
                for j in range(nj):
                    v = op("tensor", "transpose", ptp[:, j * 128:(j + 1) * 128],
                           p_sb[par][:, j * 128:(j + 1) * 128], ident_bf[:, :],
                           inc=(pe if j == nj - 1 else None))
                sR(pname, pe, v)
                sF(f"p{par}", pe, v)
                W("vector", pe, v)
                v = op("vector", "tensor_copy", out=pt_sb[par][:, 0:cols],
                       in_=ptp[:, 0:cols], inc=dve)
                sR(f"pt{par}", dve, v)
                sF(pname, dve, v)
                W("tensor", dve, v)
                if kp == 0:
                    wF("tensor", "avA")
                    wF("tensor", "avTA")
                for j in range(nj):
                    v = op("tensor", "matmul", AVP[1][:, :],
                           pt_sb[par][:, j * 128:(j + 1) * 128],
                           Vtok[:, (kbase + j * 128):(kbase + (j + 1) * 128)],
                           start=(kp == 0 and j == 0), stop=(is_diag and j == nj - 1),
                           inc=(pe if (is_diag and j == nj - 1) else None))
                if is_diag:
                    sR("avA", pe, v)
                    W("vector", act, bd.R[f"p{par}"][1])
                    wF("vector", "linv")
                    op("vector", "reduce_sum", out=lsum[:, :],
                       in_=lparts[:, (qt % 2) * 16:(qt % 2) * 16 + npr], axis=AX.X)
                    v = op("vector", "reciprocal", out=linv[:, :], in_=lsum[:, :], inc=dve)
                    sR("linv", dve, v)
                    sF(f"lparts{qt % 2}", dve, v)
                    W("scalar", dve, v)
                    W("scalar", pe, bd.R["avA"][1])
                    wF("scalar", "av_bf")
                    v = op("scalar", "activation", av_bf[:, :], AVP[1][:, :], AF.Identity,
                           bias=0.0, scale=linv[:, :], inc=act)
                    sR("av_bf", act, v)
                    sF("avA", act, v)
                    sF("linv", act, v)

                    def fin(qt=qt):
                        W("tensor", act, bd.R["av_bf"][1])
                        wF("tensor", "avTA")
                        v = op("tensor", "transpose", AVTP[1][:, :], av_bf[:, :],
                               ident_bf[:, :], inc=pe)
                        sR("avTA", pe, v)
                        sF("av_bf", pe, v)
                        W("vector", pe, v)
                        wF("vector", "avT_sb")
                        v2 = op("vector", "tensor_copy",
                                out=avT_sb[:, qt * 128:(qt + 1) * 128],
                                in_=AVTP[1][:, :], inc=dve)
                        sF("avTA", dve, v2)
                        if qt == NQT - 1:
                            sR("avT_done", dve, v2)
                    defer.append(fin)

            prev = None
            for ui, (qt, kp, npr) in enumerate(units):
                par = ui % 2
                scname, scp = SC[par]
                cols = min(1024, (qt + 1) * 128 - kp * 1024)
                kbase = kp * 1024
                is_diag = (kp == npr - 1)
                wF("tensor", scname)
                nm = (cols + 511) // 512
                for j in range(nm):
                    n = min(512, cols - j * 512)
                    v = op("tensor", "matmul", scp[:, j * 512:j * 512 + n],
                           QT[:, qt * 128:(qt + 1) * 128],
                           KT[:, kbase + j * 512:kbase + j * 512 + n],
                           start=True, stop=True, inc=(pe if j == nm - 1 else None))
                sR(scname, pe, v)
                if is_diag:
                    W("vector", pe, v)
                    v = op("vector", "tensor_tensor", out=scp[:, cols - 128:cols],
                           in0=scp[:, cols - 128:cols], in1=cmask[:, :], op=ALU.add, inc=dve)
                    sR(scname + "m", dve, v)
                W("scalar", pe, bd.R[scname][1])
                if is_diag:
                    W("scalar", dve, bd.R[scname + "m"][1])
                wF("scalar", f"p{par}")
                if kp == 0:
                    wF("scalar", f"lparts{qt % 2}")
                v = op("scalar", "activation", p_sb[par][:, 0:cols], scp[:, 0:cols],
                       AF.Exp, bias=bias_cols[:, qt:qt + 1], scale=INV_SQRT_HD,
                       accum_out=lparts[:, (qt % 2) * 16 + kp:(qt % 2) * 16 + kp + 1],
                       inc=act)
                sR(f"p{par}", act, v)
                sF(scname, act, v)
                fins, defer[:] = defer[:], []
                for f in fins:
                    f()
                if prev is not None:
                    emit_trav(*prev)
                prev = (qt, kp, npr, par)
            fins, defer[:] = defer[:], []
            for f in fins:
                f()
            emit_trav(*prev)
            for f in defer:
                f()
            W("gpsimd", dve, bd.R["avT_done"][1])
            v = bd.dma("gpsimd", dgp, out=ag_in[h * 128:(h + 1) * 128, :], in_=avT_sb[:, :])
            sF("avT_sb", dgp, v)
            if h == HPC - 1:
                sR("ag_in", dgp, v)

        PE_END_A = pe.n

        W("gpsimd", dgp, bd.R["ag_in"][1])
        v = op("gpsimd", "collective_compute", "AllGather", ALU.bypass,
               replica_groups=[[0, 1, 2, 3], [4, 5, 6, 7]],
               ins=[ag_in.ap().opt()], outs=[ag_out.ap().opt()], inc=cc)
        sR("ag", cc, v)

        if "dbg_att" in dbg:
            W("gpsimd", cc, bd.R["ag"][1])
            bd.dma("gpsimd", dgp, out=dbg["dbg_att"][:, :], in_=ag_out[:, :])

        POP = [("po0", pbank(0)), ("po1", pbank(1))]

        W("sync", pe, PE_END_A)
        v = bd.dma("sync", dsy, out=wout_sb[:, :], in_=wout[:, :])
        sR("wout", dsy, v)

        for tt in range(4):
            def agdma(e, tt=tt, nprev=None):
                e.wait_ge(dsy.h, nprev)
                with e.register(f"g2_{tt}") as g2:
                    e.reg_load(g2, coff_sb[:1, :1])
                    e.reg_add(g2, g2, tt * 512)
                    e.dma_start(
                        out=attT[:, :].rearrange("p (k t) -> p k t", t=512),
                        in_=bass.AP(ag_out, g2, [[S, 128], [128 * S, NKC], [1, 512]]),
                    ).then_inc(dsy.h, 16)
            W("sync", cc, bd.R["ag"][1])
            W("sync", dgp, bd.R["smallin"][1])
            wF("sync", "attT")
            _np = dsy.n
            bd.raw("sync", lambda e, f=agdma, n=_np: f(e, nprev=n))
            vA = dsy.inc(16)
            sR("attT", dsy, vA)
            wF("sync", "xr_t")
            v = bd.dma("sync", dsy, out=xr_t[:, :],
                   in_=bass.AP(xr, tt * 512, [[NKC * TC, 128], [TC, NKC], [1, 512]]))
            sR("xr_t", dsy, v)
            W("tensor", dsy, bd.R["attT"][1])
            if tt == 0:
                W("tensor", dsy, bd.R["wout"][1])
            for m in range(NKC):
                par = m % 2
                bname, psum = POP[par]
                wF("tensor", bname)
                for kc in range(NKC):
                    v = op("tensor", "matmul", psum[:, :],
                           wout_sb[:, kc * D + m * 128:kc * D + m * 128 + 128],
                           attT[:, kc * 512:(kc + 1) * 512],
                           start=(kc == 0), stop=(kc == NKC - 1),
                           inc=(pe if kc == NKC - 1 else None))
                sR(bname, pe, v)
                if m == NKC - 1:
                    sF("attT", pe, v)
                W("vector", pe, v)
                W("vector", dsy, bd.R["xr_t"][1])
                v = op("vector", "scalar_tensor_tensor",
                       out=r1[:, m * TC + tt * 512:m * TC + tt * 512 + 512],
                       in0=psum[:, :], scalar=bo_f[:, m:m + 1],
                       in1=xr_t[:, m * 512:(m + 1) * 512],
                       op0=ALU.add, op1=ALU.add, inc=dve)
                sF(bname, dve, v)
                if m == NKC - 1:
                    sF("xr_t", dve, v)
                    if tt == 3:
                        sR("r1", dve, v)

        SSUM = ("ssum", pbank(4)[0:1, :])
        SSQ = ("ssq", pbank(5)[0:1, :])
        BCP = ("bcp", pbank(6))
        W("scalar", dve, bd.R["r1"][1])
        for tt in range(4):
            for kc in range(NKC):
                par = kc % 2
                W("tensor", dve, bd.R["r1"][1])
                if kc == 0:
                    wF("tensor", "ssum")
                    wF("tensor", "ssq")
                v = op("tensor", "matmul", SSUM[1][:, :], ones_c_bf[:, :],
                       r1[:, kc * TC + tt * 512:kc * TC + tt * 512 + 512],
                       start=(kc == 0), stop=(kc == NKC - 1),
                       inc=(pe if kc == NKC - 1 else None))
                wF("scalar", f"sqB{par}")
                v2 = op("scalar", "activation", sq_tmp[par][:, :],
                        r1[:, kc * TC + tt * 512:kc * TC + tt * 512 + 512],
                        AF.Square, bias=0.0, scale=1.0, inc=act)
                sR(f"sqB{par}", act, v2)
                W("tensor", act, v2)
                v = op("tensor", "matmul", SSQ[1][:, :], ones_c_bf[:, :], sq_tmp[par][:, :],
                       start=(kc == 0), stop=(kc == NKC - 1), inc=pe)
                sF(f"sqB{par}", pe, v)
            sR("stats", pe, v)
            W("vector", pe, bd.R["stats"][1])
            op("vector", "tensor_copy", out=srow[0:1, tt * 512:(tt + 1) * 512], in_=SSUM[1][:, :])
            v = op("vector", "tensor_copy", out=sqrow[0:1, tt * 512:(tt + 1) * 512],
                   in_=SSQ[1][:, :], inc=dve)
            sR("statrows", dve, v)
            sF("ssum", dve, v)
            sF("ssq", dve, v)
        op("vector", "tensor_scalar", out=mean_r[:, :], in0=srow[:, :],
           scalar1=1.0 / D, scalar2=None, op0=ALU.mult)
        op("vector", "tensor_tensor", out=mrstd_r[:, :], in0=mean_r[:, :], in1=mean_r[:, :],
           op=ALU.mult)
        v = op("vector", "scalar_tensor_tensor", out=rstd_r[:, :], in0=sqrow[:, :],
               scalar=1.0 / D, in1=mrstd_r[:, :], op0=ALU.mult, op1=ALU.subtract, inc=dve)
        W("scalar", dve, v)
        op("scalar", "activation", rstd_r[:, :], rstd_r[:, :], AF.Ln, bias=eps5[0:1, :], scale=1.0)
        v = op("scalar", "activation", rstd_r[:, :], rstd_r[:, :], AF.Exp, bias=0.0,
               scale=-0.5, inc=act)
        sR("rstd_r", act, v)
        W("vector", act, v)
        v = op("vector", "tensor_tensor", out=mrstd_r[:, :], in0=mean_r[:, :], in1=rstd_r[:, :],
               op=ALU.mult, inc=dve)
        sR("mrstd_r", dve, v)
        for tt in range(4):
            W("tensor", dve, bd.R["mrstd_r"][1])
            wF("tensor", "bcp")
            v = op("tensor", "matmul", BCP[1][:, :], ones_r_f[:, :],
                   rstd_r[0:1, tt * 512:(tt + 1) * 512], start=True, stop=True, inc=pe)
            W("vector", pe, v)
            v = op("vector", "tensor_copy", out=rstd_b[:, tt * 512:(tt + 1) * 512],
                   in_=BCP[1][:, :], inc=dve)
            sF("bcp", dve, v)
            wF("tensor", "bcp")
            v = op("tensor", "matmul", BCP[1][:, :], ones_r_f[:, :],
                   mrstd_r[0:1, tt * 512:(tt + 1) * 512], start=True, stop=True, inc=pe)
            W("vector", pe, v)
            v = op("vector", "tensor_copy", out=mrstd_b[:, tt * 512:(tt + 1) * 512],
                   in_=BCP[1][:, :], inc=dve)
            sF("bcp", dve, v)
            if tt == 3:
                sR("bcast1", dve, v)

        LGP = [(f"lg{tt}", pbank(tt)[0:8, :]) for tt in range(4)]
        for tt in range(4):
            for kc in range(NKC):
                par = kc % 2
                W("vector", dve, bd.R["bcast1"][1])
                op("vector", "tensor_tensor", out=lnt[:, :],
                   in0=r1[:, kc * TC + tt * 512:kc * TC + tt * 512 + 512],
                   in1=rstd_b[:, tt * 512:(tt + 1) * 512], op=ALU.mult)
                wF("vector", f"x1s{par}")
                wF("vector", f"x1sD{par}")
                v = op("vector", "tensor_tensor", out=x1s[par][:, :], in0=lnt[:, :],
                       in1=mrstd_b[:, tt * 512:(tt + 1) * 512], op=ALU.subtract, inc=dve)
                sR(f"x1s{par}", dve, v)
                W("scalar", dve, v)
                v = bd.dma("scalar", dsc,
                       out=bass.AP(x1d, kc * 128 * TC + tt * 512, [[TC, 128], [1, 512]]),
                       in_=x1s[par][:, :])
                sF(f"x1sD{par}", dsc, v)
                vx1 = v
                W("tensor", dve, bd.R[f"x1s{par}"][1])
                if tt == 0 and kc == 0:
                    W("tensor", pool, bd.R["wg_lo"][1])
                    wF("tensor", f"lgb{tt}")
                op("tensor", "matmul", LGP[tt][1][:, :], wg_hi[:, kc * 8:(kc + 1) * 8],
                   x1s[par][:, :], start=(kc == 0), stop=False)
                v = op("tensor", "matmul", LGP[tt][1][:, :], wg_lo[:, kc * 8:(kc + 1) * 8],
                       x1s[par][:, :], start=False, stop=(kc == NKC - 1), inc=pe)
                sF(f"x1s{par}", pe, v)
                if kc == NKC - 1:
                    sR(LGP[tt][0], pe, v)
        sR("x1d_all", dsc, vx1)

        TRP = ("trp", pbank(4))
        CMP_ = ("cmp2", pa[0:8, 0:2048])
        for tt in range(4):
            W("vector", pe, bd.R[LGP[tt][0]][1])
            v = op("vector", "tensor_copy", out=lg_sb[:, tt * 512:(tt + 1) * 512],
                   in_=LGP[tt][1][:, :], inc=dve)
            if tt == 3:
                sR("lg", dve, v)
        W("tensor", dve, bd.R["lg"][1])
        wF("tensor", "trp")
        for ti in range(NKC):
            v = op("tensor", "transpose", TRP[1][:, ti * 8:(ti + 1) * 8],
                   lg_sb[0:8, ti * 128:(ti + 1) * 128], ident_f[0:8, 0:8],
                   inc=(pe if ti == NKC - 1 else None))
        sR("trp", pe, v)
        W("vector", pe, v)
        v = op("vector", "tensor_copy", out=ltok[:, :], in_=TRP[1][:, 0:NKC * 8], inc=dve)
        sF("trp", dve, v)
        for ti in range(NKC):
            lt = ltok[:, ti * 8:(ti + 1) * 8]
            op("vector", "reduce_max", out=m1c[:, :], in_=lt, axis=AX.X)
            op("vector", "reduce_max", out=nm1c[:, :], in_=lt, axis=AX.X, negate=True)
            op("vector", "tensor_scalar", out=msk8[:, :], in0=lt, scalar1=m1c[:, :],
               scalar2=None, op0=ALU.is_ge)
            op("vector", "scalar_tensor_tensor", out=msk8[:, :], in0=msk8[:, :],
               scalar=-1e30, in1=lt, op0=ALU.mult, op1=ALU.add)
            v = op("vector", "reduce_max", out=m2c[:, :], in_=msk8[:, :], axis=AX.X, inc=dve)
            W("scalar", dve, v)
            op("scalar", "activation", e1t[:, :], lt, AF.Exp, bias=nm1c[:, :], scale=1.0)
            v = op("scalar", "activation", e2c[:, :], m2c[:, :], AF.Exp, bias=nm1c[:, :],
                   scale=1.0, inc=act)
            W("vector", act, v)
            op("vector", "tensor_scalar", out=e2c[:, :], in0=e2c[:, :], scalar1=1.0,
               scalar2=None, op0=ALU.add)
            op("vector", "reciprocal", out=dinvc[:, :], in_=e2c[:, :])
            op("vector", "tensor_scalar", out=msk8[:, :], in0=lt, scalar1=m2c[:, :],
               scalar2=None, op0=ALU.is_ge)
            v = op("vector", "scalar_tensor_tensor", out=ltok[:, ti * 8:(ti + 1) * 8],
                   in0=e1t[:, :], scalar=dinvc[:, :], in1=msk8[:, :],
                   op0=ALU.mult, op1=ALU.mult, inc=dve)
            sR(f"cmb{ti}", dve, v)
        W("tensor", dve, bd.R[f"cmb{NKC - 1}"][1])
        for ti in range(NKC):
            v = op("tensor", "transpose", CMP_[1][:, ti * 128:(ti + 1) * 128],
                   ltok[:, ti * 8:(ti + 1) * 8], ident_f[:, :],
                   inc=(pe if ti == NKC - 1 else None))
        sR("cmp2", pe, v)
        W("vector", pe, v)
        v = op("vector", "tensor_copy", out=lg_sb[:, :], in_=CMP_[1][:, :], inc=dve)
        sR("cmbT", dve, v)
        sF("cmp2", dve, v)
        W("gpsimd", dve, v)
        v = bd.dma("gpsimd", dgp, out=cmb_d[:, :], in_=lg_sb[:, :])
        sR("cmb_d", dgp, v)

        if "dbg_x1" in dbg:
            W("gpsimd", dsc, bd.R["x1d_all"][1])
            bd.dma("gpsimd", dgp, out=dbg["dbg_x1"][:, :, :], in_=x1d[:, :, :])
        if "dbg_cmb" in dbg:
            W("gpsimd", dgp, bd.R["cmb_d"][1])
            bd.dma("gpsimd", dgp, out=dbg["dbg_cmb"][:, :], in_=cmb_d[:, :])

        GUP = [(f"gu{i}", pbank(i)) for i in range(4)]
        CBP = ("cbp", pbank(6))
        S2SUM = ("s2sum", pbank(4)[0:1, :])
        S2SQ = ("s2sq", pbank(5)[0:1, :])
        gu_rot = [0]
        nstage = [0]

        def stream_weight(src_ap):
            par = nstage[0] % 2
            nstage[0] += 1
            wF("sync", f"wst{par}")
            v = bd.dma("sync", dsy, out=wstage[par][:, :], in_=src_ap)
            sR(f"wst{par}", dsy, v)
            W("vector", dsy, v)
            wF("vector", f"whi{par}")
            v = op("vector", "tensor_copy", out=w_hi[par][:, :], in_=wstage[par][:, :], inc=dve)
            sR(f"whi{par}", dve, v)
            W("gpsimd", dve, v)
            wF("gpsimd", f"wlo{par}")
            v = op("gpsimd", "tensor_tensor", out=w_lo[par][:, :], in0=wstage[par][:, :],
                   in1=w_hi[par][:, :], op=ALU.subtract, inc=pool)
            sR(f"wlo{par}", pool, v)
            sF(f"wst{par}", pool, v)
            return par

        def gate_up_part(par_w, psrc_is_g, jp, tts_act_col, e_cmb):
            for tts in range(2):
                rot = gu_rot[0] % 4
                gu_rot[0] += 1
                bname, psum = GUP[rot]
                wF("tensor", bname)
                W("tensor", dve, bd.R[f"whi{par_w}"][1])
                W("tensor", pool, bd.R[f"wlo{par_w}"][1])
                nmm = 0
                for kc in range(NKC):
                    acol = kc * TSW + tts * 512
                    for wt in (w_hi[par_w], w_lo[par_w]):
                        nmm += 1
                        v = op("tensor", "matmul", psum[:, :],
                               wt[:, kc * 128:(kc + 1) * 128], x1_sw[:, acol:acol + 512],
                               start=(nmm == 1), stop=(nmm == NKC * 2),
                               inc=(pe if nmm == NKC * 2 else None))
                sR(bname, pe, v)
                if tts == 1:
                    sF(f"whi{par_w}", pe, v)
                    sF(f"wlo{par_w}", pe, v)
                if psrc_is_g:
                    W("scalar", pe, v)
                    wF("scalar", f"silu{tts}")
                    v = op("scalar", "activation", silu_g[:, tts * 512:(tts + 1) * 512],
                           psum[:, :], AF.Silu, bias=0.0, scale=1.0, inc=act)
                    sR(f"siluR{tts}", act, v)
                    sF(bname, act, v)
                else:
                    W("vector", pe, v)
                    W("vector", act, bd.R[f"siluR{tts}"][1])
                    if e_cmb is not None:
                        W("vector", dve, bd.R[f"cmb_bR{tts}"][1])
                        op("vector", "tensor_tensor", out=act_f[:, tts * 512:(tts + 1) * 512],
                           in0=psum[:, :], in1=cmb_b[:, tts * 512:(tts + 1) * 512], op=ALU.mult)
                        v = op("vector", "tensor_tensor",
                               out=act_f[:, tts * 512:(tts + 1) * 512],
                               in0=act_f[:, tts * 512:(tts + 1) * 512],
                               in1=silu_g[:, tts * 512:(tts + 1) * 512], op=ALU.mult, inc=dve)
                    else:
                        v = op("vector", "tensor_tensor",
                               out=act_f[:, tts * 512:(tts + 1) * 512], in0=psum[:, :],
                               in1=silu_g[:, tts * 512:(tts + 1) * 512], op=ALU.mult, inc=dve)
                    sF(bname, dve, v)
                    sF(f"silu{tts}", dve, v)
                    acol = tts_act_col + tts * 512
                    wF("vector", "act_hi")
                    v = op("vector", "tensor_copy", out=act_hi[:, acol:acol + 512],
                           in_=act_f[:, tts * 512:(tts + 1) * 512], inc=dve)
                    sR(f"ahi{jp}_{tts}", dve, v)
                    W("gpsimd", dve, v)
                    wF("gpsimd", "act_lo")
                    v = op("gpsimd", "tensor_tensor", out=act_lo[:, acol:acol + 512],
                           in0=act_f[:, tts * 512:(tts + 1) * 512],
                           in1=act_hi[:, acol:acol + 512], op=ALU.subtract, inc=pool)
                    sR(f"alo{jp}_{tts}", pool, v)

        def down_proj(src_fn, first_acc):
            for ds in range(8):
                par = stream_weight(src_fn(ds))
                for d2 in range(2):
                    for tts in range(2):
                        rot = gu_rot[0] % 4
                        gu_rot[0] += 1
                        bname, psum = GUP[rot]
                        wF("tensor", bname)
                        W("tensor", dve, bd.R[f"whi{par}"][1])
                        W("tensor", pool, bd.R[f"wlo{par}"][1])
                        nmm = 0
                        for ic in range(8):
                            wcol = ic * 256 + d2 * 128
                            acol = ic * TSW + tts * 512
                            for wt, at in ((w_hi[par], act_hi), (w_hi[par], act_lo),
                                           (w_lo[par], act_hi)):
                                nmm += 1
                                v = op("tensor", "matmul", psum[:, :],
                                       wt[:, wcol:wcol + 128], at[:, acol:acol + 512],
                                       start=(nmm == 1), stop=(nmm == 24),
                                       inc=(pe if nmm == 24 else None))
                        sR(bname, pe, v)
                        if d2 == 1 and tts == 1:
                            sF(f"whi{par}", pe, v)
                            sF(f"wlo{par}", pe, v)
                        dch = ds * 2 + d2
                        W("vector", pe, v)
                        accap = ffn_acc[:, dch * TSW + tts * 512:dch * TSW + tts * 512 + 512]
                        if first_acc[dch * 2 + tts]:
                            first_acc[dch * 2 + tts] = False
                            v = op("vector", "tensor_copy", out=accap, in_=psum[:, :], inc=dve)
                        else:
                            v = op("vector", "tensor_tensor", out=accap, in0=accap,
                                   in1=psum[:, :], op=ALU.add, inc=dve)
                        sF(bname, dve, v)

        for sw in range(SW):
            wF("sync", "x1_sw")
            W("sync", dsc, bd.R["x1d_all"][1])
            W("sync", dgp, bd.R["cmb_d"][1])
            if True:
                v = bd.dma("sync", dsy,
                       out=x1_sw[:, :].rearrange("p (k t) -> p k t", t=TSW),
                       in_=bass.AP(x1d, sw * TSW, [[TC, 128], [128 * TC, NKC], [1, TSW]]))
            sR("x1_sw", dsy, v)
            W("tensor", dsy, v)

            first_acc = [True] * (NKC * 2)
            for e in range(E):
                wF("gpsimd", "cmb_r")
                vcr = bd.dma("gpsimd", dgp, out=cmb_r[:, :],
                         in_=cmb_d[e:e + 1, sw * TSW:(sw + 1) * TSW])
                sR("cmb_rR", dgp, vcr)
                for tts in range(2):
                    W("tensor", dgp, bd.R["cmb_rR"][1])
                    wF("tensor", "cbp")
                    v = op("tensor", "matmul", CBP[1][:, :], ones_r_f[:, :],
                           cmb_r[0:1, tts * 512:(tts + 1) * 512],
                           start=True, stop=True, inc=pe)
                    sR("cbp", pe, v)
                    sF("cmb_r", pe, v)
                    W("vector", pe, v)
                    wF("vector", f"cmb_b{tts}")
                    v = op("vector", "tensor_copy", out=cmb_b[:, tts * 512:(tts + 1) * 512],
                           in_=CBP[1][:, :], inc=dve)
                    sR(f"cmb_bR{tts}", dve, v)
                    sF("cbp", dve, v)
                for jp in range(8):
                    parg = stream_weight(wgu[e, jp, :, :])
                    gate_up_part(parg, True, jp, None, None)
                    paru = stream_weight(wgu[e, jp + 8, :, :])
                    gate_up_part(paru, False, jp, jp * TSW, e)
                down_proj(lambda ds, e=e: wdn[e, ds, :, :], first_acc)
                sF("act_hi", pe, pe.n)
                sF("act_lo", pe, pe.n)
                sF("silu0", pe, pe.n)
                sF("silu1", pe, pe.n)
                sF("cmb_b0", pe, pe.n)
                sF("cmb_b1", pe, pe.n)
            for ig in range(4):
                for jp in range(8):
                    parg = stream_weight(wsgu[ig * 8 + jp, :, :])
                    gate_up_part(parg, True, jp, None, None)
                    paru = stream_weight(wsgu[32 + ig * 8 + jp, :, :])
                    gate_up_part(paru, False, jp, jp * TSW, None)
                down_proj(lambda ds, ig=ig: wsdn[ig, ds, :, :], first_acc)
                sF("act_hi", pe, pe.n)
                sF("act_lo", pe, pe.n)
                sF("silu0", pe, pe.n)
                sF("silu1", pe, pe.n)

            for dch in range(NKC):
                ap_ = ffn_acc[:, dch * TSW:(dch + 1) * TSW]
                v = op("vector", "tensor_tensor", out=ap_,
                       in0=x1_sw[:, dch * TSW:(dch + 1) * TSW], in1=ap_, op=ALU.add, inc=dve)
                if dch == NKC - 1:
                    sR("r2", dve, v)
                    sF("x1_sw", dve, v)
            for tts in range(2):
                for dch in range(NKC):
                    W("tensor", dve, bd.R["r2"][1])
                    rc = ffn_acc[:, dch * TSW + tts * 512:dch * TSW + tts * 512 + 512]
                    if dch == 0:
                        wF("tensor", "s2sum")
                        wF("tensor", "s2sq")
                    v = op("tensor", "matmul", S2SUM[1][:, :], ones_c_f[:, :], rc,
                           start=(dch == 0), stop=(dch == NKC - 1),
                           inc=(pe if dch == NKC - 1 else None))
                    W("scalar", dve, bd.R["r2"][1])
                    wF("scalar", "sq2_0")
                    v2 = op("scalar", "activation", sq2[0][:, :], rc, AF.Square,
                            bias=0.0, scale=1.0, inc=act)
                    sR("sq2R", act, v2)
                    W("tensor", act, v2)
                    v = op("tensor", "matmul", S2SQ[1][:, :], ones_c_f[:, :], sq2[0][:, :],
                           start=(dch == 0), stop=(dch == NKC - 1), inc=pe)
                    sF("sq2_0", pe, v)
                sR("stats2", pe, v)
                W("vector", pe, bd.R["stats2"][1])
                op("vector", "tensor_copy", out=s2row[:, :], in_=S2SUM[1][:, :])
                v = op("vector", "tensor_copy", out=sq2row[:, :], in_=S2SQ[1][:, :], inc=dve)
                sF("s2sum", dve, v)
                sF("s2sq", dve, v)
                op("vector", "tensor_scalar", out=mean2_r[:, :], in0=s2row[:, :],
                   scalar1=1.0 / D, scalar2=None, op0=ALU.mult)
                op("vector", "tensor_tensor", out=mrstd2_r[:, :], in0=mean2_r[:, :],
                   in1=mean2_r[:, :], op=ALU.mult)
                v = op("vector", "scalar_tensor_tensor", out=rstd2_r[:, :], in0=sq2row[:, :],
                       scalar=1.0 / D, in1=mrstd2_r[:, :], op0=ALU.mult, op1=ALU.subtract,
                       inc=dve)
                W("scalar", dve, v)
                op("scalar", "activation", rstd2_r[:, :], rstd2_r[:, :], AF.Ln, bias=eps5[0:1, :],
                   scale=1.0)
                v = op("scalar", "activation", rstd2_r[:, :], rstd2_r[:, :], AF.Exp, bias=0.0,
                       scale=-0.5, inc=act)
                W("vector", act, v)
                v = op("vector", "tensor_tensor", out=mrstd2_r[:, :], in0=mean2_r[:, :],
                       in1=rstd2_r[:, :], op=ALU.mult, inc=dve)
                sR("mrstd2", dve, v)
                W("tensor", dve, v)
                wF("tensor", "cbp")
                v = op("tensor", "matmul", CBP[1][:, :], ones_r_f[:, :], rstd2_r[0:1, :],
                       start=True, stop=True, inc=pe)
                W("vector", pe, v)
                v = op("vector", "tensor_copy", out=rstd2_b[:, :], in_=CBP[1][:, :], inc=dve)
                sF("cbp", dve, v)
                wF("tensor", "cbp")
                v = op("tensor", "matmul", CBP[1][:, :], ones_r_f[:, :], mrstd2_r[0:1, :],
                       start=True, stop=True, inc=pe)
                W("vector", pe, v)
                v = op("vector", "tensor_copy", out=mrstd2_b[:, :], in_=CBP[1][:, :], inc=dve)
                sF("cbp", dve, v)
                for dch in range(NKC):
                    par = dch % 2
                    rc = ffn_acc[:, dch * TSW + tts * 512:dch * TSW + tts * 512 + 512]
                    op("vector", "tensor_tensor", out=t2f[:, :], in0=rc, in1=rstd2_b[:, :],
                       op=ALU.mult)
                    wF("vector", f"ost{par}")
                    v = op("vector", "tensor_tensor", out=ost[par][:, :], in0=t2f[:, :],
                           in1=mrstd2_b[:, :], op=ALU.subtract, inc=dve)
                    W("scalar", dve, v)
                    v = bd.dma("scalar", dsc,
                           out=out_d[dch * 128:(dch + 1) * 128,
                                     sw * TSW + tts * 512:sw * TSW + tts * 512 + 512],
                           in_=ost[par][:, :])
                    sF(f"ost{par}", dsc, v)

        W("scalar", dsc, dsc.n)
        W("sync", dsy, dsy.n)
        W("gpsimd", dgp, dgp.n)

        with nc.Block() as block:
            @block.tensor
            def _(eng):
                for f in bd.plan["tensor"]:
                    f(eng)

            @block.vector
            def _(eng):
                for f in bd.plan["vector"]:
                    f(eng)

            @block.scalar
            def _(eng):
                for f in bd.plan["scalar"]:
                    f(eng)

            @block.gpsimd
            def _(eng):
                for f in bd.plan["gpsimd"]:
                    f(eng)

            @block.sync
            def _(eng):
                for f in bd.plan["sync"]:
                    f(eng)

    return nc



def _layout_inputs(inputs):
    x = np.asarray(inputs["L0_attn_x"])
    Wqkv = np.asarray(inputs["L0_attn_W_qkv"])
    bqkv = np.asarray(inputs["L0_attn_b_qkv"])
    Wout = np.asarray(inputs["L0_attn_W_out"])
    bout = np.asarray(inputs["L0_attn_b_out"])
    wg = np.asarray(inputs["L0_ffn_w_gate"])
    wgu = np.asarray(inputs["L0_ffn_w_gate_up"])
    wdn = np.asarray(inputs["L0_ffn_w_down"])
    wsgu = np.asarray(inputs["L0_ffn_w_sh_gate_up"])
    wsdn = np.asarray(inputs["L0_ffn_w_sh_down"])

    wout_h = np.ascontiguousarray(
        Wout.reshape(16, 128, 2048).transpose(1, 0, 2).reshape(128, 16 * 2048))
    bout_h = np.ascontiguousarray(bout.reshape(16, 128).T)
    wg_h = np.ascontiguousarray(
        wg.reshape(16, 128, 8).transpose(1, 0, 2).reshape(128, 128))
    wgu_h = np.ascontiguousarray(
        wgu.reshape(8, 16, 128, 16, 128).transpose(0, 3, 2, 1, 4).reshape(8, 16, 128, 2048))
    wdn_h = np.ascontiguousarray(
        wdn.reshape(8, 8, 128, 8, 256).transpose(0, 3, 2, 1, 4).reshape(8, 8, 128, 2048))
    wsgu_h = np.ascontiguousarray(
        wsgu.reshape(16, 128, 64, 128).transpose(2, 1, 0, 3).reshape(64, 128, 2048))
    wsdn_h = np.ascontiguousarray(
        wsdn.reshape(4, 8, 128, 8, 256).transpose(0, 3, 2, 1, 4).reshape(4, 8, 128, 2048))

    Wq5 = Wqkv.reshape(16, 128, 3, 16, 128)
    bq3 = bqkv.reshape(3, 16, 128)

    in_maps = []
    for c in range(8):
        b, hg = c // 4, c % 4
        s = c % 4
        xa_h = np.ascontiguousarray(
            x[b].reshape(NTA, TTA, 16, 128).transpose(0, 3, 2, 1).reshape(NTA, 128, 16 * TTA))
        xr_h = np.ascontiguousarray(
            x[b, s * TC:(s + 1) * TC].reshape(TC, 16, 128).transpose(2, 1, 0).reshape(128, 16 * TC))
        wqkv_h = np.ascontiguousarray(
            Wq5[:, :, :, hg * 4:(hg + 1) * 4, :].transpose(1, 0, 3, 2, 4).reshape(128, 16 * 1536))
        bq_h = np.ascontiguousarray(
            bq3[:, hg * 4:(hg + 1) * 4, :].transpose(2, 1, 0).reshape(128, 12))
        in_maps.append({
            "xa": xa_h, "xr": xr_h, "wqkv": wqkv_h, "bqkv": bq_h,
            "wout": wout_h, "bout": bout_h, "wg": wg_h,
            "wgu": wgu_h, "wdn": wdn_h, "wsgu": wsgu_h, "wsdn": wsdn_h,
            "coff": np.array([[s * TC]], dtype=np.int32),
        })
    return in_maps


def kernel(**inputs):
    in_maps = _layout_inputs(inputs)
    nc = build_nc(DEBUG_TAPS)
    res = run_bass_kernel_spmd(nc, in_maps, core_ids=list(range(8)))
    out = np.empty((B, S, D), np.float32)
    for c in range(8):
        b, s = c // 4, c % 4
        out[b, s * TC:(s + 1) * TC, :] = res.results[c]["out"].T
    if DEBUG_TAPS:
        kernel.last_results = res
    return out


# version 03 (final)
# speedup vs baseline: 1.3598x; 1.1947x over previous
import numpy as np
import ml_dtypes
from contextlib import ExitStack

import concourse.bass as bass
import concourse.mybir as mybir
from concourse.bass_utils import run_bass_kernel_spmd

F32 = mybir.dt.float32
BF16 = mybir.dt.bfloat16
I32 = mybir.dt.int32
U8 = mybir.dt.uint8
AF = mybir.ActivationFunctionType
ALU = mybir.AluOpType
AX = mybir.AxisListType

B, S, D, H, HD = 2, 8192, 2048, 16, 128
HPC = 4
NKC = D // 128
TC = 2048
TTA = 512
NTA = S // TTA
NQT = S // 128
E, IM, ISH = 8, 1024, 4096
SW, TSW = 2, 1024
INV_SQRT_HD = float(1.0 / np.sqrt(HD))

ENGS = ("tensor", "vector", "scalar", "gpsimd", "sync")
DEBUG_TAPS = []


class Sem:
    def __init__(self, h):
        self.h = h
        self.n = 0

    def inc(self, k=1):
        self.n += k
        return self.n


class Builder:
    SELF_EDGE = {"vector", "scalar", "gpsimd"}
    ASYNC_METHS = {"dma_start", "collective_compute", "indirect_dma_start"}

    def __init__(self, selfsems=None):
        self.plan = {e: [] for e in ENGS}
        self.R = {}
        self.F = {}
        self.selfsems = selfsems or {}

    def op(self, eng, meth, *a, inc=None, k=1, **kw):
        if eng in self.SELF_EDGE and meth not in self.ASYNC_METHS:
            sem = self.selfsems[eng]
            if sem.n > 0:
                self.wait(eng, sem, sem.n)
            if inc is None:
                inc, k = sem, 1
            else:
                assert inc is sem, (eng, meth)
        if inc is None:
            self.plan[eng].append(lambda e, m=meth, aa=a, kk=kw: getattr(e, m)(*aa, **kk))
            return None
        v = inc.inc(k)
        self.plan[eng].append(
            lambda e, m=meth, aa=a, kk=kw, s=inc.h, n=k: getattr(e, m)(*aa, **kk).then_inc(s, n))
        return v

    def dma(self, eng, qsem, **kw):
        self.wait(eng, qsem, qsem.n)
        return self.op(eng, "dma_start", inc=qsem, k=16, **kw)

    def wait(self, eng, sem, val):
        if val and val > 0:
            self.plan[eng].append(lambda e, s=sem.h, v=val: e.wait_ge(s, v))

    def raw(self, eng, fn):
        self.plan[eng].append(fn)

    def wR(self, eng, *names):
        for n in names:
            g = self.R.get(n)
            if g is not None:
                self.wait(eng, g[0], g[1])

    def wF(self, eng, *names):
        for n in names:
            g = self.F.get(n)
            if g is not None:
                self.wait(eng, g[0], g[1])

    def sR(self, name, sem, val):
        self.R[name] = (sem, val)

    def sF(self, name, sem, val):
        self.F[name] = (sem, val)


def build_nc(debug_taps=()):
    nc = bass.Bass("TRN2", target_bir_lowering=False)

    xa = nc.dram_tensor("xa", [NTA, 128, NKC * TTA], BF16, kind="ExternalInput")
    xr = nc.dram_tensor("xr", [128, NKC * TC], BF16, kind="ExternalInput")
    wqkv = nc.dram_tensor("wqkv", [128, NKC * HPC * 3 * 128], BF16, kind="ExternalInput")
    bqkv = nc.dram_tensor("bqkv", [128, HPC * 3], BF16, kind="ExternalInput")
    wout = nc.dram_tensor("wout", [128, NKC * D], BF16, kind="ExternalInput")
    bout = nc.dram_tensor("bout", [128, NKC], BF16, kind="ExternalInput")
    wg = nc.dram_tensor("wg", [128, NKC * E], F32, kind="ExternalInput")
    wgu = nc.dram_tensor("wgu", [E, 16, 128, NKC * 128], F32, kind="ExternalInput")
    wdn = nc.dram_tensor("wdn", [E, 8, 128, 8 * 256], F32, kind="ExternalInput")
    wsgu = nc.dram_tensor("wsgu", [64, 128, NKC * 128], F32, kind="ExternalInput")
    wsdn = nc.dram_tensor("wsdn", [4, 8, 128, 8 * 256], F32, kind="ExternalInput")
    coff = nc.dram_tensor("coff", [1, 1], I32, kind="ExternalInput")
    out_d = nc.dram_tensor("out", [D, TC], F32, kind="ExternalOutput")

    ag_in = nc.dram_tensor("ag_in", [HPC * 128, S], BF16)
    ag_out = nc.dram_tensor("ag_out", [D, S], BF16)
    x1d = nc.dram_tensor("x1d", [NKC, 128, TC], BF16)
    cmb_d = nc.dram_tensor("cmb_d", [8, TC], F32)

    dbg = {}
    for name, shape, dt in debug_taps:
        dbg[name] = nc.dram_tensor(name, shape, dt, kind="ExternalOutput")

    with ExitStack() as es:
        ARENA = 210944
        arena = es.enter_context(nc.sbuf_tensor("arena", [128, ARENA], U8))
        abase = nc.lookup_mloc(arena).addr

        def sb(name, shape, dtype, off):
            per = int(np.prod(shape[1:])) * mybir.dt.size(dtype)
            assert off + per <= ARENA, (name, off, per)
            return nc.alloc_sbuf_tensor_at(name, shape, dtype, offset=abase + off)

        ident_bf = sb("ident_bf", [128, 128], BF16, 0)
        ident_f = sb("ident_f", [128, 128], F32, 256)
        cmask = sb("cmask", [128, 128], F32, 768)
        ones_c_bf = sb("ones_c_bf", [128, 1], BF16, 1280)
        ones_c_f = sb("ones_c_f", [128, 1], F32, 1312)
        ones_r_f = sb("ones_r_f", [1, 128], F32, 1344)
        bq_bf = sb("bq_bf", [128, HPC * 3], BF16, 1856)
        bq_f = sb("bq_f", [128, HPC * 3], F32, 1888)
        bo_bf = sb("bo_bf", [128, NKC], BF16, 1952)
        bo_f = sb("bo_f", [128, NKC], F32, 1984)
        wg_f = sb("wg_f", [128, NKC * E], F32, 2048)
        wg_hi = sb("wg_hi", [128, NKC * E], BF16, 2560)
        wg_lo = sb("wg_lo", [128, NKC * E], BF16, 2816)
        eps12 = sb("eps12", [128, 1], F32, 205824)
        eps5 = sb("eps5", [128, 1], F32, 205856)
        coff_sb = sb("coff_sb", [1, 1], I32, 205888)

        wqkv_sb = sb("wqkv_sb", [128, NKC * 1536], BF16, 3072)
        xt = [sb(f"xt{i}", [128, NKC * TTA], BF16, 52224 + i * 16384) for i in range(2)]
        QT = sb("QT", [128, S], BF16, 84992)
        KT = sb("KT", [128, S], BF16, 101376)
        VT = sb("VT", [128, S], BF16, 117760)
        Vtok = sb("Vtok", [128, S], BF16, 134144)
        avT_sb = sb("avT_sb", [128, S], BF16, 150528)
        p_sb = [sb(f"p_sb{i}", [128, 1024], BF16, 166912 + i * 2048) for i in range(2)]
        pt_sb = [sb(f"pt_sb{i}", [128, 1024], BF16, 171008 + i * 2048) for i in range(2)]
        sq_tmp = [sb(f"sq_tmp{i}", [128, 512], BF16, 175104 + i * 1024) for i in range(2)]
        row_buf = sb("row_buf", [1, 512], F32, 177152)
        braw = sb("braw", [128, NQT], F32, 179200)
        bias_cols = sb("bias_cols", [128, NQT], F32, 179456)
        km_tmp = sb("km_tmp", [128, 1], F32, 179712)
        kmax_col = sb("kmax_col", [128, 1], F32, 179744)
        km_s = sb("km_s", [1, 1], F32, 179776)
        lparts = sb("lparts", [128, 32], F32, 179808)
        lsum = sb("lsum", [128, 1], F32, 179936)
        linv = sb("linv", [128, 1], F32, 179968)
        av_bf = sb("av_bf", [128, 128], BF16, 180000)

        wout_sb = sb("wout_sb", [128, NKC * D], BF16, 3072)
        attT = sb("attT", [128, NKC * 512], BF16, 68608)
        xr_t = sb("xr_t", [128, NKC * 512], BF16, 84992)
        r1 = sb("r1", [128, NKC * TC], BF16, 101376)

        srow = sb("srow", [1, TC], F32, 3072)
        sqrow = sb("sqrow", [1, TC], F32, 11264)
        mean_r = sb("mean_r", [1, TC], F32, 19456)
        rstd_r = sb("rstd_r", [1, TC], F32, 27648)
        mrstd_r = sb("mrstd_r", [1, TC], F32, 35840)
        rstd_b = sb("rstd_b", [128, TC], F32, 44032)
        mrstd_b = sb("mrstd_b", [128, TC], F32, 52224)
        lnt = sb("lnt", [128, 512], F32, 60416)
        x1s = [sb(f"x1s{i}", [128, 512], BF16, 62464 + i * 1024) for i in range(2)]

        lg_sb = sb("lg_sb", [8, TC], F32, 68608)
        ltok = sb("ltok", [128, NKC * 8], F32, 76800)
        m1c = sb("m1c", [128, 1], F32, 77312)
        nm1c = sb("nm1c", [128, 1], F32, 77344)
        m2c = sb("m2c", [128, 1], F32, 77376)
        e2c = sb("e2c", [128, 1], F32, 77408)
        dinvc = sb("dinvc", [128, 1], F32, 77440)
        msk8 = sb("msk8", [128, 8], F32, 77472)
        e1t = sb("e1t", [128, 8], F32, 77504)
        cmb_r = sb("cmb_r", [1, TSW], F32, 201728)

        ffn_acc = sb("ffn_acc", [128, NKC * TSW], F32, 3072)
        x1_sw = sb("x1_sw", [128, NKC * TSW], BF16, 68608)
        wstage = [sb(f"wstage{i}", [128, 1024], F32, 101376 + i * 4096) for i in range(3)]
        w_hi = [sb(f"w_hi{i}", [128, 2048], BF16, 117760 + i * 4096) for i in range(2)]
        w_lo = [sb(f"w_lo{i}", [128, 2048], BF16, 125952 + i * 4096) for i in range(2)]
        silu_g = sb("silu_g", [128, TSW], F32, 134144)
        act_f = sb("act_f", [128, TSW], F32, 138240)
        act_hi = sb("act_hi", [128, 8 * TSW], BF16, 142336)
        act_lo = sb("act_lo", [128, 8 * TSW], BF16, 158720)
        cmb_b = sb("cmb_b", [128, TSW], F32, 175104)
        ost = [sb(f"ost{i}", [128, 512], F32, 179200 + i * 2048) for i in range(2)]
        sq2 = [sb("sq2_0", [128, 512], F32, 183296)] * 2
        s2row = sb("s2row", [1, 512], F32, 185344)
        sq2row = sb("sq2row", [1, 512], F32, 187392)
        mean2_r = sb("mean2_r", [1, 512], F32, 189440)
        rstd2_r = sb("rstd2_r", [1, 512], F32, 191488)
        mrstd2_r = sb("mrstd2_r", [1, 512], F32, 193536)
        rstd2_b = sb("rstd2_b", [128, 512], F32, 195584)
        mrstd2_b = sb("mrstd2_b", [128, 512], F32, 197632)
        t2f = sb("t2f", [128, 512], F32, 199680)

        pa = es.enter_context(nc.psum_tensor("parena", [128, 4096], F32))

        def pbank(k, n=1):
            return pa[:, k * 512:(k + n) * 512]

        sems = {n: Sem(es.enter_context(nc.semaphore(n)))
                for n in ("pe", "act", "dve", "pool", "dsy", "dsc", "dgp", "cc")}
        pe, act, dve, pool = sems["pe"], sems["act"], sems["dve"], sems["pool"]
        dsy, dsc, dgp, cc = sems["dsy"], sems["dsc"], sems["dgp"], sems["cc"]

        bd = Builder({"vector": dve, "scalar": act, "gpsimd": pool})
        op, W, wR, wF, sR, sF = bd.op, bd.wait, bd.wR, bd.wF, bd.sR, bd.sF

        op("gpsimd", "memset", ident_bf[:, :], 0.0)
        op("gpsimd", "affine_select", out=ident_bf[:, :], in_=ident_bf[:, :],
           compare_op=ALU.not_equal, fill=1.0, base=0, pattern=[[-1, 128]], channel_multiplier=1)
        op("gpsimd", "memset", ident_f[:, :], 0.0)
        op("gpsimd", "affine_select", out=ident_f[:, :], in_=ident_f[:, :],
           compare_op=ALU.not_equal, fill=1.0, base=0, pattern=[[-1, 128]], channel_multiplier=1)
        op("gpsimd", "memset", cmask[:, :], 0.0)
        op("gpsimd", "affine_select", out=cmask[:, :], in_=cmask[:, :],
           compare_op=ALU.is_ge, fill=-1e30, base=0, pattern=[[-1, 128]], channel_multiplier=1)
        op("gpsimd", "memset", ones_c_bf[:, :], 1.0)
        op("gpsimd", "memset", ones_c_f[:, :], 1.0)
        op("gpsimd", "memset", ones_r_f[:, :], 1.0)
        op("gpsimd", "memset", eps12[:, :], 1e-12)
        v = op("gpsimd", "memset", eps5[:, :], 1e-5, inc=pool)
        sR("consts", pool, v)

        bd.dma("gpsimd", dgp, out=bq_bf[:, :], in_=bqkv[:, :])
        bd.dma("gpsimd", dgp, out=bo_bf[:, :], in_=bout[:, :])
        bd.dma("gpsimd", dgp, out=wg_f[:, :], in_=wg[:, :])
        v = bd.dma("gpsimd", dgp, out=coff_sb[:, :], in_=coff[:, :])
        sR("smallin", dgp, v)
        W("vector", dgp, v)
        op("vector", "tensor_copy", out=bq_f[:, :], in_=bq_bf[:, :])
        op("vector", "tensor_copy", out=bo_f[:, :], in_=bo_bf[:, :])
        v = op("vector", "tensor_copy", out=wg_hi[:, :], in_=wg_f[:, :], inc=dve)
        sR("wg_hi", dve, v)
        W("gpsimd", dve, v)
        v = op("gpsimd", "tensor_tensor", out=wg_lo[:, :], in0=wg_f[:, :], in1=wg_hi[:, :],
               op=ALU.subtract, inc=pool)
        sR("wg_lo", pool, v)

        v = bd.dma("sync", dsy, out=wqkv_sb[:, :], in_=wqkv[:, :])
        sR("wqkv", dsy, v)

        SC = [("scA0", pbank(0, 2)), ("scA1", pbank(2, 2))]
        PTP = [("ptA0", pbank(4).bitcast(BF16)), ("ptA1", pbank(5).bitcast(BF16))]
        AVP = ("avA", pbank(6)[:, 0:128])
        BIP = ("biA", pbank(6)[:, 128:192])
        AVTP = ("avTA", pbank(7).bitcast(BF16)[:, 0:128])
        Q2P = ("q2A", pbank(7)[:, 0:512])
        KMP = ("kmA", pbank(7)[:, 0:128])
        QKVP = [[(f"qkv{w}{p}", pbank(w * 2 + p)) for p in range(2)] for w in range(3)]

        for h in range(HPC):
            wF("scalar", "xt0")
            v = bd.dma("scalar", dsc, out=xt[0][:, :], in_=xa[0, :, :])
            sR("xt0", dsc, v)
            for tt in range(NTA):
                par = tt % 2
                xtile = xt[par]
                if tt + 1 < NTA:
                    npar = (tt + 1) % 2
                    wF("scalar", f"xt{npar}")
                    v = bd.dma("scalar", dsc, out=xt[npar][:, :], in_=xa[tt + 1, :, :])
                    sR(f"xt{npar}", dsc, v)
                W("tensor", dsc, bd.R[f"xt{par}"][1])
                if h == 0 and tt == 0:
                    W("tensor", dsy, bd.R["wqkv"][1])
                for w in range(3):
                    bname, psum = QKVP[w][par]
                    wF("tensor", bname)
                    for kc in range(NKC):
                        col = kc * 1536 + h * 384 + w * 128
                        v = op("tensor", "matmul", psum[:, :],
                               wqkv_sb[:, col:col + 128], xtile[:, kc * TTA:(kc + 1) * TTA],
                               start=(kc == 0), stop=(kc == NKC - 1),
                               inc=(pe if kc == NKC - 1 else None))
                    sR(bname, pe, v)
                    if w == 2:
                        sF(f"xt{par}", pe, v)
                for w, dst in enumerate((QT, KT, VT)):
                    bname, psum = QKVP[w][par]
                    W("scalar", pe, bd.R[bname][1])
                    if h == 0 and tt == 0 and w == 0:
                        W("scalar", dve, bd.R["wg_hi"][1])
                    v = op("scalar", "activation", dst[:, tt * TTA:(tt + 1) * TTA], psum[:, :],
                           AF.Identity, bias=bq_f[:, h * 3 + w:h * 3 + w + 1], scale=1.0,
                           inc=act)
                    sF(bname, act, v)
                    if tt == NTA - 1:
                        sR("QT KT VT".split()[w], act, v)

            for r in range(8):
                par = r % 2
                pname, ptp = PTP[par]
                W("tensor", act, bd.R["VT"][1])
                wF("tensor", pname)
                for j in range(8):
                    kc = r * 8 + j
                    v = op("tensor", "transpose", ptp[:, j * 128:(j + 1) * 128],
                           VT[:, kc * 128:(kc + 1) * 128], ident_bf[:, :],
                           inc=(pe if j == 7 else None))
                sR(pname, pe, v)
                W("vector", pe, v)
                v = op("vector", "tensor_copy", out=Vtok[:, r * 1024:(r + 1) * 1024],
                       in_=ptp[:, :], inc=dve)
                sF(pname, dve, v)
                if r == 7:
                    sR("Vtok", dve, v)

            for c in range(NKC):
                par = c % 2
                W("scalar", act, bd.R["QT"][1])
                wF("scalar", f"sq{par}")
                v = op("scalar", "activation", sq_tmp[par][:, :], QT[:, c * 512:(c + 1) * 512],
                       AF.Square, bias=0.0, scale=1.0, inc=act)
                sR(f"sq{par}", act, v)
                W("tensor", act, v)
                wF("tensor", "q2A")
                if c == 0:
                    wF("tensor", "avTA")
                if h == 0 and c == 0:
                    W("tensor", pool, bd.R["consts"][1])
                v = op("tensor", "matmul", Q2P[1][0:1, :], ones_c_bf[:, :], sq_tmp[par][:, :],
                       start=True, stop=True, inc=pe)
                sR("q2A", pe, v)
                sF(f"sq{par}", pe, v)
                W("vector", pe, v)
                wF("vector", "row_buf")
                v = op("vector", "tensor_copy", out=row_buf[:, :], in_=Q2P[1][0:1, :], inc=dve)
                sR("row_buf", dve, v)
                sF("q2A", dve, v)
                W("tensor", dve, v)
                for j in range(4):
                    qt = c * 4 + j
                    v = op("tensor", "transpose", BIP[1][:, qt:qt + 1],
                           row_buf[0:1, j * 128:(j + 1) * 128], ident_f[0:1, 0:1],
                           inc=(pe if j == 3 else None))
                sF("row_buf", pe, v)
            sR("biA", pe, v)
            W("vector", act, bd.R["KT"][1])
            for c in range(NKC):
                if c == 0:
                    op("vector", "reduce_max", out=kmax_col[:, :], in_=KT[:, c * 512:(c + 1) * 512],
                       axis=AX.X, apply_absolute_value=True)
                else:
                    op("vector", "reduce_max", out=km_tmp[:, :], in_=KT[:, c * 512:(c + 1) * 512],
                       axis=AX.X, apply_absolute_value=True)
                    lastv = op("vector", "tensor_tensor", out=kmax_col[:, :], in0=kmax_col[:, :],
                               in1=km_tmp[:, :], op=ALU.max,
                               inc=(dve if c == NKC - 1 else None))
            sR("kmax_col", dve, lastv)
            W("tensor", dve, lastv)
            wF("tensor", "kmA")
            v = op("tensor", "transpose", KMP[1][0:1, 0:128], kmax_col[:, :], ident_f[:, :],
                   inc=pe)
            sR("kmA", pe, v)
            W("vector", pe, v)
            op("vector", "reduce_max", out=km_s[:, :], in_=KMP[1][0:1, 0:128], axis=AX.X)
            v = op("vector", "tensor_tensor", out=km_s[:, :], in0=km_s[:, :], in1=km_s[:, :],
                   op=ALU.mult, inc=dve)
            sR("km2", dve, v)
            sF("kmA", dve, v)
            W("tensor", dve, v)
            wF("tensor", "kmA")
            v = op("tensor", "matmul", KMP[1][:, 0:1], ones_r_f[:, :], km_s[:, :],
                   start=True, stop=True, inc=pe)
            sR("km2b", pe, v)
            W("vector", pe, v)
            v = op("vector", "tensor_copy", out=kmax_col[:, :], in_=KMP[1][:, 0:1], inc=dve)
            sR("k2col", dve, v)
            sF("kmA", dve, v)
            W("vector", pe, bd.R["biA"][1])
            op("vector", "tensor_copy", out=braw[:, :], in_=BIP[1][:, :])
            v = op("vector", "tensor_scalar", out=braw[:, :], in0=braw[:, :],
                   scalar1=kmax_col[:, :], scalar2=1.0 / 128.0, op0=ALU.mult, op1=ALU.mult,
                   inc=dve)
            sF("biA", dve, v)
            W("scalar", dve, v)
            op("scalar", "activation", braw[:, :], braw[:, :], AF.Sqrt, bias=eps12[:, :], scale=1.0)
            v = op("scalar", "activation", bias_cols[:, :], braw[:, :], AF.Copy, bias=0.0,
                   scale=-1.0, inc=act)
            sR("bias_cols", act, v)

            W("tensor", act, bd.R["bias_cols"][1])
            W("tensor", dve, bd.R["Vtok"][1])
            units = []
            for qt in range(NQT):
                npr = ((qt + 1) * 128 + 1023) // 1024
                for kp in range(npr):
                    units.append((qt, kp, npr))
            defer = []

            def emit_trav(qt, kp, npr, par):
                cols = min(1024, (qt + 1) * 128 - kp * 1024)
                kbase = kp * 1024
                is_diag = (kp == npr - 1)
                nj = cols // 128
                pname, ptp = PTP[par]
                W("tensor", act, bd.R[f"p{par}"][1])
                wF("tensor", pname)
                for j in range(nj):
                    v = op("tensor", "transpose", ptp[:, j * 128:(j + 1) * 128],
                           p_sb[par][:, j * 128:(j + 1) * 128], ident_bf[:, :],
                           inc=(pe if j == nj - 1 else None))
                sR(pname, pe, v)
                sF(f"p{par}", pe, v)
                W("vector", pe, v)
                v = op("vector", "tensor_copy", out=pt_sb[par][:, 0:cols],
                       in_=ptp[:, 0:cols], inc=dve)
                sR(f"pt{par}", dve, v)
                sF(pname, dve, v)
                W("tensor", dve, v)
                if kp == 0:
                    wF("tensor", "avA")
                    wF("tensor", "avTA")
                for j in range(nj):
                    v = op("tensor", "matmul", AVP[1][:, :],
                           pt_sb[par][:, j * 128:(j + 1) * 128],
                           Vtok[:, (kbase + j * 128):(kbase + (j + 1) * 128)],
                           start=(kp == 0 and j == 0), stop=(is_diag and j == nj - 1),
                           inc=(pe if (is_diag and j == nj - 1) else None))
                if is_diag:
                    sR("avA", pe, v)
                    W("vector", act, bd.R[f"p{par}"][1])
                    wF("vector", "linv")
                    op("vector", "reduce_sum", out=lsum[:, :],
                       in_=lparts[:, (qt % 2) * 16:(qt % 2) * 16 + npr], axis=AX.X)
                    v = op("vector", "reciprocal", out=linv[:, :], in_=lsum[:, :], inc=dve)
                    sR("linv", dve, v)
                    sF(f"lparts{qt % 2}", dve, v)
                    W("scalar", dve, v)
                    W("scalar", pe, bd.R["avA"][1])
                    wF("scalar", "av_bf")
                    v = op("scalar", "activation", av_bf[:, :], AVP[1][:, :], AF.Identity,
                           bias=0.0, scale=linv[:, :], inc=act)
                    sR("av_bf", act, v)
                    sF("avA", act, v)
                    sF("linv", act, v)

                    def fin(qt=qt):
                        W("tensor", act, bd.R["av_bf"][1])
                        wF("tensor", "avTA")
                        v = op("tensor", "transpose", AVTP[1][:, :], av_bf[:, :],
                               ident_bf[:, :], inc=pe)
                        sR("avTA", pe, v)
                        sF("av_bf", pe, v)
                        W("vector", pe, v)
                        wF("vector", "avT_sb")
                        v2 = op("vector", "tensor_copy",
                                out=avT_sb[:, qt * 128:(qt + 1) * 128],
                                in_=AVTP[1][:, :], inc=dve)
                        sF("avTA", dve, v2)
                        if qt == NQT - 1:
                            sR("avT_done", dve, v2)
                    defer.append(fin)

            prev = None
            for ui, (qt, kp, npr) in enumerate(units):
                par = ui % 2
                scname, scp = SC[par]
                cols = min(1024, (qt + 1) * 128 - kp * 1024)
                kbase = kp * 1024
                is_diag = (kp == npr - 1)
                wF("tensor", scname)
                nm = (cols + 511) // 512
                for j in range(nm):
                    n = min(512, cols - j * 512)
                    v = op("tensor", "matmul", scp[:, j * 512:j * 512 + n],
                           QT[:, qt * 128:(qt + 1) * 128],
                           KT[:, kbase + j * 512:kbase + j * 512 + n],
                           start=True, stop=True, inc=(pe if j == nm - 1 else None))
                sR(scname, pe, v)
                if is_diag:
                    W("vector", pe, v)
                    v = op("vector", "tensor_tensor", out=scp[:, cols - 128:cols],
                           in0=scp[:, cols - 128:cols], in1=cmask[:, :], op=ALU.add, inc=dve)
                    sR(scname + "m", dve, v)
                W("scalar", pe, bd.R[scname][1])
                if is_diag:
                    W("scalar", dve, bd.R[scname + "m"][1])
                wF("scalar", f"p{par}")
                if kp == 0:
                    wF("scalar", f"lparts{qt % 2}")
                v = op("scalar", "activation", p_sb[par][:, 0:cols], scp[:, 0:cols],
                       AF.Exp, bias=bias_cols[:, qt:qt + 1], scale=INV_SQRT_HD,
                       accum_out=lparts[:, (qt % 2) * 16 + kp:(qt % 2) * 16 + kp + 1],
                       inc=act)
                sR(f"p{par}", act, v)
                sF(scname, act, v)
                fins, defer[:] = defer[:], []
                for f in fins:
                    f()
                if prev is not None:
                    emit_trav(*prev)
                prev = (qt, kp, npr, par)
            fins, defer[:] = defer[:], []
            for f in fins:
                f()
            emit_trav(*prev)
            for f in defer:
                f()
            W("gpsimd", dve, bd.R["avT_done"][1])
            v = bd.dma("gpsimd", dgp, out=ag_in[h * 128:(h + 1) * 128, :], in_=avT_sb[:, :])
            sF("avT_sb", dgp, v)
            if h == HPC - 1:
                sR("ag_in", dgp, v)

        PE_END_A = pe.n

        W("gpsimd", dgp, bd.R["ag_in"][1])
        v = op("gpsimd", "collective_compute", "AllGather", ALU.bypass,
               replica_groups=[[0, 1, 2, 3], [4, 5, 6, 7]],
               ins=[ag_in.ap().opt()], outs=[ag_out.ap().opt()], inc=cc)
        sR("ag", cc, v)

        if "dbg_att" in dbg:
            W("gpsimd", cc, bd.R["ag"][1])
            bd.dma("gpsimd", dgp, out=dbg["dbg_att"][:, :], in_=ag_out[:, :])

        POP = [("po0", pbank(0)), ("po1", pbank(1))]

        W("sync", pe, PE_END_A)
        v = bd.dma("sync", dsy, out=wout_sb[:, :], in_=wout[:, :])
        sR("wout", dsy, v)

        for tt in range(4):
            def agdma(e, tt=tt, nprev=None):
                e.wait_ge(dsy.h, nprev)
                with e.register(f"g2_{tt}") as g2:
                    e.reg_load(g2, coff_sb[:1, :1])
                    e.reg_add(g2, g2, tt * 512)
                    e.dma_start(
                        out=attT[:, :].rearrange("p (k t) -> p k t", t=512),
                        in_=bass.AP(ag_out, g2, [[S, 128], [128 * S, NKC], [1, 512]]),
                    ).then_inc(dsy.h, 16)
            W("sync", cc, bd.R["ag"][1])
            W("sync", dgp, bd.R["smallin"][1])
            wF("sync", "attT")
            _np = dsy.n
            bd.raw("sync", lambda e, f=agdma, n=_np: f(e, nprev=n))
            vA = dsy.inc(16)
            sR("attT", dsy, vA)
            wF("sync", "xr_t")
            v = bd.dma("sync", dsy, out=xr_t[:, :],
                   in_=bass.AP(xr, tt * 512, [[NKC * TC, 128], [TC, NKC], [1, 512]]))
            sR("xr_t", dsy, v)
            W("tensor", dsy, bd.R["attT"][1])
            if tt == 0:
                W("tensor", dsy, bd.R["wout"][1])
            for m in range(NKC):
                par = m % 2
                bname, psum = POP[par]
                wF("tensor", bname)
                for kc in range(NKC):
                    v = op("tensor", "matmul", psum[:, :],
                           wout_sb[:, kc * D + m * 128:kc * D + m * 128 + 128],
                           attT[:, kc * 512:(kc + 1) * 512],
                           start=(kc == 0), stop=(kc == NKC - 1),
                           inc=(pe if kc == NKC - 1 else None))
                sR(bname, pe, v)
                if m == NKC - 1:
                    sF("attT", pe, v)
                W("vector", pe, v)
                W("vector", dsy, bd.R["xr_t"][1])
                v = op("vector", "scalar_tensor_tensor",
                       out=r1[:, m * TC + tt * 512:m * TC + tt * 512 + 512],
                       in0=psum[:, :], scalar=bo_f[:, m:m + 1],
                       in1=xr_t[:, m * 512:(m + 1) * 512],
                       op0=ALU.add, op1=ALU.add, inc=dve)
                sF(bname, dve, v)
                if m == NKC - 1:
                    sF("xr_t", dve, v)
                    if tt == 3:
                        sR("r1", dve, v)

        SSUM = ("ssum", pbank(4)[0:1, :])
        SSQ = ("ssq", pbank(5)[0:1, :])
        BCP = ("bcp", pbank(6))
        W("scalar", dve, bd.R["r1"][1])
        for tt in range(4):
            for kc in range(NKC):
                par = kc % 2
                W("tensor", dve, bd.R["r1"][1])
                if kc == 0:
                    wF("tensor", "ssum")
                    wF("tensor", "ssq")
                v = op("tensor", "matmul", SSUM[1][:, :], ones_c_bf[:, :],
                       r1[:, kc * TC + tt * 512:kc * TC + tt * 512 + 512],
                       start=(kc == 0), stop=(kc == NKC - 1),
                       inc=(pe if kc == NKC - 1 else None))
                wF("scalar", f"sqB{par}")
                v2 = op("scalar", "activation", sq_tmp[par][:, :],
                        r1[:, kc * TC + tt * 512:kc * TC + tt * 512 + 512],
                        AF.Square, bias=0.0, scale=1.0, inc=act)
                sR(f"sqB{par}", act, v2)
                W("tensor", act, v2)
                v = op("tensor", "matmul", SSQ[1][:, :], ones_c_bf[:, :], sq_tmp[par][:, :],
                       start=(kc == 0), stop=(kc == NKC - 1), inc=pe)
                sF(f"sqB{par}", pe, v)
            sR("stats", pe, v)
            W("vector", pe, bd.R["stats"][1])
            op("vector", "tensor_copy", out=srow[0:1, tt * 512:(tt + 1) * 512], in_=SSUM[1][:, :])
            v = op("vector", "tensor_copy", out=sqrow[0:1, tt * 512:(tt + 1) * 512],
                   in_=SSQ[1][:, :], inc=dve)
            sR("statrows", dve, v)
            sF("ssum", dve, v)
            sF("ssq", dve, v)
        op("vector", "tensor_scalar", out=mean_r[:, :], in0=srow[:, :],
           scalar1=1.0 / D, scalar2=None, op0=ALU.mult)
        op("vector", "tensor_tensor", out=mrstd_r[:, :], in0=mean_r[:, :], in1=mean_r[:, :],
           op=ALU.mult)
        v = op("vector", "scalar_tensor_tensor", out=rstd_r[:, :], in0=sqrow[:, :],
               scalar=1.0 / D, in1=mrstd_r[:, :], op0=ALU.mult, op1=ALU.subtract, inc=dve)
        W("scalar", dve, v)
        op("scalar", "activation", rstd_r[:, :], rstd_r[:, :], AF.Ln, bias=eps5[0:1, :], scale=1.0)
        v = op("scalar", "activation", rstd_r[:, :], rstd_r[:, :], AF.Exp, bias=0.0,
               scale=-0.5, inc=act)
        sR("rstd_r", act, v)
        W("vector", act, v)
        v = op("vector", "tensor_tensor", out=mrstd_r[:, :], in0=mean_r[:, :], in1=rstd_r[:, :],
               op=ALU.mult, inc=dve)
        sR("mrstd_r", dve, v)
        for tt in range(4):
            W("tensor", dve, bd.R["mrstd_r"][1])
            wF("tensor", "bcp")
            v = op("tensor", "matmul", BCP[1][:, :], ones_r_f[:, :],
                   rstd_r[0:1, tt * 512:(tt + 1) * 512], start=True, stop=True, inc=pe)
            W("vector", pe, v)
            v = op("vector", "tensor_copy", out=rstd_b[:, tt * 512:(tt + 1) * 512],
                   in_=BCP[1][:, :], inc=dve)
            sF("bcp", dve, v)
            wF("tensor", "bcp")
            v = op("tensor", "matmul", BCP[1][:, :], ones_r_f[:, :],
                   mrstd_r[0:1, tt * 512:(tt + 1) * 512], start=True, stop=True, inc=pe)
            W("vector", pe, v)
            v = op("vector", "tensor_copy", out=mrstd_b[:, tt * 512:(tt + 1) * 512],
                   in_=BCP[1][:, :], inc=dve)
            sF("bcp", dve, v)
            if tt == 3:
                sR("bcast1", dve, v)

        LGP = [(f"lg{tt}", pbank(tt)[0:8, :]) for tt in range(4)]
        for tt in range(4):
            for kc in range(NKC):
                par = kc % 2
                W("vector", dve, bd.R["bcast1"][1])
                op("vector", "tensor_tensor", out=lnt[:, :],
                   in0=r1[:, kc * TC + tt * 512:kc * TC + tt * 512 + 512],
                   in1=rstd_b[:, tt * 512:(tt + 1) * 512], op=ALU.mult)
                wF("vector", f"x1s{par}")
                wF("vector", f"x1sD{par}")
                v = op("vector", "tensor_tensor", out=x1s[par][:, :], in0=lnt[:, :],
                       in1=mrstd_b[:, tt * 512:(tt + 1) * 512], op=ALU.subtract, inc=dve)
                sR(f"x1s{par}", dve, v)
                W("scalar", dve, v)
                v = bd.dma("scalar", dsc,
                       out=bass.AP(x1d, kc * 128 * TC + tt * 512, [[TC, 128], [1, 512]]),
                       in_=x1s[par][:, :])
                sF(f"x1sD{par}", dsc, v)
                vx1 = v
                W("tensor", dve, bd.R[f"x1s{par}"][1])
                if tt == 0 and kc == 0:
                    W("tensor", pool, bd.R["wg_lo"][1])
                    wF("tensor", f"lgb{tt}")
                op("tensor", "matmul", LGP[tt][1][:, :], wg_hi[:, kc * 8:(kc + 1) * 8],
                   x1s[par][:, :], start=(kc == 0), stop=False)
                v = op("tensor", "matmul", LGP[tt][1][:, :], wg_lo[:, kc * 8:(kc + 1) * 8],
                       x1s[par][:, :], start=False, stop=(kc == NKC - 1), inc=pe)
                sF(f"x1s{par}", pe, v)
                if kc == NKC - 1:
                    sR(LGP[tt][0], pe, v)
        sR("x1d_all", dsc, vx1)

        TRP = ("trp", pbank(4))
        CMP_ = ("cmp2", pa[0:8, 0:2048])
        for tt in range(4):
            W("vector", pe, bd.R[LGP[tt][0]][1])
            v = op("vector", "tensor_copy", out=lg_sb[:, tt * 512:(tt + 1) * 512],
                   in_=LGP[tt][1][:, :], inc=dve)
            if tt == 3:
                sR("lg", dve, v)
        W("tensor", dve, bd.R["lg"][1])
        wF("tensor", "trp")
        for ti in range(NKC):
            v = op("tensor", "transpose", TRP[1][:, ti * 8:(ti + 1) * 8],
                   lg_sb[0:8, ti * 128:(ti + 1) * 128], ident_f[0:8, 0:8],
                   inc=(pe if ti == NKC - 1 else None))
        sR("trp", pe, v)
        W("vector", pe, v)
        v = op("vector", "tensor_copy", out=ltok[:, :], in_=TRP[1][:, 0:NKC * 8], inc=dve)
        sF("trp", dve, v)
        for ti in range(NKC):
            lt = ltok[:, ti * 8:(ti + 1) * 8]
            op("vector", "reduce_max", out=m1c[:, :], in_=lt, axis=AX.X)
            op("vector", "reduce_max", out=nm1c[:, :], in_=lt, axis=AX.X, negate=True)
            op("vector", "tensor_scalar", out=msk8[:, :], in0=lt, scalar1=m1c[:, :],
               scalar2=None, op0=ALU.is_ge)
            op("vector", "scalar_tensor_tensor", out=msk8[:, :], in0=msk8[:, :],
               scalar=-1e30, in1=lt, op0=ALU.mult, op1=ALU.add)
            v = op("vector", "reduce_max", out=m2c[:, :], in_=msk8[:, :], axis=AX.X, inc=dve)
            W("scalar", dve, v)
            op("scalar", "activation", e1t[:, :], lt, AF.Exp, bias=nm1c[:, :], scale=1.0)
            v = op("scalar", "activation", e2c[:, :], m2c[:, :], AF.Exp, bias=nm1c[:, :],
                   scale=1.0, inc=act)
            W("vector", act, v)
            op("vector", "tensor_scalar", out=e2c[:, :], in0=e2c[:, :], scalar1=1.0,
               scalar2=None, op0=ALU.add)
            op("vector", "reciprocal", out=dinvc[:, :], in_=e2c[:, :])
            op("vector", "tensor_scalar", out=msk8[:, :], in0=lt, scalar1=m2c[:, :],
               scalar2=None, op0=ALU.is_ge)
            v = op("vector", "scalar_tensor_tensor", out=ltok[:, ti * 8:(ti + 1) * 8],
                   in0=e1t[:, :], scalar=dinvc[:, :], in1=msk8[:, :],
                   op0=ALU.mult, op1=ALU.mult, inc=dve)
            sR(f"cmb{ti}", dve, v)
        W("tensor", dve, bd.R[f"cmb{NKC - 1}"][1])
        for ti in range(NKC):
            v = op("tensor", "transpose", CMP_[1][:, ti * 128:(ti + 1) * 128],
                   ltok[:, ti * 8:(ti + 1) * 8], ident_f[:, :],
                   inc=(pe if ti == NKC - 1 else None))
        sR("cmp2", pe, v)
        W("vector", pe, v)
        v = op("vector", "tensor_copy", out=lg_sb[:, :], in_=CMP_[1][:, :], inc=dve)
        sR("cmbT", dve, v)
        sF("cmp2", dve, v)
        W("gpsimd", dve, v)
        v = bd.dma("gpsimd", dgp, out=cmb_d[:, :], in_=lg_sb[:, :])
        sR("cmb_d", dgp, v)

        if "dbg_x1" in dbg:
            W("gpsimd", dsc, bd.R["x1d_all"][1])
            bd.dma("gpsimd", dgp, out=dbg["dbg_x1"][:, :, :], in_=x1d[:, :, :])
        if "dbg_cmb" in dbg:
            W("gpsimd", dgp, bd.R["cmb_d"][1])
            bd.dma("gpsimd", dgp, out=dbg["dbg_cmb"][:, :], in_=cmb_d[:, :])

        GUP = [(f"gu{i}", pbank(i)) for i in range(4)]
        CBP = ("cbp", pbank(6))
        S2SUM = ("s2sum", pbank(4)[0:1, :])
        S2SQ = ("s2sq", pbank(5)[0:1, :])
        gu_rot = [0]
        nstage = [0]
        nslice = [0]
        QWQ = [("sync", dsy), ("gpsimd", dgp)]

        def stream_weight(src_ap):
            par = nslice[0] % 2
            nslice[0] += 1
            for half in range(2):
                s3 = nstage[0] % 3
                qeng, qsem = QWQ[nstage[0] % 2]
                nstage[0] += 1
                wF(qeng, f"wst{s3}")
                v = bd.dma(qeng, qsem, out=wstage[s3][:, :],
                           in_=src_ap[:, half * 1024:(half + 1) * 1024])
                sR(f"wst{s3}", qsem, v)
                bd.wait("vector", qsem, v)
                if half == 0:
                    wF("vector", f"whi{par}")
                    wF("vector", f"wlo{par}")
                vhi = op("vector", "tensor_copy", out=w_hi[par][:, half * 1024:(half + 1) * 1024],
                         in_=wstage[s3][:, :], inc=dve)
                vlo = op("vector", "tensor_tensor", out=w_lo[par][:, half * 1024:(half + 1) * 1024],
                         in0=wstage[s3][:, :], in1=w_hi[par][:, half * 1024:(half + 1) * 1024],
                         op=ALU.subtract, inc=dve)
                sF(f"wst{s3}", dve, vlo)
            return (par, vhi, vlo)

        def gate_up_part(wm, psrc_is_g, jp, tts_act_col, e_cmb):
            par_w, vhi, vlo = wm
            for tts in range(2):
                rot = gu_rot[0] % 4
                gu_rot[0] += 1
                bname, psum = GUP[rot]
                wF("tensor", bname)
                W("tensor", dve, vlo)
                nmm = 0
                for kc in range(NKC):
                    acol = kc * TSW + tts * 512
                    for wt in (w_hi[par_w], w_lo[par_w]):
                        nmm += 1
                        v = op("tensor", "matmul", psum[:, :],
                               wt[:, kc * 128:(kc + 1) * 128], x1_sw[:, acol:acol + 512],
                               start=(nmm == 1), stop=(nmm == NKC * 2),
                               inc=(pe if nmm == NKC * 2 else None))
                sR(bname, pe, v)
                if tts == 1:
                    sF(f"whi{par_w}", pe, v)
                    sF(f"wlo{par_w}", pe, v)
                if psrc_is_g:
                    W("scalar", pe, v)
                    wF("scalar", f"silu{tts}")
                    v = op("scalar", "activation", silu_g[:, tts * 512:(tts + 1) * 512],
                           psum[:, :], AF.Silu, bias=0.0, scale=1.0, inc=act)
                    sR(f"siluR{tts}", act, v)
                    sF(bname, act, v)
                else:
                    W("vector", pe, v)
                    W("vector", act, bd.R[f"siluR{tts}"][1])
                    if e_cmb is not None:
                        W("vector", dve, bd.R[f"cmb_bR{tts}"][1])
                        op("vector", "tensor_tensor", out=act_f[:, tts * 512:(tts + 1) * 512],
                           in0=psum[:, :], in1=cmb_b[:, tts * 512:(tts + 1) * 512], op=ALU.mult)
                        v = op("vector", "tensor_tensor",
                               out=act_f[:, tts * 512:(tts + 1) * 512],
                               in0=act_f[:, tts * 512:(tts + 1) * 512],
                               in1=silu_g[:, tts * 512:(tts + 1) * 512], op=ALU.mult, inc=dve)
                    else:
                        v = op("vector", "tensor_tensor",
                               out=act_f[:, tts * 512:(tts + 1) * 512], in0=psum[:, :],
                               in1=silu_g[:, tts * 512:(tts + 1) * 512], op=ALU.mult, inc=dve)
                    sF(bname, dve, v)
                    sF(f"silu{tts}", dve, v)
                    acol = tts_act_col + tts * 512
                    wF("vector", "act_hi")
                    v = op("vector", "tensor_copy", out=act_hi[:, acol:acol + 512],
                           in_=act_f[:, tts * 512:(tts + 1) * 512], inc=dve)
                    sR(f"ahi{jp}_{tts}", dve, v)
                    W("gpsimd", dve, v)
                    wF("gpsimd", "act_lo")
                    v = op("gpsimd", "tensor_tensor", out=act_lo[:, acol:acol + 512],
                           in0=act_f[:, tts * 512:(tts + 1) * 512],
                           in1=act_hi[:, acol:acol + 512], op=ALU.subtract, inc=pool)
                    sR(f"alo{jp}_{tts}", pool, v)

        def down_slice(wm, ds, first_acc):
            par, vhi, vlo = wm
            for d2 in range(2):
                for tts in range(2):
                    rot = gu_rot[0] % 4
                    gu_rot[0] += 1
                    bname, psum = GUP[rot]
                    wF("tensor", bname)
                    W("tensor", dve, vlo)
                    nmm = 0
                    for ic in range(8):
                        wcol = ic * 256 + d2 * 128
                        acol = ic * TSW + tts * 512
                        for wt, at in ((w_hi[par], act_hi), (w_hi[par], act_lo),
                                       (w_lo[par], act_hi)):
                            nmm += 1
                            v = op("tensor", "matmul", psum[:, :],
                                   wt[:, wcol:wcol + 128], at[:, acol:acol + 512],
                                   start=(nmm == 1), stop=(nmm == 24),
                                   inc=(pe if nmm == 24 else None))
                    sR(bname, pe, v)
                    if d2 == 1 and tts == 1:
                        sF(f"whi{par}", pe, v)
                        sF(f"wlo{par}", pe, v)
                    dch = ds * 2 + d2
                    W("vector", pe, v)
                    accap = ffn_acc[:, dch * TSW + tts * 512:dch * TSW + tts * 512 + 512]
                    if first_acc[dch * 2 + tts]:
                        first_acc[dch * 2 + tts] = False
                        v = op("vector", "tensor_copy", out=accap, in_=psum[:, :], inc=dve)
                    else:
                        v = op("vector", "tensor_tensor", out=accap, in0=accap,
                               in1=psum[:, :], op=ALU.add, inc=dve)
                    sF(bname, dve, v)

        for sw in range(SW):
            wF("sync", "x1_sw")
            W("sync", dsc, bd.R["x1d_all"][1])
            W("sync", dgp, bd.R["cmb_d"][1])
            v = bd.dma("sync", dsy,
                       out=x1_sw[:, :].rearrange("p (k t) -> p k t", t=TSW),
                       in_=bass.AP(x1d, sw * TSW, [[TC, 128], [128 * TC, NKC], [1, TSW]]))
            sR("x1_sw", dsy, v)
            W("tensor", dsy, v)

            first_acc = [True] * (NKC * 2)

            units = []

            def mk_pre_e(e, sw=sw):
                def pre():
                    wF("gpsimd", "cmb_r")
                    vcr = bd.dma("gpsimd", dgp, out=cmb_r[:, :],
                                 in_=cmb_d[e:e + 1, sw * TSW:(sw + 1) * TSW])
                    for tts in range(2):
                        W("tensor", dgp, vcr)
                        wF("tensor", "cbp")
                        v = op("tensor", "matmul", CBP[1][:, :], ones_r_f[:, :],
                               cmb_r[0:1, tts * 512:(tts + 1) * 512],
                               start=True, stop=True, inc=pe)
                        sR("cbp", pe, v)
                        sF("cmb_r", pe, v)
                        W("vector", pe, v)
                        wF("vector", f"cmb_b{tts}")
                        v = op("vector", "tensor_copy",
                               out=cmb_b[:, tts * 512:(tts + 1) * 512],
                               in_=CBP[1][:, :], inc=dve)
                        sR(f"cmb_bR{tts}", dve, v)
                        sF("cbp", dve, v)
                return pre

            def pre_dn():
                for jp in range(8):
                    for tts in range(2):
                        W("tensor", dve, bd.R[f"ahi{jp}_{tts}"][1])
                        W("tensor", pool, bd.R[f"alo{jp}_{tts}"][1])

            def post_e():
                sF("act_hi", pe, pe.n)
                sF("act_lo", pe, pe.n)
                sF("silu0", pe, pe.n)
                sF("silu1", pe, pe.n)
                sF("cmb_b0", pe, pe.n)
                sF("cmb_b1", pe, pe.n)

            for e in range(E):
                for jp in range(8):
                    units.append((wgu[e, jp, :, :],
                                  (lambda wm, jp=jp: gate_up_part(wm, True, jp, None, None)),
                                  mk_pre_e(e) if jp == 0 else None, None))
                    units.append((wgu[e, jp + 8, :, :],
                                  (lambda wm, jp=jp, e=e: gate_up_part(wm, False, jp, jp * TSW, e)),
                                  None, None))
                for ds in range(8):
                    units.append((wdn[e, ds, :, :],
                                  (lambda wm, ds=ds: down_slice(wm, ds, first_acc)),
                                  pre_dn if ds == 0 else None,
                                  post_e if ds == 7 else None))
            for ig in range(4):
                for jp in range(8):
                    units.append((wsgu[ig * 8 + jp, :, :],
                                  (lambda wm, jp=jp: gate_up_part(wm, True, jp, None, None)),
                                  None, None))
                    units.append((wsgu[32 + ig * 8 + jp, :, :],
                                  (lambda wm, jp=jp: gate_up_part(wm, False, jp, jp * TSW, None)),
                                  None, None))
                for ds in range(8):
                    units.append((wsdn[ig, ds, :, :],
                                  (lambda wm, ds=ds: down_slice(wm, ds, first_acc)),
                                  pre_dn if ds == 0 else None,
                                  post_e if ds == 7 else None))

            wms = {}
            streamed = [0]

            def ensure(k):
                while streamed[0] <= k:
                    wms[streamed[0]] = stream_weight(units[streamed[0]][0])
                    streamed[0] += 1

            for i, (srcap, fn, pre, post) in enumerate(units):
                ensure(min(i + 1, len(units) - 1))
                if pre is not None:
                    pre()
                fn(wms.pop(i))
                if post is not None:
                    post()

            for dch in range(NKC):
                ap_ = ffn_acc[:, dch * TSW:(dch + 1) * TSW]
                v = op("vector", "tensor_tensor", out=ap_,
                       in0=x1_sw[:, dch * TSW:(dch + 1) * TSW], in1=ap_, op=ALU.add, inc=dve)
                if dch == NKC - 1:
                    sR("r2", dve, v)
                    sF("x1_sw", dve, v)
            for tts in range(2):
                for dch in range(NKC):
                    W("tensor", dve, bd.R["r2"][1])
                    rc = ffn_acc[:, dch * TSW + tts * 512:dch * TSW + tts * 512 + 512]
                    if dch == 0:
                        wF("tensor", "s2sum")
                        wF("tensor", "s2sq")
                    v = op("tensor", "matmul", S2SUM[1][:, :], ones_c_f[:, :], rc,
                           start=(dch == 0), stop=(dch == NKC - 1),
                           inc=(pe if dch == NKC - 1 else None))
                    W("scalar", dve, bd.R["r2"][1])
                    wF("scalar", "sq2_0")
                    v2 = op("scalar", "activation", sq2[0][:, :], rc, AF.Square,
                            bias=0.0, scale=1.0, inc=act)
                    sR("sq2R", act, v2)
                    W("tensor", act, v2)
                    v = op("tensor", "matmul", S2SQ[1][:, :], ones_c_f[:, :], sq2[0][:, :],
                           start=(dch == 0), stop=(dch == NKC - 1), inc=pe)
                    sF("sq2_0", pe, v)
                sR("stats2", pe, v)
                W("vector", pe, bd.R["stats2"][1])
                op("vector", "tensor_copy", out=s2row[:, :], in_=S2SUM[1][:, :])
                v = op("vector", "tensor_copy", out=sq2row[:, :], in_=S2SQ[1][:, :], inc=dve)
                sF("s2sum", dve, v)
                sF("s2sq", dve, v)
                op("vector", "tensor_scalar", out=mean2_r[:, :], in0=s2row[:, :],
                   scalar1=1.0 / D, scalar2=None, op0=ALU.mult)
                op("vector", "tensor_tensor", out=mrstd2_r[:, :], in0=mean2_r[:, :],
                   in1=mean2_r[:, :], op=ALU.mult)
                v = op("vector", "scalar_tensor_tensor", out=rstd2_r[:, :], in0=sq2row[:, :],
                       scalar=1.0 / D, in1=mrstd2_r[:, :], op0=ALU.mult, op1=ALU.subtract,
                       inc=dve)
                W("scalar", dve, v)
                op("scalar", "activation", rstd2_r[:, :], rstd2_r[:, :], AF.Ln, bias=eps5[0:1, :],
                   scale=1.0)
                v = op("scalar", "activation", rstd2_r[:, :], rstd2_r[:, :], AF.Exp, bias=0.0,
                       scale=-0.5, inc=act)
                W("vector", act, v)
                v = op("vector", "tensor_tensor", out=mrstd2_r[:, :], in0=mean2_r[:, :],
                       in1=rstd2_r[:, :], op=ALU.mult, inc=dve)
                sR("mrstd2", dve, v)
                W("tensor", dve, v)
                wF("tensor", "cbp")
                v = op("tensor", "matmul", CBP[1][:, :], ones_r_f[:, :], rstd2_r[0:1, :],
                       start=True, stop=True, inc=pe)
                W("vector", pe, v)
                v = op("vector", "tensor_copy", out=rstd2_b[:, :], in_=CBP[1][:, :], inc=dve)
                sF("cbp", dve, v)
                wF("tensor", "cbp")
                v = op("tensor", "matmul", CBP[1][:, :], ones_r_f[:, :], mrstd2_r[0:1, :],
                       start=True, stop=True, inc=pe)
                W("vector", pe, v)
                v = op("vector", "tensor_copy", out=mrstd2_b[:, :], in_=CBP[1][:, :], inc=dve)
                sF("cbp", dve, v)
                for dch in range(NKC):
                    par = dch % 2
                    rc = ffn_acc[:, dch * TSW + tts * 512:dch * TSW + tts * 512 + 512]
                    op("vector", "tensor_tensor", out=t2f[:, :], in0=rc, in1=rstd2_b[:, :],
                       op=ALU.mult)
                    wF("vector", f"ost{par}")
                    v = op("vector", "tensor_tensor", out=ost[par][:, :], in0=t2f[:, :],
                           in1=mrstd2_b[:, :], op=ALU.subtract, inc=dve)
                    W("scalar", dve, v)
                    v = bd.dma("scalar", dsc,
                           out=out_d[dch * 128:(dch + 1) * 128,
                                     sw * TSW + tts * 512:sw * TSW + tts * 512 + 512],
                           in_=ost[par][:, :])
                    sF(f"ost{par}", dsc, v)

        W("scalar", dsc, dsc.n)
        W("sync", dsy, dsy.n)
        W("gpsimd", dgp, dgp.n)

        with nc.Block() as block:
            @block.tensor
            def _(eng):
                for f in bd.plan["tensor"]:
                    f(eng)

            @block.vector
            def _(eng):
                for f in bd.plan["vector"]:
                    f(eng)

            @block.scalar
            def _(eng):
                for f in bd.plan["scalar"]:
                    f(eng)

            @block.gpsimd
            def _(eng):
                for f in bd.plan["gpsimd"]:
                    f(eng)

            @block.sync
            def _(eng):
                for f in bd.plan["sync"]:
                    f(eng)

    return nc



def _layout_inputs(inputs):
    x = np.asarray(inputs["L0_attn_x"])
    Wqkv = np.asarray(inputs["L0_attn_W_qkv"])
    bqkv = np.asarray(inputs["L0_attn_b_qkv"])
    Wout = np.asarray(inputs["L0_attn_W_out"])
    bout = np.asarray(inputs["L0_attn_b_out"])
    wg = np.asarray(inputs["L0_ffn_w_gate"])
    wgu = np.asarray(inputs["L0_ffn_w_gate_up"])
    wdn = np.asarray(inputs["L0_ffn_w_down"])
    wsgu = np.asarray(inputs["L0_ffn_w_sh_gate_up"])
    wsdn = np.asarray(inputs["L0_ffn_w_sh_down"])

    wout_h = np.ascontiguousarray(
        Wout.reshape(16, 128, 2048).transpose(1, 0, 2).reshape(128, 16 * 2048))
    bout_h = np.ascontiguousarray(bout.reshape(16, 128).T)
    wg_h = np.ascontiguousarray(
        wg.reshape(16, 128, 8).transpose(1, 0, 2).reshape(128, 128))
    wgu_h = np.ascontiguousarray(
        wgu.reshape(8, 16, 128, 16, 128).transpose(0, 3, 2, 1, 4).reshape(8, 16, 128, 2048))
    wdn_h = np.ascontiguousarray(
        wdn.reshape(8, 8, 128, 8, 256).transpose(0, 3, 2, 1, 4).reshape(8, 8, 128, 2048))
    wsgu_h = np.ascontiguousarray(
        wsgu.reshape(16, 128, 64, 128).transpose(2, 1, 0, 3).reshape(64, 128, 2048))
    wsdn_h = np.ascontiguousarray(
        wsdn.reshape(4, 8, 128, 8, 256).transpose(0, 3, 2, 1, 4).reshape(4, 8, 128, 2048))

    Wq5 = Wqkv.reshape(16, 128, 3, 16, 128)
    bq3 = bqkv.reshape(3, 16, 128)

    in_maps = []
    for c in range(8):
        b, hg = c // 4, c % 4
        s = c % 4
        xa_h = np.ascontiguousarray(
            x[b].reshape(NTA, TTA, 16, 128).transpose(0, 3, 2, 1).reshape(NTA, 128, 16 * TTA))
        xr_h = np.ascontiguousarray(
            x[b, s * TC:(s + 1) * TC].reshape(TC, 16, 128).transpose(2, 1, 0).reshape(128, 16 * TC))
        wqkv_h = np.ascontiguousarray(
            Wq5[:, :, :, hg * 4:(hg + 1) * 4, :].transpose(1, 0, 3, 2, 4).reshape(128, 16 * 1536))
        bq_h = np.ascontiguousarray(
            bq3[:, hg * 4:(hg + 1) * 4, :].transpose(2, 1, 0).reshape(128, 12))
        in_maps.append({
            "xa": xa_h, "xr": xr_h, "wqkv": wqkv_h, "bqkv": bq_h,
            "wout": wout_h, "bout": bout_h, "wg": wg_h,
            "wgu": wgu_h, "wdn": wdn_h, "wsgu": wsgu_h, "wsdn": wsdn_h,
            "coff": np.array([[s * TC]], dtype=np.int32),
        })
    return in_maps


def kernel(**inputs):
    in_maps = _layout_inputs(inputs)
    nc = build_nc(DEBUG_TAPS)
    res = run_bass_kernel_spmd(nc, in_maps, core_ids=list(range(8)))
    out = np.empty((B, S, D), np.float32)
    for c in range(8):
        b, s = c // 4, c % 4
        out[b, s * TC:(s + 1) * TC, :] = res.results[c]["out"].T
    if DEBUG_TAPS:
        kernel.last_results = res
    return out


# version 04
# speedup vs baseline: 1.3696x; 1.0072x over previous
import numpy as np
import ml_dtypes
from contextlib import ExitStack

import concourse.bass as bass
import concourse.mybir as mybir
from concourse.bass_utils import run_bass_kernel_spmd

F32 = mybir.dt.float32
BF16 = mybir.dt.bfloat16
I32 = mybir.dt.int32
U8 = mybir.dt.uint8
AF = mybir.ActivationFunctionType
ALU = mybir.AluOpType
AX = mybir.AxisListType

B, S, D, H, HD = 2, 8192, 2048, 16, 128
HPC = 4
NKC = D // 128
TC = 2048
TTA = 512
NTA = S // TTA
NQT = S // 128
E, IM, ISH = 8, 1024, 4096
SW, TSW = 2, 1024
INV_SQRT_HD = float(1.0 / np.sqrt(HD))

ENGS = ("tensor", "vector", "scalar", "gpsimd", "sync")
DEBUG_TAPS = []


class Sem:
    def __init__(self, h):
        self.h = h
        self.n = 0

    def inc(self, k=1):
        self.n += k
        return self.n


class Builder:
    SELF_EDGE = {"vector", "scalar", "gpsimd"}
    ASYNC_METHS = {"dma_start", "collective_compute", "indirect_dma_start"}

    def __init__(self, selfsems=None):
        self.plan = {e: [] for e in ENGS}
        self.R = {}
        self.F = {}
        self.selfsems = selfsems or {}

    def op(self, eng, meth, *a, inc=None, k=1, **kw):
        if eng in self.SELF_EDGE and meth not in self.ASYNC_METHS:
            sem = self.selfsems[eng]
            if sem.n > 0:
                self.wait(eng, sem, sem.n)
            if inc is None:
                inc, k = sem, 1
            else:
                assert inc is sem, (eng, meth)
        if inc is None:
            self.plan[eng].append(lambda e, m=meth, aa=a, kk=kw: getattr(e, m)(*aa, **kk))
            return None
        v = inc.inc(k)
        self.plan[eng].append(
            lambda e, m=meth, aa=a, kk=kw, s=inc.h, n=k: getattr(e, m)(*aa, **kk).then_inc(s, n))
        return v

    def dma(self, eng, qsem, **kw):
        self.wait(eng, qsem, qsem.n)
        return self.op(eng, "dma_start", inc=qsem, k=16, **kw)

    def wait(self, eng, sem, val):
        if val and val > 0:
            self.plan[eng].append(lambda e, s=sem.h, v=val: e.wait_ge(s, v))

    def raw(self, eng, fn):
        self.plan[eng].append(fn)

    def wR(self, eng, *names):
        for n in names:
            g = self.R.get(n)
            if g is not None:
                self.wait(eng, g[0], g[1])

    def wF(self, eng, *names):
        for n in names:
            g = self.F.get(n)
            if g is not None:
                self.wait(eng, g[0], g[1])

    def sR(self, name, sem, val):
        self.R[name] = (sem, val)

    def sF(self, name, sem, val):
        self.F[name] = (sem, val)


def build_nc(debug_taps=()):
    nc = bass.Bass("TRN2", target_bir_lowering=False)

    xa = nc.dram_tensor("xa", [NTA, 128, NKC * TTA], BF16, kind="ExternalInput")
    xr = nc.dram_tensor("xr", [128, NKC * TC], BF16, kind="ExternalInput")
    wqkv = nc.dram_tensor("wqkv", [128, NKC * HPC * 3 * 128], BF16, kind="ExternalInput")
    bqkv = nc.dram_tensor("bqkv", [128, HPC * 3], BF16, kind="ExternalInput")
    wout = nc.dram_tensor("wout", [128, NKC * D], BF16, kind="ExternalInput")
    bout = nc.dram_tensor("bout", [128, NKC], BF16, kind="ExternalInput")
    wg = nc.dram_tensor("wg", [128, NKC * E], F32, kind="ExternalInput")
    wgu = nc.dram_tensor("wgu", [E, 16, 128, NKC * 128], F32, kind="ExternalInput")
    wdn = nc.dram_tensor("wdn", [E, 8, 128, 8 * 256], F32, kind="ExternalInput")
    wsgu = nc.dram_tensor("wsgu", [64, 128, NKC * 128], F32, kind="ExternalInput")
    wsdn = nc.dram_tensor("wsdn", [4, 8, 128, 8 * 256], F32, kind="ExternalInput")
    coff = nc.dram_tensor("coff", [1, 1], I32, kind="ExternalInput")
    out_d = nc.dram_tensor("out", [D, TC], F32, kind="ExternalOutput")

    ag_in_h = [nc.dram_tensor(f"ag_in{h}", [128, S], BF16) for h in range(HPC)]
    ag_out_h = [nc.dram_tensor(f"ag_out{h}", [4 * 128, S], BF16) for h in range(HPC)]
    x1d = nc.dram_tensor("x1d", [NKC, 128, TC], BF16)
    cmb_d = nc.dram_tensor("cmb_d", [8, TC], F32)

    dbg = {}
    for name, shape, dt in debug_taps:
        dbg[name] = nc.dram_tensor(name, shape, dt, kind="ExternalOutput")

    with ExitStack() as es:
        ARENA = 210944
        arena = es.enter_context(nc.sbuf_tensor("arena", [128, ARENA], U8))
        abase = nc.lookup_mloc(arena).addr

        def sb(name, shape, dtype, off):
            per = int(np.prod(shape[1:])) * mybir.dt.size(dtype)
            assert off + per <= ARENA, (name, off, per)
            return nc.alloc_sbuf_tensor_at(name, shape, dtype, offset=abase + off)

        ident_bf = sb("ident_bf", [128, 128], BF16, 0)
        ident_f = sb("ident_f", [128, 128], F32, 256)
        cmask = sb("cmask", [128, 128], F32, 768)
        ones_c_bf = sb("ones_c_bf", [128, 1], BF16, 1280)
        ones_c_f = sb("ones_c_f", [128, 1], F32, 1312)
        ones_r_f = sb("ones_r_f", [1, 128], F32, 1344)
        bq_bf = sb("bq_bf", [128, HPC * 3], BF16, 1856)
        bq_f = sb("bq_f", [128, HPC * 3], F32, 1888)
        bo_bf = sb("bo_bf", [128, NKC], BF16, 1952)
        bo_f = sb("bo_f", [128, NKC], F32, 1984)
        wg_f = sb("wg_f", [128, NKC * E], F32, 2048)
        wg_hi = sb("wg_hi", [128, NKC * E], BF16, 2560)
        wg_lo = sb("wg_lo", [128, NKC * E], BF16, 2816)
        eps12 = sb("eps12", [128, 1], F32, 205824)
        eps5 = sb("eps5", [128, 1], F32, 205856)
        coff_sb = sb("coff_sb", [1, 1], I32, 205888)

        wqkv_sb = sb("wqkv_sb", [128, NKC * 1536], BF16, 3072)
        xt = [sb(f"xt{i}", [128, NKC * TTA], BF16, 52224 + i * 16384) for i in range(2)]
        QT = sb("QT", [128, S], BF16, 84992)
        KT = sb("KT", [128, S], BF16, 101376)
        VT = sb("VT", [128, S], BF16, 117760)
        Vtok = sb("Vtok", [128, S], BF16, 134144)
        avT_sb = sb("avT_sb", [128, S], BF16, 150528)
        p_sb = [sb(f"p_sb{i}", [128, 1024], BF16, 166912 + i * 2048) for i in range(2)]
        pt_sb = [sb(f"pt_sb{i}", [128, 1024], BF16, 171008 + i * 2048) for i in range(2)]
        sq_tmp = [sb(f"sq_tmp{i}", [128, 512], BF16, 175104 + i * 1024) for i in range(2)]
        row_buf = sb("row_buf", [1, 512], F32, 177152)
        braw = sb("braw", [128, NQT], F32, 179200)
        bias_cols = sb("bias_cols", [128, NQT], F32, 179456)
        km_tmp = sb("km_tmp", [128, 1], F32, 179712)
        kmax_col = sb("kmax_col", [128, 1], F32, 179744)
        km_s = sb("km_s", [1, 1], F32, 179776)
        lparts = sb("lparts", [128, 32], F32, 179808)
        lsum = sb("lsum", [128, 1], F32, 179936)
        linv = sb("linv", [128, 1], F32, 179968)
        av_bf = sb("av_bf", [128, 128], BF16, 180000)

        wout_sb = sb("wout_sb", [128, NKC * D], BF16, 3072)
        attT = sb("attT", [128, NKC * 512], BF16, 68608)
        xr_t = sb("xr_t", [128, NKC * 512], BF16, 84992)
        r1 = sb("r1", [128, NKC * TC], BF16, 101376)

        srow = sb("srow", [1, TC], F32, 3072)
        sqrow = sb("sqrow", [1, TC], F32, 11264)
        mean_r = sb("mean_r", [1, TC], F32, 19456)
        rstd_r = sb("rstd_r", [1, TC], F32, 27648)
        mrstd_r = sb("mrstd_r", [1, TC], F32, 35840)
        rstd_b = sb("rstd_b", [128, TC], F32, 44032)
        mrstd_b = sb("mrstd_b", [128, TC], F32, 52224)
        lnt = sb("lnt", [128, 512], F32, 60416)
        x1s = [sb(f"x1s{i}", [128, 512], BF16, 62464 + i * 1024) for i in range(2)]

        lg_sb = sb("lg_sb", [8, TC], F32, 68608)
        ltok = sb("ltok", [128, NKC * 8], F32, 76800)
        m1c = sb("m1c", [128, 1], F32, 77312)
        nm1c = sb("nm1c", [128, 1], F32, 77344)
        m2c = sb("m2c", [128, 1], F32, 77376)
        e2c = sb("e2c", [128, 1], F32, 77408)
        dinvc = sb("dinvc", [128, 1], F32, 77440)
        msk8 = sb("msk8", [128, 8], F32, 77472)
        e1t = sb("e1t", [128, 8], F32, 77504)
        cmb_r = sb("cmb_r", [1, TSW], F32, 201728)

        ffn_acc = sb("ffn_acc", [128, NKC * TSW], F32, 3072)
        x1_sw = sb("x1_sw", [128, NKC * TSW], BF16, 68608)
        wstage = [sb(f"wstage{i}", [128, 1024], F32, 101376 + i * 4096) for i in range(3)]
        w_hi = [sb(f"w_hi{i}", [128, 2048], BF16, 117760 + i * 4096) for i in range(2)]
        w_lo = [sb(f"w_lo{i}", [128, 2048], BF16, 125952 + i * 4096) for i in range(2)]
        silu_g = sb("silu_g", [128, TSW], F32, 134144)
        act_f = sb("act_f", [128, TSW], F32, 138240)
        act_hi = sb("act_hi", [128, 8 * TSW], BF16, 142336)
        act_lo = sb("act_lo", [128, 8 * TSW], BF16, 158720)
        cmb_b = sb("cmb_b", [128, TSW], F32, 175104)
        ost = [sb(f"ost{i}", [128, 512], F32, 179200 + i * 2048) for i in range(2)]
        sq2 = [sb("sq2_0", [128, 512], F32, 183296),
               sb("sq2_1", [128, 512], F32, 205920)]
        s2row = sb("s2row", [1, 512], F32, 185344)
        sq2row = sb("sq2row", [1, 512], F32, 187392)
        mean2_r = sb("mean2_r", [1, 512], F32, 189440)
        rstd2_r = sb("rstd2_r", [1, 512], F32, 191488)
        mrstd2_r = sb("mrstd2_r", [1, 512], F32, 193536)
        rstd2_b = sb("rstd2_b", [128, 512], F32, 195584)
        mrstd2_b = sb("mrstd2_b", [128, 512], F32, 197632)
        t2f = sb("t2f", [128, 512], F32, 199680)

        pa = es.enter_context(nc.psum_tensor("parena", [128, 4096], F32))

        def pbank(k, n=1):
            return pa[:, k * 512:(k + n) * 512]

        sems = {n: Sem(es.enter_context(nc.semaphore(n)))
                for n in ("pe", "act", "dve", "pool", "dsy", "dsc", "dgp", "cc")}
        pe, act, dve, pool = sems["pe"], sems["act"], sems["dve"], sems["pool"]
        dsy, dsc, dgp, cc = sems["dsy"], sems["dsc"], sems["dgp"], sems["cc"]

        bd = Builder({"vector": dve, "scalar": act, "gpsimd": pool})
        op, W, wR, wF, sR, sF = bd.op, bd.wait, bd.wR, bd.wF, bd.sR, bd.sF

        op("gpsimd", "memset", ident_bf[:, :], 0.0)
        op("gpsimd", "affine_select", out=ident_bf[:, :], in_=ident_bf[:, :],
           compare_op=ALU.not_equal, fill=1.0, base=0, pattern=[[-1, 128]], channel_multiplier=1)
        op("gpsimd", "memset", ident_f[:, :], 0.0)
        op("gpsimd", "affine_select", out=ident_f[:, :], in_=ident_f[:, :],
           compare_op=ALU.not_equal, fill=1.0, base=0, pattern=[[-1, 128]], channel_multiplier=1)
        op("gpsimd", "memset", cmask[:, :], 0.0)
        op("gpsimd", "affine_select", out=cmask[:, :], in_=cmask[:, :],
           compare_op=ALU.is_ge, fill=-1e30, base=0, pattern=[[-1, 128]], channel_multiplier=1)
        op("gpsimd", "memset", ones_c_bf[:, :], 1.0)
        op("gpsimd", "memset", ones_c_f[:, :], 1.0)
        op("gpsimd", "memset", ones_r_f[:, :], 1.0)
        op("gpsimd", "memset", eps12[:, :], 1e-12)
        v = op("gpsimd", "memset", eps5[:, :], 1e-5, inc=pool)
        sR("consts", pool, v)

        bd.dma("gpsimd", dgp, out=bq_bf[:, :], in_=bqkv[:, :])
        bd.dma("gpsimd", dgp, out=bo_bf[:, :], in_=bout[:, :])
        bd.dma("gpsimd", dgp, out=wg_f[:, :], in_=wg[:, :])
        v = bd.dma("gpsimd", dgp, out=coff_sb[:, :], in_=coff[:, :])
        sR("smallin", dgp, v)
        W("vector", dgp, v)
        op("vector", "tensor_copy", out=bq_f[:, :], in_=bq_bf[:, :])
        op("vector", "tensor_copy", out=bo_f[:, :], in_=bo_bf[:, :])
        v = op("vector", "tensor_copy", out=wg_hi[:, :], in_=wg_f[:, :], inc=dve)
        sR("wg_hi", dve, v)
        W("gpsimd", dve, v)
        v = op("gpsimd", "tensor_tensor", out=wg_lo[:, :], in0=wg_f[:, :], in1=wg_hi[:, :],
               op=ALU.subtract, inc=pool)
        sR("wg_lo", pool, v)

        v = bd.dma("sync", dsy, out=wqkv_sb[:, :], in_=wqkv[:, :])
        sR("wqkv", dsy, v)

        SC = [("scA0", pbank(0, 2)), ("scA1", pbank(2, 2))]
        PTP = [("ptA0", pbank(4).bitcast(BF16)), ("ptA1", pbank(5).bitcast(BF16))]
        AVP = ("avA", pbank(6)[:, 0:128])
        BIP = ("biA", pbank(6)[:, 128:192])
        AVTP = ("avTA", pbank(7).bitcast(BF16)[:, 0:128])
        Q2P = ("q2A", pbank(7)[:, 0:512])
        KMP = ("kmA", pbank(7)[:, 0:128])
        QKVP = [[(f"qkv{w}{p}", pbank(w * 2 + p)) for p in range(2)] for w in range(3)]

        for h in range(HPC):
            wF("scalar", "xt0")
            v = bd.dma("scalar", dsc, out=xt[0][:, :], in_=xa[0, :, :])
            sR("xt0", dsc, v)
            for tt in range(NTA):
                par = tt % 2
                xtile = xt[par]
                if tt + 1 < NTA:
                    npar = (tt + 1) % 2
                    wF("scalar", f"xt{npar}")
                    v = bd.dma("scalar", dsc, out=xt[npar][:, :], in_=xa[tt + 1, :, :])
                    sR(f"xt{npar}", dsc, v)
                W("tensor", dsc, bd.R[f"xt{par}"][1])
                if h == 0 and tt == 0:
                    W("tensor", dsy, bd.R["wqkv"][1])
                for w in range(3):
                    bname, psum = QKVP[w][par]
                    wF("tensor", bname)
                    for kc in range(NKC):
                        col = kc * 1536 + h * 384 + w * 128
                        v = op("tensor", "matmul", psum[:, :],
                               wqkv_sb[:, col:col + 128], xtile[:, kc * TTA:(kc + 1) * TTA],
                               start=(kc == 0), stop=(kc == NKC - 1),
                               inc=(pe if kc == NKC - 1 else None))
                    sR(bname, pe, v)
                    if w == 2:
                        sF(f"xt{par}", pe, v)
                for w, dst in enumerate((QT, KT, VT)):
                    bname, psum = QKVP[w][par]
                    W("scalar", pe, bd.R[bname][1])
                    if h == 0 and tt == 0 and w == 0:
                        W("scalar", dve, bd.R["wg_hi"][1])
                    v = op("scalar", "activation", dst[:, tt * TTA:(tt + 1) * TTA], psum[:, :],
                           AF.Identity, bias=bq_f[:, h * 3 + w:h * 3 + w + 1], scale=1.0,
                           inc=act)
                    sF(bname, act, v)
                    if tt == NTA - 1:
                        sR("QT KT VT".split()[w], act, v)

            for r in range(8):
                par = r % 2
                pname, ptp = PTP[par]
                W("tensor", act, bd.R["VT"][1])
                wF("tensor", pname)
                for j in range(8):
                    kc = r * 8 + j
                    v = op("tensor", "transpose", ptp[:, j * 128:(j + 1) * 128],
                           VT[:, kc * 128:(kc + 1) * 128], ident_bf[:, :],
                           inc=(pe if j == 7 else None))
                sR(pname, pe, v)
                W("vector", pe, v)
                v = op("vector", "tensor_copy", out=Vtok[:, r * 1024:(r + 1) * 1024],
                       in_=ptp[:, :], inc=dve)
                sF(pname, dve, v)
                if r == 7:
                    sR("Vtok", dve, v)

            for c in range(NKC):
                par = c % 2
                W("scalar", act, bd.R["QT"][1])
                wF("scalar", f"sq{par}")
                v = op("scalar", "activation", sq_tmp[par][:, :], QT[:, c * 512:(c + 1) * 512],
                       AF.Square, bias=0.0, scale=1.0, inc=act)
                sR(f"sq{par}", act, v)
                W("tensor", act, v)
                wF("tensor", "q2A")
                if c == 0:
                    wF("tensor", "avTA")
                if h == 0 and c == 0:
                    W("tensor", pool, bd.R["consts"][1])
                v = op("tensor", "matmul", Q2P[1][0:1, :], ones_c_bf[:, :], sq_tmp[par][:, :],
                       start=True, stop=True, inc=pe)
                sR("q2A", pe, v)
                sF(f"sq{par}", pe, v)
                W("vector", pe, v)
                wF("vector", "row_buf")
                v = op("vector", "tensor_copy", out=row_buf[:, :], in_=Q2P[1][0:1, :], inc=dve)
                sR("row_buf", dve, v)
                sF("q2A", dve, v)
                W("tensor", dve, v)
                for j in range(4):
                    qt = c * 4 + j
                    v = op("tensor", "transpose", BIP[1][:, qt:qt + 1],
                           row_buf[0:1, j * 128:(j + 1) * 128], ident_f[0:1, 0:1],
                           inc=(pe if j == 3 else None))
                sF("row_buf", pe, v)
            sR("biA", pe, v)
            W("vector", act, bd.R["KT"][1])
            for c in range(NKC):
                if c == 0:
                    op("vector", "reduce_max", out=kmax_col[:, :], in_=KT[:, c * 512:(c + 1) * 512],
                       axis=AX.X, apply_absolute_value=True)
                else:
                    op("vector", "reduce_max", out=km_tmp[:, :], in_=KT[:, c * 512:(c + 1) * 512],
                       axis=AX.X, apply_absolute_value=True)
                    lastv = op("vector", "tensor_tensor", out=kmax_col[:, :], in0=kmax_col[:, :],
                               in1=km_tmp[:, :], op=ALU.max,
                               inc=(dve if c == NKC - 1 else None))
            sR("kmax_col", dve, lastv)
            W("tensor", dve, lastv)
            wF("tensor", "kmA")
            v = op("tensor", "transpose", KMP[1][0:1, 0:128], kmax_col[:, :], ident_f[:, :],
                   inc=pe)
            sR("kmA", pe, v)
            W("vector", pe, v)
            op("vector", "reduce_max", out=km_s[:, :], in_=KMP[1][0:1, 0:128], axis=AX.X)
            v = op("vector", "tensor_tensor", out=km_s[:, :], in0=km_s[:, :], in1=km_s[:, :],
                   op=ALU.mult, inc=dve)
            sR("km2", dve, v)
            sF("kmA", dve, v)
            W("tensor", dve, v)
            wF("tensor", "kmA")
            v = op("tensor", "matmul", KMP[1][:, 0:1], ones_r_f[:, :], km_s[:, :],
                   start=True, stop=True, inc=pe)
            sR("km2b", pe, v)
            W("vector", pe, v)
            v = op("vector", "tensor_copy", out=kmax_col[:, :], in_=KMP[1][:, 0:1], inc=dve)
            sR("k2col", dve, v)
            sF("kmA", dve, v)
            W("vector", pe, bd.R["biA"][1])
            op("vector", "tensor_copy", out=braw[:, :], in_=BIP[1][:, :])
            v = op("vector", "tensor_scalar", out=braw[:, :], in0=braw[:, :],
                   scalar1=kmax_col[:, :], scalar2=1.0 / 128.0, op0=ALU.mult, op1=ALU.mult,
                   inc=dve)
            sF("biA", dve, v)
            W("scalar", dve, v)
            op("scalar", "activation", braw[:, :], braw[:, :], AF.Sqrt, bias=eps12[:, :], scale=1.0)
            v = op("scalar", "activation", bias_cols[:, :], braw[:, :], AF.Copy, bias=0.0,
                   scale=-1.0, inc=act)
            sR("bias_cols", act, v)

            W("tensor", act, bd.R["bias_cols"][1])
            W("tensor", dve, bd.R["Vtok"][1])
            units = []
            for qt in range(NQT):
                npr = ((qt + 1) * 128 + 1023) // 1024
                for kp in range(npr):
                    units.append((qt, kp, npr))
            defer = []

            def emit_trav(qt, kp, npr, par):
                cols = min(1024, (qt + 1) * 128 - kp * 1024)
                kbase = kp * 1024
                is_diag = (kp == npr - 1)
                nj = cols // 128
                pname, ptp = PTP[par]
                W("tensor", act, bd.R[f"p{par}"][1])
                wF("tensor", pname)
                for j in range(nj):
                    v = op("tensor", "transpose", ptp[:, j * 128:(j + 1) * 128],
                           p_sb[par][:, j * 128:(j + 1) * 128], ident_bf[:, :],
                           inc=(pe if j == nj - 1 else None))
                sR(pname, pe, v)
                sF(f"p{par}", pe, v)
                W("vector", pe, v)
                v = op("vector", "tensor_copy", out=pt_sb[par][:, 0:cols],
                       in_=ptp[:, 0:cols], inc=dve)
                sR(f"pt{par}", dve, v)
                sF(pname, dve, v)
                W("tensor", dve, v)
                if kp == 0:
                    wF("tensor", "avA")
                    wF("tensor", "avTA")
                for j in range(nj):
                    v = op("tensor", "matmul", AVP[1][:, :],
                           pt_sb[par][:, j * 128:(j + 1) * 128],
                           Vtok[:, (kbase + j * 128):(kbase + (j + 1) * 128)],
                           start=(kp == 0 and j == 0), stop=(is_diag and j == nj - 1),
                           inc=(pe if (is_diag and j == nj - 1) else None))
                if is_diag:
                    sR("avA", pe, v)
                    W("vector", act, bd.R[f"p{par}"][1])
                    wF("vector", "linv")
                    op("vector", "reduce_sum", out=lsum[:, :],
                       in_=lparts[:, (qt % 2) * 16:(qt % 2) * 16 + npr], axis=AX.X)
                    v = op("vector", "reciprocal", out=linv[:, :], in_=lsum[:, :], inc=dve)
                    sR("linv", dve, v)
                    sF(f"lparts{qt % 2}", dve, v)
                    W("scalar", dve, v)
                    W("scalar", pe, bd.R["avA"][1])
                    wF("scalar", "av_bf")
                    v = op("scalar", "activation", av_bf[:, :], AVP[1][:, :], AF.Identity,
                           bias=0.0, scale=linv[:, :], inc=act)
                    sR("av_bf", act, v)
                    sF("avA", act, v)
                    sF("linv", act, v)

                    def fin(qt=qt):
                        W("tensor", act, bd.R["av_bf"][1])
                        wF("tensor", "avTA")
                        v = op("tensor", "transpose", AVTP[1][:, :], av_bf[:, :],
                               ident_bf[:, :], inc=pe)
                        sR("avTA", pe, v)
                        sF("av_bf", pe, v)
                        W("vector", pe, v)
                        wF("vector", "avT_sb")
                        v2 = op("vector", "tensor_copy",
                                out=avT_sb[:, qt * 128:(qt + 1) * 128],
                                in_=AVTP[1][:, :], inc=dve)
                        sF("avTA", dve, v2)
                        if qt == NQT - 1:
                            sR("avT_done", dve, v2)
                    defer.append(fin)

            prev = None
            for ui, (qt, kp, npr) in enumerate(units):
                par = ui % 2
                scname, scp = SC[par]
                cols = min(1024, (qt + 1) * 128 - kp * 1024)
                kbase = kp * 1024
                is_diag = (kp == npr - 1)
                wF("tensor", scname)
                nm = (cols + 511) // 512
                for j in range(nm):
                    n = min(512, cols - j * 512)
                    v = op("tensor", "matmul", scp[:, j * 512:j * 512 + n],
                           QT[:, qt * 128:(qt + 1) * 128],
                           KT[:, kbase + j * 512:kbase + j * 512 + n],
                           start=True, stop=True, inc=(pe if j == nm - 1 else None))
                sR(scname, pe, v)
                if is_diag:
                    W("vector", pe, v)
                    v = op("vector", "tensor_tensor", out=scp[:, cols - 128:cols],
                           in0=scp[:, cols - 128:cols], in1=cmask[:, :], op=ALU.add, inc=dve)
                    sR(scname + "m", dve, v)
                W("scalar", pe, bd.R[scname][1])
                if is_diag:
                    W("scalar", dve, bd.R[scname + "m"][1])
                wF("scalar", f"p{par}")
                if kp == 0:
                    wF("scalar", f"lparts{qt % 2}")
                v = op("scalar", "activation", p_sb[par][:, 0:cols], scp[:, 0:cols],
                       AF.Exp, bias=bias_cols[:, qt:qt + 1], scale=INV_SQRT_HD,
                       accum_out=lparts[:, (qt % 2) * 16 + kp:(qt % 2) * 16 + kp + 1],
                       inc=act)
                sR(f"p{par}", act, v)
                sF(scname, act, v)
                fins, defer[:] = defer[:], []
                for f in fins:
                    f()
                if prev is not None:
                    emit_trav(*prev)
                prev = (qt, kp, npr, par)
            fins, defer[:] = defer[:], []
            for f in fins:
                f()
            emit_trav(*prev)
            for f in defer:
                f()
            W("gpsimd", dve, bd.R["avT_done"][1])
            v = bd.dma("gpsimd", dgp, out=ag_in_h[h][:, :], in_=avT_sb[:, :])
            sF("avT_sb", dgp, v)
            W("gpsimd", dgp, v)
            v = op("gpsimd", "collective_compute", "AllGather", ALU.bypass,
                   replica_groups=[[0, 1, 2, 3], [4, 5, 6, 7]],
                   ins=[ag_in_h[h].ap().opt()], outs=[ag_out_h[h].ap().opt()], inc=cc)
        sR("ag", cc, cc.n)

        PE_END_A = pe.n

        POP = [("po0", pbank(0)), ("po1", pbank(1))]

        W("sync", pe, PE_END_A)
        v = bd.dma("sync", dsy, out=wout_sb[:, :], in_=wout[:, :])
        sR("wout", dsy, v)

        for tt in range(4):
            def agdma(e, tt=tt, nprev=None):
                with e.register(f"g2_{tt}") as g2:
                    for ah in range(HPC):
                        e.wait_ge(dsy.h, nprev + ah * 16)
                        e.reg_load(g2, coff_sb[:1, :1])
                        e.reg_add(g2, g2, tt * 512)
                        e.dma_start(
                            out=attT[:, ah * 2048:(ah + 1) * 2048].rearrange(
                                "p (r t) -> p r t", t=512),
                            in_=bass.AP(ag_out_h[ah], g2, [[S, 128], [128 * S, 4], [1, 512]]),
                        ).then_inc(dsy.h, 16)
            W("sync", cc, bd.R["ag"][1])
            W("sync", dgp, bd.R["smallin"][1])
            wF("sync", "attT")
            _np = dsy.n
            bd.raw("sync", lambda e, f=agdma, n=_np: f(e, nprev=n))
            vA = dsy.inc(16 * HPC)
            sR("attT", dsy, vA)
            wF("sync", "xr_t")
            v = bd.dma("sync", dsy, out=xr_t[:, :],
                   in_=bass.AP(xr, tt * 512, [[NKC * TC, 128], [TC, NKC], [1, 512]]))
            sR("xr_t", dsy, v)
            W("tensor", dsy, bd.R["attT"][1])
            if tt == 0:
                W("tensor", dsy, bd.R["wout"][1])
            for m in range(NKC):
                par = m % 2
                bname, psum = POP[par]
                wF("tensor", bname)
                for kc in range(NKC):
                    kcg = (kc % 4) * 4 + kc // 4
                    v = op("tensor", "matmul", psum[:, :],
                           wout_sb[:, kcg * D + m * 128:kcg * D + m * 128 + 128],
                           attT[:, kc * 512:(kc + 1) * 512],
                           start=(kc == 0), stop=(kc == NKC - 1),
                           inc=(pe if kc == NKC - 1 else None))
                sR(bname, pe, v)
                if m == NKC - 1:
                    sF("attT", pe, v)
                W("vector", pe, v)
                W("vector", dsy, bd.R["xr_t"][1])
                v = op("vector", "scalar_tensor_tensor",
                       out=r1[:, m * TC + tt * 512:m * TC + tt * 512 + 512],
                       in0=psum[:, :], scalar=bo_f[:, m:m + 1],
                       in1=xr_t[:, m * 512:(m + 1) * 512],
                       op0=ALU.add, op1=ALU.add, inc=dve)
                sF(bname, dve, v)
                if m == NKC - 1:
                    sF("xr_t", dve, v)
                    if tt == 3:
                        sR("r1", dve, v)

        SSUM = ("ssum", pbank(4)[0:1, :])
        SSQ = ("ssq", pbank(5)[0:1, :])
        BCP = ("bcp", pbank(6))
        W("scalar", dve, bd.R["r1"][1])
        for tt in range(4):
            for kc in range(NKC):
                par = kc % 2
                W("tensor", dve, bd.R["r1"][1])
                if kc == 0:
                    wF("tensor", "ssum")
                    wF("tensor", "ssq")
                v = op("tensor", "matmul", SSUM[1][:, :], ones_c_bf[:, :],
                       r1[:, kc * TC + tt * 512:kc * TC + tt * 512 + 512],
                       start=(kc == 0), stop=(kc == NKC - 1),
                       inc=(pe if kc == NKC - 1 else None))
                wF("scalar", f"sqB{par}")
                v2 = op("scalar", "activation", sq_tmp[par][:, :],
                        r1[:, kc * TC + tt * 512:kc * TC + tt * 512 + 512],
                        AF.Square, bias=0.0, scale=1.0, inc=act)
                sR(f"sqB{par}", act, v2)
                W("tensor", act, v2)
                v = op("tensor", "matmul", SSQ[1][:, :], ones_c_bf[:, :], sq_tmp[par][:, :],
                       start=(kc == 0), stop=(kc == NKC - 1), inc=pe)
                sF(f"sqB{par}", pe, v)
            sR("stats", pe, v)
            W("vector", pe, bd.R["stats"][1])
            op("vector", "tensor_copy", out=srow[0:1, tt * 512:(tt + 1) * 512], in_=SSUM[1][:, :])
            v = op("vector", "tensor_copy", out=sqrow[0:1, tt * 512:(tt + 1) * 512],
                   in_=SSQ[1][:, :], inc=dve)
            sR("statrows", dve, v)
            sF("ssum", dve, v)
            sF("ssq", dve, v)
        op("vector", "tensor_scalar", out=mean_r[:, :], in0=srow[:, :],
           scalar1=1.0 / D, scalar2=None, op0=ALU.mult)
        op("vector", "tensor_tensor", out=mrstd_r[:, :], in0=mean_r[:, :], in1=mean_r[:, :],
           op=ALU.mult)
        v = op("vector", "scalar_tensor_tensor", out=rstd_r[:, :], in0=sqrow[:, :],
               scalar=1.0 / D, in1=mrstd_r[:, :], op0=ALU.mult, op1=ALU.subtract, inc=dve)
        W("scalar", dve, v)
        op("scalar", "activation", rstd_r[:, :], rstd_r[:, :], AF.Ln, bias=eps5[0:1, :], scale=1.0)
        v = op("scalar", "activation", rstd_r[:, :], rstd_r[:, :], AF.Exp, bias=0.0,
               scale=-0.5, inc=act)
        sR("rstd_r", act, v)
        W("vector", act, v)
        v = op("vector", "tensor_tensor", out=mrstd_r[:, :], in0=mean_r[:, :], in1=rstd_r[:, :],
               op=ALU.mult, inc=dve)
        sR("mrstd_r", dve, v)
        for tt in range(4):
            W("tensor", dve, bd.R["mrstd_r"][1])
            wF("tensor", "bcp")
            v = op("tensor", "matmul", BCP[1][:, :], ones_r_f[:, :],
                   rstd_r[0:1, tt * 512:(tt + 1) * 512], start=True, stop=True, inc=pe)
            W("vector", pe, v)
            v = op("vector", "tensor_copy", out=rstd_b[:, tt * 512:(tt + 1) * 512],
                   in_=BCP[1][:, :], inc=dve)
            sF("bcp", dve, v)
            wF("tensor", "bcp")
            v = op("tensor", "matmul", BCP[1][:, :], ones_r_f[:, :],
                   mrstd_r[0:1, tt * 512:(tt + 1) * 512], start=True, stop=True, inc=pe)
            W("vector", pe, v)
            v = op("vector", "tensor_copy", out=mrstd_b[:, tt * 512:(tt + 1) * 512],
                   in_=BCP[1][:, :], inc=dve)
            sF("bcp", dve, v)
            if tt == 3:
                sR("bcast1", dve, v)

        LGP = [(f"lg{tt}", pbank(tt)[0:8, :]) for tt in range(4)]
        for tt in range(4):
            for kc in range(NKC):
                par = kc % 2
                W("vector", dve, bd.R["bcast1"][1])
                op("vector", "tensor_tensor", out=lnt[:, :],
                   in0=r1[:, kc * TC + tt * 512:kc * TC + tt * 512 + 512],
                   in1=rstd_b[:, tt * 512:(tt + 1) * 512], op=ALU.mult)
                wF("vector", f"x1s{par}")
                wF("vector", f"x1sD{par}")
                v = op("vector", "tensor_tensor", out=x1s[par][:, :], in0=lnt[:, :],
                       in1=mrstd_b[:, tt * 512:(tt + 1) * 512], op=ALU.subtract, inc=dve)
                sR(f"x1s{par}", dve, v)
                W("scalar", dve, v)
                v = bd.dma("scalar", dsc,
                       out=bass.AP(x1d, kc * 128 * TC + tt * 512, [[TC, 128], [1, 512]]),
                       in_=x1s[par][:, :])
                sF(f"x1sD{par}", dsc, v)
                vx1 = v
                W("tensor", dve, bd.R[f"x1s{par}"][1])
                if tt == 0 and kc == 0:
                    W("tensor", pool, bd.R["wg_lo"][1])
                    wF("tensor", f"lgb{tt}")
                op("tensor", "matmul", LGP[tt][1][:, :], wg_hi[:, kc * 8:(kc + 1) * 8],
                   x1s[par][:, :], start=(kc == 0), stop=False)
                v = op("tensor", "matmul", LGP[tt][1][:, :], wg_lo[:, kc * 8:(kc + 1) * 8],
                       x1s[par][:, :], start=False, stop=(kc == NKC - 1), inc=pe)
                sF(f"x1s{par}", pe, v)
                if kc == NKC - 1:
                    sR(LGP[tt][0], pe, v)
        sR("x1d_all", dsc, vx1)

        TRP = ("trp", pbank(4))
        CMP_ = ("cmp2", pa[0:8, 0:2048])
        for tt in range(4):
            W("vector", pe, bd.R[LGP[tt][0]][1])
            v = op("vector", "tensor_copy", out=lg_sb[:, tt * 512:(tt + 1) * 512],
                   in_=LGP[tt][1][:, :], inc=dve)
            if tt == 3:
                sR("lg", dve, v)
        W("tensor", dve, bd.R["lg"][1])
        wF("tensor", "trp")
        for ti in range(NKC):
            v = op("tensor", "transpose", TRP[1][:, ti * 8:(ti + 1) * 8],
                   lg_sb[0:8, ti * 128:(ti + 1) * 128], ident_f[0:8, 0:8],
                   inc=(pe if ti == NKC - 1 else None))
        sR("trp", pe, v)
        W("vector", pe, v)
        v = op("vector", "tensor_copy", out=ltok[:, :], in_=TRP[1][:, 0:NKC * 8], inc=dve)
        sF("trp", dve, v)
        for ti in range(NKC):
            lt = ltok[:, ti * 8:(ti + 1) * 8]
            op("vector", "reduce_max", out=m1c[:, :], in_=lt, axis=AX.X)
            op("vector", "reduce_max", out=nm1c[:, :], in_=lt, axis=AX.X, negate=True)
            op("vector", "tensor_scalar", out=msk8[:, :], in0=lt, scalar1=m1c[:, :],
               scalar2=None, op0=ALU.is_ge)
            op("vector", "scalar_tensor_tensor", out=msk8[:, :], in0=msk8[:, :],
               scalar=-1e30, in1=lt, op0=ALU.mult, op1=ALU.add)
            v = op("vector", "reduce_max", out=m2c[:, :], in_=msk8[:, :], axis=AX.X, inc=dve)
            W("scalar", dve, v)
            op("scalar", "activation", e1t[:, :], lt, AF.Exp, bias=nm1c[:, :], scale=1.0)
            v = op("scalar", "activation", e2c[:, :], m2c[:, :], AF.Exp, bias=nm1c[:, :],
                   scale=1.0, inc=act)
            W("vector", act, v)
            op("vector", "tensor_scalar", out=e2c[:, :], in0=e2c[:, :], scalar1=1.0,
               scalar2=None, op0=ALU.add)
            op("vector", "reciprocal", out=dinvc[:, :], in_=e2c[:, :])
            op("vector", "tensor_scalar", out=msk8[:, :], in0=lt, scalar1=m2c[:, :],
               scalar2=None, op0=ALU.is_ge)
            v = op("vector", "scalar_tensor_tensor", out=ltok[:, ti * 8:(ti + 1) * 8],
                   in0=e1t[:, :], scalar=dinvc[:, :], in1=msk8[:, :],
                   op0=ALU.mult, op1=ALU.mult, inc=dve)
            sR(f"cmb{ti}", dve, v)
        W("tensor", dve, bd.R[f"cmb{NKC - 1}"][1])
        for ti in range(NKC):
            v = op("tensor", "transpose", CMP_[1][:, ti * 128:(ti + 1) * 128],
                   ltok[:, ti * 8:(ti + 1) * 8], ident_f[:, :],
                   inc=(pe if ti == NKC - 1 else None))
        sR("cmp2", pe, v)
        W("vector", pe, v)
        v = op("vector", "tensor_copy", out=lg_sb[:, :], in_=CMP_[1][:, :], inc=dve)
        sR("cmbT", dve, v)
        sF("cmp2", dve, v)
        W("gpsimd", dve, v)
        v = bd.dma("gpsimd", dgp, out=cmb_d[:, :], in_=lg_sb[:, :])
        sR("cmb_d", dgp, v)

        if "dbg_x1" in dbg:
            W("gpsimd", dsc, bd.R["x1d_all"][1])
            bd.dma("gpsimd", dgp, out=dbg["dbg_x1"][:, :, :], in_=x1d[:, :, :])
        if "dbg_cmb" in dbg:
            W("gpsimd", dgp, bd.R["cmb_d"][1])
            bd.dma("gpsimd", dgp, out=dbg["dbg_cmb"][:, :], in_=cmb_d[:, :])

        GUP = [(f"gu{i}", pbank(i)) for i in range(4)]
        CBP = ("cbp", pbank(6))
        S2SUM = ("s2sum", pbank(4)[0:1, :])
        S2SQ = ("s2sq", pbank(5)[0:1, :])
        gu_rot = [0]
        nstage = [0]
        nslice = [0]
        QWQ = [("sync", dsy), ("gpsimd", dgp)]

        def stream_weight(src_ap):
            par = nslice[0] % 2
            nslice[0] += 1
            for half in range(2):
                s3 = nstage[0] % 3
                qeng, qsem = QWQ[nstage[0] % 2]
                nstage[0] += 1
                wF(qeng, f"wst{s3}")
                v = bd.dma(qeng, qsem, out=wstage[s3][:, :],
                           in_=src_ap[:, half * 1024:(half + 1) * 1024])
                sR(f"wst{s3}", qsem, v)
                bd.wait("vector", qsem, v)
                if half == 0:
                    wF("vector", f"whi{par}")
                    wF("vector", f"wlo{par}")
                vhi = op("vector", "tensor_copy", out=w_hi[par][:, half * 1024:(half + 1) * 1024],
                         in_=wstage[s3][:, :], inc=dve)
                vlo = op("vector", "tensor_tensor", out=w_lo[par][:, half * 1024:(half + 1) * 1024],
                         in0=wstage[s3][:, :], in1=w_hi[par][:, half * 1024:(half + 1) * 1024],
                         op=ALU.subtract, inc=dve)
                sF(f"wst{s3}", dve, vlo)
            return (par, vhi, vlo)

        def gate_up_part(wm, psrc_is_g, jp, tts_act_col, e_cmb):
            par_w, vhi, vlo = wm
            for tts in range(2):
                rot = gu_rot[0] % 4
                gu_rot[0] += 1
                bname, psum = GUP[rot]
                wF("tensor", bname)
                W("tensor", dve, vlo)
                nmm = 0
                for kc in range(NKC):
                    acol = kc * TSW + tts * 512
                    for wt in (w_hi[par_w], w_lo[par_w]):
                        nmm += 1
                        v = op("tensor", "matmul", psum[:, :],
                               wt[:, kc * 128:(kc + 1) * 128], x1_sw[:, acol:acol + 512],
                               start=(nmm == 1), stop=(nmm == NKC * 2),
                               inc=(pe if nmm == NKC * 2 else None))
                sR(bname, pe, v)
                if tts == 1:
                    sF(f"whi{par_w}", pe, v)
                    sF(f"wlo{par_w}", pe, v)
                if psrc_is_g:
                    W("scalar", pe, v)
                    wF("scalar", f"silu{tts}")
                    v = op("scalar", "activation", silu_g[:, tts * 512:(tts + 1) * 512],
                           psum[:, :], AF.Silu, bias=0.0, scale=1.0, inc=act)
                    sR(f"siluR{tts}", act, v)
                    sF(bname, act, v)
                else:
                    W("vector", pe, v)
                    W("vector", act, bd.R[f"siluR{tts}"][1])
                    if e_cmb is not None:
                        W("vector", dve, bd.R[f"cmb_bR{tts}"][1])
                        op("vector", "tensor_tensor", out=act_f[:, tts * 512:(tts + 1) * 512],
                           in0=psum[:, :], in1=cmb_b[:, tts * 512:(tts + 1) * 512], op=ALU.mult)
                        v = op("vector", "tensor_tensor",
                               out=act_f[:, tts * 512:(tts + 1) * 512],
                               in0=act_f[:, tts * 512:(tts + 1) * 512],
                               in1=silu_g[:, tts * 512:(tts + 1) * 512], op=ALU.mult, inc=dve)
                    else:
                        v = op("vector", "tensor_tensor",
                               out=act_f[:, tts * 512:(tts + 1) * 512], in0=psum[:, :],
                               in1=silu_g[:, tts * 512:(tts + 1) * 512], op=ALU.mult, inc=dve)
                    sF(bname, dve, v)
                    sF(f"silu{tts}", dve, v)
                    acol = tts_act_col + tts * 512
                    wF("vector", "act_hi")
                    v = op("vector", "tensor_copy", out=act_hi[:, acol:acol + 512],
                           in_=act_f[:, tts * 512:(tts + 1) * 512], inc=dve)
                    sR(f"ahi{jp}_{tts}", dve, v)
                    W("gpsimd", dve, v)
                    wF("gpsimd", "act_lo")
                    v = op("gpsimd", "tensor_tensor", out=act_lo[:, acol:acol + 512],
                           in0=act_f[:, tts * 512:(tts + 1) * 512],
                           in1=act_hi[:, acol:acol + 512], op=ALU.subtract, inc=pool)
                    sR(f"alo{jp}_{tts}", pool, v)

        def down_slice(wm, ds, first_acc):
            par, vhi, vlo = wm
            for d2 in range(2):
                for tts in range(2):
                    rot = gu_rot[0] % 4
                    gu_rot[0] += 1
                    bname, psum = GUP[rot]
                    wF("tensor", bname)
                    W("tensor", dve, vlo)
                    nmm = 0
                    for ic in range(8):
                        wcol = ic * 256 + d2 * 128
                        acol = ic * TSW + tts * 512
                        for wt, at in ((w_hi[par], act_hi), (w_hi[par], act_lo),
                                       (w_lo[par], act_hi)):
                            nmm += 1
                            v = op("tensor", "matmul", psum[:, :],
                                   wt[:, wcol:wcol + 128], at[:, acol:acol + 512],
                                   start=(nmm == 1), stop=(nmm == 24),
                                   inc=(pe if nmm == 24 else None))
                    sR(bname, pe, v)
                    if d2 == 1 and tts == 1:
                        sF(f"whi{par}", pe, v)
                        sF(f"wlo{par}", pe, v)
                    dch = ds * 2 + d2
                    W("vector", pe, v)
                    accap = ffn_acc[:, dch * TSW + tts * 512:dch * TSW + tts * 512 + 512]
                    if first_acc[dch * 2 + tts]:
                        first_acc[dch * 2 + tts] = False
                        v = op("vector", "tensor_copy", out=accap, in_=psum[:, :], inc=dve)
                    else:
                        v = op("vector", "tensor_tensor", out=accap, in0=accap,
                               in1=psum[:, :], op=ALU.add, inc=dve)
                    sF(bname, dve, v)

        for sw in range(SW):
            wF("sync", "x1_sw")
            W("sync", dsc, bd.R["x1d_all"][1])
            W("sync", dgp, bd.R["cmb_d"][1])
            v = bd.dma("sync", dsy,
                       out=x1_sw[:, :].rearrange("p (k t) -> p k t", t=TSW),
                       in_=bass.AP(x1d, sw * TSW, [[TC, 128], [128 * TC, NKC], [1, TSW]]))
            sR("x1_sw", dsy, v)
            W("tensor", dsy, v)

            first_acc = [True] * (NKC * 2)

            units = []

            def mk_pre_e(e, sw=sw):
                def pre():
                    wF("gpsimd", "cmb_r")
                    vcr = bd.dma("gpsimd", dgp, out=cmb_r[:, :],
                                 in_=cmb_d[e:e + 1, sw * TSW:(sw + 1) * TSW])
                    for tts in range(2):
                        W("tensor", dgp, vcr)
                        wF("tensor", "cbp")
                        v = op("tensor", "matmul", CBP[1][:, :], ones_r_f[:, :],
                               cmb_r[0:1, tts * 512:(tts + 1) * 512],
                               start=True, stop=True, inc=pe)
                        sR("cbp", pe, v)
                        sF("cmb_r", pe, v)
                        W("vector", pe, v)
                        wF("vector", f"cmb_b{tts}")
                        v = op("vector", "tensor_copy",
                               out=cmb_b[:, tts * 512:(tts + 1) * 512],
                               in_=CBP[1][:, :], inc=dve)
                        sR(f"cmb_bR{tts}", dve, v)
                        sF("cbp", dve, v)
                return pre

            def pre_dn():
                for jp in range(8):
                    for tts in range(2):
                        W("tensor", dve, bd.R[f"ahi{jp}_{tts}"][1])
                        W("tensor", pool, bd.R[f"alo{jp}_{tts}"][1])

            def post_e():
                sF("act_hi", pe, pe.n)
                sF("act_lo", pe, pe.n)
                sF("silu0", pe, pe.n)
                sF("silu1", pe, pe.n)
                sF("cmb_b0", pe, pe.n)
                sF("cmb_b1", pe, pe.n)

            for e in range(E):
                for jp in range(8):
                    units.append((wgu[e, jp, :, :],
                                  (lambda wm, jp=jp: gate_up_part(wm, True, jp, None, None)),
                                  mk_pre_e(e) if jp == 0 else None, None))
                    units.append((wgu[e, jp + 8, :, :],
                                  (lambda wm, jp=jp, e=e: gate_up_part(wm, False, jp, jp * TSW, e)),
                                  None, None))
                for ds in range(8):
                    units.append((wdn[e, ds, :, :],
                                  (lambda wm, ds=ds: down_slice(wm, ds, first_acc)),
                                  pre_dn if ds == 0 else None,
                                  post_e if ds == 7 else None))
            for ig in range(4):
                for jp in range(8):
                    units.append((wsgu[ig * 8 + jp, :, :],
                                  (lambda wm, jp=jp: gate_up_part(wm, True, jp, None, None)),
                                  None, None))
                    units.append((wsgu[32 + ig * 8 + jp, :, :],
                                  (lambda wm, jp=jp: gate_up_part(wm, False, jp, jp * TSW, None)),
                                  None, None))
                for ds in range(8):
                    units.append((wsdn[ig, ds, :, :],
                                  (lambda wm, ds=ds: down_slice(wm, ds, first_acc)),
                                  pre_dn if ds == 0 else None,
                                  post_e if ds == 7 else None))

            wms = {}
            streamed = [0]

            def ensure(k):
                while streamed[0] <= k:
                    wms[streamed[0]] = stream_weight(units[streamed[0]][0])
                    streamed[0] += 1

            for i, (srcap, fn, pre, post) in enumerate(units):
                ensure(min(i + 1, len(units) - 1))
                if pre is not None:
                    pre()
                fn(wms.pop(i))
                if post is not None:
                    post()

            for dch in range(NKC):
                ap_ = ffn_acc[:, dch * TSW:(dch + 1) * TSW]
                v = op("vector", "tensor_tensor", out=ap_,
                       in0=x1_sw[:, dch * TSW:(dch + 1) * TSW], in1=ap_, op=ALU.add, inc=dve)
                if dch == NKC - 1:
                    sR("r2", dve, v)
                    sF("x1_sw", dve, v)
            for tts in range(2):
                for dch in range(NKC):
                    spar = dch % 2
                    W("tensor", dve, bd.R["r2"][1])
                    rc = ffn_acc[:, dch * TSW + tts * 512:dch * TSW + tts * 512 + 512]
                    if dch == 0:
                        wF("tensor", "s2sum")
                        wF("tensor", "s2sq")
                    v = op("tensor", "matmul", S2SUM[1][:, :], ones_c_f[:, :], rc,
                           start=(dch == 0), stop=(dch == NKC - 1),
                           inc=(pe if dch == NKC - 1 else None))
                    W("scalar", dve, bd.R["r2"][1])
                    wF("scalar", f"sq2_{spar}")
                    v2 = op("scalar", "activation", sq2[spar][:, :], rc, AF.Square,
                            bias=0.0, scale=1.0, inc=act)
                    sR("sq2R", act, v2)
                    W("tensor", act, v2)
                    v = op("tensor", "matmul", S2SQ[1][:, :], ones_c_f[:, :], sq2[spar][:, :],
                           start=(dch == 0), stop=(dch == NKC - 1), inc=pe)
                    sF(f"sq2_{spar}", pe, v)
                sR("stats2", pe, v)
                W("vector", pe, bd.R["stats2"][1])
                op("vector", "tensor_copy", out=s2row[:, :], in_=S2SUM[1][:, :])
                v = op("vector", "tensor_copy", out=sq2row[:, :], in_=S2SQ[1][:, :], inc=dve)
                sF("s2sum", dve, v)
                sF("s2sq", dve, v)
                op("vector", "tensor_scalar", out=mean2_r[:, :], in0=s2row[:, :],
                   scalar1=1.0 / D, scalar2=None, op0=ALU.mult)
                op("vector", "tensor_tensor", out=mrstd2_r[:, :], in0=mean2_r[:, :],
                   in1=mean2_r[:, :], op=ALU.mult)
                v = op("vector", "scalar_tensor_tensor", out=rstd2_r[:, :], in0=sq2row[:, :],
                       scalar=1.0 / D, in1=mrstd2_r[:, :], op0=ALU.mult, op1=ALU.subtract,
                       inc=dve)
                W("scalar", dve, v)
                op("scalar", "activation", rstd2_r[:, :], rstd2_r[:, :], AF.Ln, bias=eps5[0:1, :],
                   scale=1.0)
                v = op("scalar", "activation", rstd2_r[:, :], rstd2_r[:, :], AF.Exp, bias=0.0,
                       scale=-0.5, inc=act)
                W("vector", act, v)
                v = op("vector", "tensor_tensor", out=mrstd2_r[:, :], in0=mean2_r[:, :],
                       in1=rstd2_r[:, :], op=ALU.mult, inc=dve)
                sR("mrstd2", dve, v)
                W("tensor", dve, v)
                wF("tensor", "cbp")
                v = op("tensor", "matmul", CBP[1][:, :], ones_r_f[:, :], rstd2_r[0:1, :],
                       start=True, stop=True, inc=pe)
                W("vector", pe, v)
                v = op("vector", "tensor_copy", out=rstd2_b[:, :], in_=CBP[1][:, :], inc=dve)
                sF("cbp", dve, v)
                wF("tensor", "cbp")
                v = op("tensor", "matmul", CBP[1][:, :], ones_r_f[:, :], mrstd2_r[0:1, :],
                       start=True, stop=True, inc=pe)
                W("vector", pe, v)
                v = op("vector", "tensor_copy", out=mrstd2_b[:, :], in_=CBP[1][:, :], inc=dve)
                sF("cbp", dve, v)
                for dch in range(NKC):
                    par = dch % 2
                    rc = ffn_acc[:, dch * TSW + tts * 512:dch * TSW + tts * 512 + 512]
                    op("vector", "tensor_tensor", out=t2f[:, :], in0=rc, in1=rstd2_b[:, :],
                       op=ALU.mult)
                    wF("vector", f"ost{par}")
                    v = op("vector", "tensor_tensor", out=ost[par][:, :], in0=t2f[:, :],
                           in1=mrstd2_b[:, :], op=ALU.subtract, inc=dve)
                    W("scalar", dve, v)
                    v = bd.dma("scalar", dsc,
                           out=out_d[dch * 128:(dch + 1) * 128,
                                     sw * TSW + tts * 512:sw * TSW + tts * 512 + 512],
                           in_=ost[par][:, :])
                    sF(f"ost{par}", dsc, v)

        W("scalar", dsc, dsc.n)
        W("sync", dsy, dsy.n)
        W("gpsimd", dgp, dgp.n)

        with nc.Block() as block:
            @block.tensor
            def _(eng):
                for f in bd.plan["tensor"]:
                    f(eng)

            @block.vector
            def _(eng):
                for f in bd.plan["vector"]:
                    f(eng)

            @block.scalar
            def _(eng):
                for f in bd.plan["scalar"]:
                    f(eng)

            @block.gpsimd
            def _(eng):
                for f in bd.plan["gpsimd"]:
                    f(eng)

            @block.sync
            def _(eng):
                for f in bd.plan["sync"]:
                    f(eng)

    return nc



def _layout_inputs(inputs):
    x = np.asarray(inputs["L0_attn_x"])
    Wqkv = np.asarray(inputs["L0_attn_W_qkv"])
    bqkv = np.asarray(inputs["L0_attn_b_qkv"])
    Wout = np.asarray(inputs["L0_attn_W_out"])
    bout = np.asarray(inputs["L0_attn_b_out"])
    wg = np.asarray(inputs["L0_ffn_w_gate"])
    wgu = np.asarray(inputs["L0_ffn_w_gate_up"])
    wdn = np.asarray(inputs["L0_ffn_w_down"])
    wsgu = np.asarray(inputs["L0_ffn_w_sh_gate_up"])
    wsdn = np.asarray(inputs["L0_ffn_w_sh_down"])

    wout_h = np.ascontiguousarray(
        Wout.reshape(16, 128, 2048).transpose(1, 0, 2).reshape(128, 16 * 2048))
    bout_h = np.ascontiguousarray(bout.reshape(16, 128).T)
    wg_h = np.ascontiguousarray(
        wg.reshape(16, 128, 8).transpose(1, 0, 2).reshape(128, 128))
    wgu_h = np.ascontiguousarray(
        wgu.reshape(8, 16, 128, 16, 128).transpose(0, 3, 2, 1, 4).reshape(8, 16, 128, 2048))
    wdn_h = np.ascontiguousarray(
        wdn.reshape(8, 8, 128, 8, 256).transpose(0, 3, 2, 1, 4).reshape(8, 8, 128, 2048))
    wsgu_h = np.ascontiguousarray(
        wsgu.reshape(16, 128, 64, 128).transpose(2, 1, 0, 3).reshape(64, 128, 2048))
    wsdn_h = np.ascontiguousarray(
        wsdn.reshape(4, 8, 128, 8, 256).transpose(0, 3, 2, 1, 4).reshape(4, 8, 128, 2048))

    Wq5 = Wqkv.reshape(16, 128, 3, 16, 128)
    bq3 = bqkv.reshape(3, 16, 128)

    in_maps = []
    for c in range(8):
        b, hg = c // 4, c % 4
        s = c % 4
        xa_h = np.ascontiguousarray(
            x[b].reshape(NTA, TTA, 16, 128).transpose(0, 3, 2, 1).reshape(NTA, 128, 16 * TTA))
        xr_h = np.ascontiguousarray(
            x[b, s * TC:(s + 1) * TC].reshape(TC, 16, 128).transpose(2, 1, 0).reshape(128, 16 * TC))
        wqkv_h = np.ascontiguousarray(
            Wq5[:, :, :, hg * 4:(hg + 1) * 4, :].transpose(1, 0, 3, 2, 4).reshape(128, 16 * 1536))
        bq_h = np.ascontiguousarray(
            bq3[:, hg * 4:(hg + 1) * 4, :].transpose(2, 1, 0).reshape(128, 12))
        in_maps.append({
            "xa": xa_h, "xr": xr_h, "wqkv": wqkv_h, "bqkv": bq_h,
            "wout": wout_h, "bout": bout_h, "wg": wg_h,
            "wgu": wgu_h, "wdn": wdn_h, "wsgu": wsgu_h, "wsdn": wsdn_h,
            "coff": np.array([[s * TC]], dtype=np.int32),
        })
    return in_maps


def kernel(**inputs):
    in_maps = _layout_inputs(inputs)
    nc = build_nc(DEBUG_TAPS)
    res = run_bass_kernel_spmd(nc, in_maps, core_ids=list(range(8)))
    out = np.empty((B, S, D), np.float32)
    for c in range(8):
        b, s = c // 4, c % 4
        out[b, s * TC:(s + 1) * TC, :] = res.results[c]["out"].T
    if DEBUG_TAPS:
        kernel.last_results = res
    return out


# version 05
# speedup vs baseline: 1.3746x; 1.0037x over previous
import numpy as np
import ml_dtypes
from contextlib import ExitStack

import concourse.bass as bass
import concourse.mybir as mybir
from concourse.bass_utils import run_bass_kernel_spmd

F32 = mybir.dt.float32
BF16 = mybir.dt.bfloat16
I32 = mybir.dt.int32
U32 = mybir.dt.uint32
U8 = mybir.dt.uint8
AF = mybir.ActivationFunctionType
ALU = mybir.AluOpType
AX = mybir.AxisListType

B, S, D, H, HD = 2, 8192, 2048, 16, 128
HPC = 4
NKC = D // 128
TC = 2048
TTA = 512
NTA = S // TTA
NQT = S // 128
E, IM, ISH = 8, 1024, 4096
SW, TSW = 2, 1024
INV_SQRT_HD = float(1.0 / np.sqrt(HD))

ENGS = ("tensor", "vector", "scalar", "gpsimd", "sync")
DEBUG_TAPS = []


class Sem:
    def __init__(self, h):
        self.h = h
        self.n = 0

    def inc(self, k=1):
        self.n += k
        return self.n


class Builder:
    SELF_EDGE = {"vector", "scalar", "gpsimd"}
    ASYNC_METHS = {"dma_start", "collective_compute", "indirect_dma_start"}

    def __init__(self, selfsems=None):
        self.plan = {e: [] for e in ENGS}
        self.R = {}
        self.F = {}
        self.selfsems = selfsems or {}

    def op(self, eng, meth, *a, inc=None, k=1, **kw):
        if eng in self.SELF_EDGE and meth not in self.ASYNC_METHS:
            sem = self.selfsems[eng]
            if sem.n > 0:
                self.wait(eng, sem, sem.n)
            if inc is None:
                inc, k = sem, 1
            else:
                assert inc is sem, (eng, meth)
        if inc is None:
            self.plan[eng].append(lambda e, m=meth, aa=a, kk=kw: getattr(e, m)(*aa, **kk))
            return None
        v = inc.inc(k)
        self.plan[eng].append(
            lambda e, m=meth, aa=a, kk=kw, s=inc.h, n=k: getattr(e, m)(*aa, **kk).then_inc(s, n))
        return v

    def dma(self, eng, qsem, **kw):
        self.wait(eng, qsem, qsem.n)
        return self.op(eng, "dma_start", inc=qsem, k=16, **kw)

    def wait(self, eng, sem, val):
        if val and val > 0:
            self.plan[eng].append(lambda e, s=sem.h, v=val: e.wait_ge(s, v))

    def raw(self, eng, fn):
        self.plan[eng].append(fn)

    def wR(self, eng, *names):
        for n in names:
            g = self.R.get(n)
            if g is not None:
                self.wait(eng, g[0], g[1])

    def wF(self, eng, *names):
        for n in names:
            g = self.F.get(n)
            if g is not None:
                self.wait(eng, g[0], g[1])

    def sR(self, name, sem, val):
        self.R[name] = (sem, val)

    def sF(self, name, sem, val):
        self.F[name] = (sem, val)


def build_nc(debug_taps=()):
    nc = bass.Bass("TRN2", target_bir_lowering=False)

    xa = nc.dram_tensor("xa", [NTA, 128, NKC * TTA], BF16, kind="ExternalInput")
    xr = nc.dram_tensor("xr", [128, NKC * TC], BF16, kind="ExternalInput")
    wqkv = nc.dram_tensor("wqkv", [128, NKC * HPC * 3 * 128], BF16, kind="ExternalInput")
    bqkv = nc.dram_tensor("bqkv", [128, HPC * 3], BF16, kind="ExternalInput")
    wout = nc.dram_tensor("wout", [128, NKC * D], BF16, kind="ExternalInput")
    bout = nc.dram_tensor("bout", [128, NKC], BF16, kind="ExternalInput")
    wg = nc.dram_tensor("wg", [128, NKC * E], F32, kind="ExternalInput")
    wgu = nc.dram_tensor("wgu", [E, 16, 128, NKC * 128], F32, kind="ExternalInput")
    wdn = nc.dram_tensor("wdn", [E, 4, 8, 128, 512], F32, kind="ExternalInput")
    wsgu = nc.dram_tensor("wsgu", [64, 128, NKC * 128], F32, kind="ExternalInput")
    wsdn = nc.dram_tensor("wsdn", [4, 4, 8, 128, 512], F32, kind="ExternalInput")
    coff = nc.dram_tensor("coff", [1, 1], I32, kind="ExternalInput")
    out_d = nc.dram_tensor("out", [TC, D], F32, kind="ExternalOutput")

    ag_in_h = [nc.dram_tensor(f"ag_in{h}", [128, S], BF16) for h in range(HPC)]
    ag_out_h = [nc.dram_tensor(f"ag_out{h}", [4 * 128, S], BF16) for h in range(HPC)]
    x1d = nc.dram_tensor("x1d", [NKC, 128, TC], BF16)
    x1t = nc.dram_tensor("x1t", [TC, 2176], BF16)
    xg_d = [nc.dram_tensor(f"xg_d{e}", [1024, 2176], BF16) for e in range(E)]
    eo_d = [nc.dram_tensor(f"eo_d{e}", [1024, D], F32) for e in range(E)]
    facc_d = nc.dram_tensor("facc_d", [TC, D], F32)

    dbg = {}
    for name, shape, dt in debug_taps:
        dbg[name] = nc.dram_tensor(name, shape, dt, kind="ExternalOutput")

    with ExitStack() as es:
        ARENA = 210944
        arena = es.enter_context(nc.sbuf_tensor("arena", [128, ARENA], U8))
        abase = nc.lookup_mloc(arena).addr

        def sb(name, shape, dtype, off):
            per = int(np.prod(shape[1:])) * mybir.dt.size(dtype)
            assert off + per <= ARENA, (name, off, per)
            return nc.alloc_sbuf_tensor_at(name, shape, dtype, offset=abase + off)

        ident_bf = sb("ident_bf", [128, 128], BF16, 0)
        ident_f = sb("ident_f", [128, 128], F32, 256)
        cmask = sb("cmask", [128, 128], F32, 768)
        ones_c_bf = sb("ones_c_bf", [128, 1], BF16, 1280)
        ones_c_f = sb("ones_c_f", [128, 1], F32, 1312)
        ones_r_f = sb("ones_r_f", [1, 128], F32, 1344)
        bq_bf = sb("bq_bf", [128, HPC * 3], BF16, 1856)
        bq_f = sb("bq_f", [128, HPC * 3], F32, 1888)
        bo_bf = sb("bo_bf", [128, NKC], BF16, 1952)
        bo_f = sb("bo_f", [128, NKC], F32, 1984)
        wg_f = sb("wg_f", [128, NKC * E], F32, 2048)
        wg_hi = sb("wg_hi", [128, NKC * E], BF16, 2560)
        wg_lo = sb("wg_lo", [128, NKC * E], BF16, 2816)
        eps12 = sb("eps12", [128, 1], F32, 206176)
        eps5 = sb("eps5", [128, 1], F32, 206208)
        coff_sb = sb("coff_sb", [1, 1], I32, 206240)

        wqkv_sb = sb("wqkv_sb", [128, NKC * 1536], BF16, 3072)
        xt = [sb(f"xt{i}", [128, NKC * TTA], BF16, 52224 + i * 16384) for i in range(2)]
        QT = sb("QT", [128, S], BF16, 84992)
        KT = sb("KT", [128, S], BF16, 101376)
        VT = sb("VT", [128, S], BF16, 117760)
        Vtok = sb("Vtok", [128, S], BF16, 134144)
        avT_sb = sb("avT_sb", [128, S], BF16, 150528)
        p_sb = [sb(f"p_sb{i}", [128, 1024], BF16, 166912 + i * 2048) for i in range(2)]
        pt_sb = [sb(f"pt_sb{i}", [128, 1024], BF16, 171008 + i * 2048) for i in range(2)]
        sq_tmp = [sb(f"sq_tmp{i}", [128, 512], BF16, 175104 + i * 1024) for i in range(2)]
        row_buf = sb("row_buf", [1, 512], F32, 177152)
        braw = sb("braw", [128, NQT], F32, 179200)
        bias_cols = sb("bias_cols", [128, NQT], F32, 179456)
        km_tmp = sb("km_tmp", [128, 1], F32, 179712)
        kmax_col = sb("kmax_col", [128, 1], F32, 179744)
        km_s = sb("km_s", [1, 1], F32, 179776)
        lparts = sb("lparts", [128, 32], F32, 179808)
        lsum = sb("lsum", [128, 1], F32, 179936)
        linv = sb("linv", [128, 1], F32, 179968)
        av_bf = sb("av_bf", [128, 128], BF16, 180000)

        wout_sb = sb("wout_sb", [128, NKC * D], BF16, 3072)
        attT = sb("attT", [128, NKC * 512], BF16, 68608)
        xr_t = sb("xr_t", [128, NKC * 512], BF16, 84992)
        r1 = sb("r1", [128, NKC * TC], BF16, 101376)

        srow = sb("srow", [1, TC], F32, 3072)
        sqrow = sb("sqrow", [1, TC], F32, 11264)
        mean_r = sb("mean_r", [1, TC], F32, 19456)
        rstd_r = sb("rstd_r", [1, TC], F32, 27648)
        mrstd_r = sb("mrstd_r", [1, TC], F32, 35840)
        rstd_b = sb("rstd_b", [128, TC], F32, 44032)
        mrstd_b = sb("mrstd_b", [128, TC], F32, 52224)
        lnt = sb("lnt", [128, 512], F32, 60416)
        x1s = [sb(f"x1s{i}", [128, 512], BF16, 62464 + i * 1024) for i in range(2)]

        lg_sb = sb("lg_sb", [8, TC], F32, 68608)
        ltok = sb("ltok", [128, NKC * 8], F32, 76800)
        m1c = sb("m1c", [128, 1], F32, 77312)
        nm1c = sb("nm1c", [128, 1], F32, 77344)
        m2c = sb("m2c", [128, 1], F32, 77376)
        e2c = sb("e2c", [128, 1], F32, 77408)
        dinvc = sb("dinvc", [128, 1], F32, 77440)
        msk8 = sb("msk8", [128, 8], F32, 77472)
        e1t = sb("e1t", [128, 8], F32, 77504)

        x1_sw = sb("x1_sw", [128, NKC * TSW], BF16, 3072)
        xg_sb = sb("xg_sb", [128, 17 * 1024], BF16, 35840)
        wstage = [sb(f"wstage{i}", [128, 1024], F32, 70656 + i * 4096) for i in range(3)]
        w_hi = [sb(f"w_hi{i}", [128, 4096], BF16, 82944 + i * 8192) for i in range(2)]
        w_lo = [sb(f"w_lo{i}", [128, 4096], BF16, 99328 + i * 8192) for i in range(2)]
        silu_g = sb("silu_g", [128, TSW], F32, 115712)
        act_f = sb("act_f", [128, TSW], F32, 119808)
        act_hi = sb("act_hi", [128, 8 * TSW], BF16, 123904)
        act_lo = sb("act_lo", [128, 8 * TSW], BF16, 140288)
        est = [sb(f"est{i}", [128, 512], F32, 156672 + i * 2048) for i in range(2)]
        tmj_sb = sb("tmj_sb", [128, 2176], BF16, 160768)
        fm_sb = sb("fm_sb", [128, 2048], BF16, 165152)
        ridx = sb("ridx", [128, NKC * 8], U32, 169248)
        mask_f = sb("mask_f", [128, NKC * 8], F32, 169760)
        mask_b = sb("mask_b", [128, NKC * 8], BF16, 170272)
        LT_bf = sb("LT_bf", [128, 128], BF16, 170528)
        ONES_bf = sb("ONES_bf", [128, 128], BF16, 170784)
        cslot = sb("cslot", [2, 1024], BF16, 171040)
        cmb_sb = sb("cmb_sb", [128, 1024], F32, 173088)
        facc_t = [sb(f"facc_t{i}", [128, D], F32, 177184 + i * 8192) for i in range(2)]
        x1row = sb("x1row", [128, D], BF16, 193568)
        sq_t = sb("sq_t", [128, D], F32, 197664)
        s1c = sb("s1c", [128, 1], F32, 206048)
        s2c = sb("s2c", [128, 1], F32, 206080)
        mean_c = sb("mean_c", [128, 1], F32, 206112)
        rstd_c = sb("rstd_c", [128, 1], F32, 206144)

        pa = es.enter_context(nc.psum_tensor("parena", [128, 4096], F32))

        def pbank(k, n=1):
            return pa[:, k * 512:(k + n) * 512]

        sems = {n: Sem(es.enter_context(nc.semaphore(n)))
                for n in ("pe", "act", "dve", "pool", "dsy", "dsc", "dgp", "cc")}
        pe, act, dve, pool = sems["pe"], sems["act"], sems["dve"], sems["pool"]
        dsy, dsc, dgp, cc = sems["dsy"], sems["dsc"], sems["dgp"], sems["cc"]

        bd = Builder({"vector": dve, "scalar": act, "gpsimd": pool})
        op, W, wR, wF, sR, sF = bd.op, bd.wait, bd.wR, bd.wF, bd.sR, bd.sF

        class _H:
            rb = None
        def _mk_rb(e):
            _H.rb = e.alloc_register("rbound")
            e.reg_mov(_H.rb, 1023)
        bd.raw("gpsimd", _mk_rb)

        op("gpsimd", "memset", ident_bf[:, :], 0.0)
        op("gpsimd", "affine_select", out=ident_bf[:, :], in_=ident_bf[:, :],
           compare_op=ALU.not_equal, fill=1.0, base=0, pattern=[[-1, 128]], channel_multiplier=1)
        op("gpsimd", "memset", ident_f[:, :], 0.0)
        op("gpsimd", "affine_select", out=ident_f[:, :], in_=ident_f[:, :],
           compare_op=ALU.not_equal, fill=1.0, base=0, pattern=[[-1, 128]], channel_multiplier=1)
        op("gpsimd", "memset", cmask[:, :], 0.0)
        op("gpsimd", "affine_select", out=cmask[:, :], in_=cmask[:, :],
           compare_op=ALU.is_ge, fill=-1e30, base=0, pattern=[[-1, 128]], channel_multiplier=1)
        op("gpsimd", "memset", ones_c_bf[:, :], 1.0)
        op("gpsimd", "memset", ones_c_f[:, :], 1.0)
        op("gpsimd", "memset", ones_r_f[:, :], 1.0)
        op("gpsimd", "memset", eps12[:, :], 1e-12)
        v = op("gpsimd", "memset", eps5[:, :], 1e-5, inc=pool)
        sR("consts", pool, v)

        bd.dma("gpsimd", dgp, out=bq_bf[:, :], in_=bqkv[:, :])
        bd.dma("gpsimd", dgp, out=bo_bf[:, :], in_=bout[:, :])
        bd.dma("gpsimd", dgp, out=wg_f[:, :], in_=wg[:, :])
        v = bd.dma("gpsimd", dgp, out=coff_sb[:, :], in_=coff[:, :])
        sR("smallin", dgp, v)
        W("vector", dgp, v)
        op("vector", "tensor_copy", out=bq_f[:, :], in_=bq_bf[:, :])
        op("vector", "tensor_copy", out=bo_f[:, :], in_=bo_bf[:, :])
        v = op("vector", "tensor_copy", out=wg_hi[:, :], in_=wg_f[:, :], inc=dve)
        sR("wg_hi", dve, v)
        W("gpsimd", dve, v)
        v = op("gpsimd", "tensor_tensor", out=wg_lo[:, :], in0=wg_f[:, :], in1=wg_hi[:, :],
               op=ALU.subtract, inc=pool)
        sR("wg_lo", pool, v)

        v = bd.dma("sync", dsy, out=wqkv_sb[:, :], in_=wqkv[:, :])
        sR("wqkv", dsy, v)

        SC = [("scA0", pbank(0, 2)), ("scA1", pbank(2, 2))]
        PTP = [("ptA0", pbank(4).bitcast(BF16)), ("ptA1", pbank(5).bitcast(BF16))]
        AVP = ("avA", pbank(6)[:, 0:128])
        BIP = ("biA", pbank(6)[:, 128:192])
        AVTP = ("avTA", pbank(7).bitcast(BF16)[:, 0:128])
        Q2P = ("q2A", pbank(7)[:, 0:512])
        KMP = ("kmA", pbank(7)[:, 0:128])
        QKVP = [[(f"qkv{w}{p}", pbank(w * 2 + p)) for p in range(2)] for w in range(3)]

        for h in range(HPC):
            wF("scalar", "xt0")
            v = bd.dma("scalar", dsc, out=xt[0][:, :], in_=xa[0, :, :])
            sR("xt0", dsc, v)
            for tt in range(NTA):
                par = tt % 2
                xtile = xt[par]
                if tt + 1 < NTA:
                    npar = (tt + 1) % 2
                    wF("scalar", f"xt{npar}")
                    v = bd.dma("scalar", dsc, out=xt[npar][:, :], in_=xa[tt + 1, :, :])
                    sR(f"xt{npar}", dsc, v)
                W("tensor", dsc, bd.R[f"xt{par}"][1])
                if h == 0 and tt == 0:
                    W("tensor", dsy, bd.R["wqkv"][1])
                for w in range(3):
                    bname, psum = QKVP[w][par]
                    wF("tensor", bname)
                    for kc in range(NKC):
                        col = kc * 1536 + h * 384 + w * 128
                        v = op("tensor", "matmul", psum[:, :],
                               wqkv_sb[:, col:col + 128], xtile[:, kc * TTA:(kc + 1) * TTA],
                               start=(kc == 0), stop=(kc == NKC - 1),
                               inc=(pe if kc == NKC - 1 else None))
                    sR(bname, pe, v)
                    if w == 2:
                        sF(f"xt{par}", pe, v)
                for w, dst in enumerate((QT, KT, VT)):
                    bname, psum = QKVP[w][par]
                    W("scalar", pe, bd.R[bname][1])
                    if h == 0 and tt == 0 and w == 0:
                        W("scalar", dve, bd.R["wg_hi"][1])
                    v = op("scalar", "activation", dst[:, tt * TTA:(tt + 1) * TTA], psum[:, :],
                           AF.Identity, bias=bq_f[:, h * 3 + w:h * 3 + w + 1], scale=1.0,
                           inc=act)
                    sF(bname, act, v)
                    if tt == NTA - 1:
                        sR("QT KT VT".split()[w], act, v)

            for r in range(8):
                par = r % 2
                pname, ptp = PTP[par]
                W("tensor", act, bd.R["VT"][1])
                wF("tensor", pname)
                for j in range(8):
                    kc = r * 8 + j
                    v = op("tensor", "transpose", ptp[:, j * 128:(j + 1) * 128],
                           VT[:, kc * 128:(kc + 1) * 128], ident_bf[:, :],
                           inc=(pe if j == 7 else None))
                sR(pname, pe, v)
                W("vector", pe, v)
                v = op("vector", "tensor_copy", out=Vtok[:, r * 1024:(r + 1) * 1024],
                       in_=ptp[:, :], inc=dve)
                sF(pname, dve, v)
                if r == 7:
                    sR("Vtok", dve, v)

            for c in range(NKC):
                par = c % 2
                W("scalar", act, bd.R["QT"][1])
                wF("scalar", f"sq{par}")
                v = op("scalar", "activation", sq_tmp[par][:, :], QT[:, c * 512:(c + 1) * 512],
                       AF.Square, bias=0.0, scale=1.0, inc=act)
                sR(f"sq{par}", act, v)
                W("tensor", act, v)
                wF("tensor", "q2A")
                if c == 0:
                    wF("tensor", "avTA")
                if h == 0 and c == 0:
                    W("tensor", pool, bd.R["consts"][1])
                v = op("tensor", "matmul", Q2P[1][0:1, :], ones_c_bf[:, :], sq_tmp[par][:, :],
                       start=True, stop=True, inc=pe)
                sR("q2A", pe, v)
                sF(f"sq{par}", pe, v)
                W("vector", pe, v)
                wF("vector", "row_buf")
                v = op("vector", "tensor_copy", out=row_buf[:, :], in_=Q2P[1][0:1, :], inc=dve)
                sR("row_buf", dve, v)
                sF("q2A", dve, v)
                W("tensor", dve, v)
                for j in range(4):
                    qt = c * 4 + j
                    v = op("tensor", "transpose", BIP[1][:, qt:qt + 1],
                           row_buf[0:1, j * 128:(j + 1) * 128], ident_f[0:1, 0:1],
                           inc=(pe if j == 3 else None))
                sF("row_buf", pe, v)
            sR("biA", pe, v)
            W("vector", act, bd.R["KT"][1])
            for c in range(NKC):
                if c == 0:
                    op("vector", "reduce_max", out=kmax_col[:, :], in_=KT[:, c * 512:(c + 1) * 512],
                       axis=AX.X, apply_absolute_value=True)
                else:
                    op("vector", "reduce_max", out=km_tmp[:, :], in_=KT[:, c * 512:(c + 1) * 512],
                       axis=AX.X, apply_absolute_value=True)
                    lastv = op("vector", "tensor_tensor", out=kmax_col[:, :], in0=kmax_col[:, :],
                               in1=km_tmp[:, :], op=ALU.max,
                               inc=(dve if c == NKC - 1 else None))
            sR("kmax_col", dve, lastv)
            W("tensor", dve, lastv)
            wF("tensor", "kmA")
            v = op("tensor", "transpose", KMP[1][0:1, 0:128], kmax_col[:, :], ident_f[:, :],
                   inc=pe)
            sR("kmA", pe, v)
            W("vector", pe, v)
            op("vector", "reduce_max", out=km_s[:, :], in_=KMP[1][0:1, 0:128], axis=AX.X)
            v = op("vector", "tensor_tensor", out=km_s[:, :], in0=km_s[:, :], in1=km_s[:, :],
                   op=ALU.mult, inc=dve)
            sR("km2", dve, v)
            sF("kmA", dve, v)
            W("tensor", dve, v)
            wF("tensor", "kmA")
            v = op("tensor", "matmul", KMP[1][:, 0:1], ones_r_f[:, :], km_s[:, :],
                   start=True, stop=True, inc=pe)
            sR("km2b", pe, v)
            W("vector", pe, v)
            v = op("vector", "tensor_copy", out=kmax_col[:, :], in_=KMP[1][:, 0:1], inc=dve)
            sR("k2col", dve, v)
            sF("kmA", dve, v)
            W("vector", pe, bd.R["biA"][1])
            op("vector", "tensor_copy", out=braw[:, :], in_=BIP[1][:, :])
            v = op("vector", "tensor_scalar", out=braw[:, :], in0=braw[:, :],
                   scalar1=kmax_col[:, :], scalar2=1.0 / 128.0, op0=ALU.mult, op1=ALU.mult,
                   inc=dve)
            sF("biA", dve, v)
            W("scalar", dve, v)
            op("scalar", "activation", braw[:, :], braw[:, :], AF.Sqrt, bias=eps12[:, :], scale=1.0)
            v = op("scalar", "activation", bias_cols[:, :], braw[:, :], AF.Copy, bias=0.0,
                   scale=-1.0, inc=act)
            sR("bias_cols", act, v)

            W("tensor", act, bd.R["bias_cols"][1])
            W("tensor", dve, bd.R["Vtok"][1])
            units = []
            for qt in range(NQT):
                npr = ((qt + 1) * 128 + 1023) // 1024
                for kp in range(npr):
                    units.append((qt, kp, npr))
            defer = []

            def emit_trav(qt, kp, npr, par):
                cols = min(1024, (qt + 1) * 128 - kp * 1024)
                kbase = kp * 1024
                is_diag = (kp == npr - 1)
                nj = cols // 128
                pname, ptp = PTP[par]
                W("tensor", act, bd.R[f"p{par}"][1])
                wF("tensor", pname)
                for j in range(nj):
                    v = op("tensor", "transpose", ptp[:, j * 128:(j + 1) * 128],
                           p_sb[par][:, j * 128:(j + 1) * 128], ident_bf[:, :],
                           inc=(pe if j == nj - 1 else None))
                sR(pname, pe, v)
                sF(f"p{par}", pe, v)
                W("vector", pe, v)
                v = op("vector", "tensor_copy", out=pt_sb[par][:, 0:cols],
                       in_=ptp[:, 0:cols], inc=dve)
                sR(f"pt{par}", dve, v)
                sF(pname, dve, v)
                W("tensor", dve, v)
                if kp == 0:
                    wF("tensor", "avA")
                    wF("tensor", "avTA")
                for j in range(nj):
                    v = op("tensor", "matmul", AVP[1][:, :],
                           pt_sb[par][:, j * 128:(j + 1) * 128],
                           Vtok[:, (kbase + j * 128):(kbase + (j + 1) * 128)],
                           start=(kp == 0 and j == 0), stop=(is_diag and j == nj - 1),
                           inc=(pe if (is_diag and j == nj - 1) else None))
                if is_diag:
                    sR("avA", pe, v)
                    W("vector", act, bd.R[f"p{par}"][1])
                    wF("vector", "linv")
                    op("vector", "reduce_sum", out=lsum[:, :],
                       in_=lparts[:, (qt % 2) * 16:(qt % 2) * 16 + npr], axis=AX.X)
                    v = op("vector", "reciprocal", out=linv[:, :], in_=lsum[:, :], inc=dve)
                    sR("linv", dve, v)
                    sF(f"lparts{qt % 2}", dve, v)
                    W("scalar", dve, v)
                    W("scalar", pe, bd.R["avA"][1])
                    wF("scalar", "av_bf")
                    v = op("scalar", "activation", av_bf[:, :], AVP[1][:, :], AF.Identity,
                           bias=0.0, scale=linv[:, :], inc=act)
                    sR("av_bf", act, v)
                    sF("avA", act, v)
                    sF("linv", act, v)

                    def fin(qt=qt):
                        W("tensor", act, bd.R["av_bf"][1])
                        wF("tensor", "avTA")
                        v = op("tensor", "transpose", AVTP[1][:, :], av_bf[:, :],
                               ident_bf[:, :], inc=pe)
                        sR("avTA", pe, v)
                        sF("av_bf", pe, v)
                        W("vector", pe, v)
                        wF("vector", "avT_sb")
                        v2 = op("vector", "tensor_copy",
                                out=avT_sb[:, qt * 128:(qt + 1) * 128],
                                in_=AVTP[1][:, :], inc=dve)
                        sF("avTA", dve, v2)
                        if qt == NQT - 1:
                            sR("avT_done", dve, v2)
                    defer.append(fin)

            prev = None
            for ui, (qt, kp, npr) in enumerate(units):
                par = ui % 2
                scname, scp = SC[par]
                cols = min(1024, (qt + 1) * 128 - kp * 1024)
                kbase = kp * 1024
                is_diag = (kp == npr - 1)
                wF("tensor", scname)
                nm = (cols + 511) // 512
                for j in range(nm):
                    n = min(512, cols - j * 512)
                    v = op("tensor", "matmul", scp[:, j * 512:j * 512 + n],
                           QT[:, qt * 128:(qt + 1) * 128],
                           KT[:, kbase + j * 512:kbase + j * 512 + n],
                           start=True, stop=True, inc=(pe if j == nm - 1 else None))
                sR(scname, pe, v)
                if is_diag:
                    W("vector", pe, v)
                    v = op("vector", "tensor_tensor", out=scp[:, cols - 128:cols],
                           in0=scp[:, cols - 128:cols], in1=cmask[:, :], op=ALU.add, inc=dve)
                    sR(scname + "m", dve, v)
                W("scalar", pe, bd.R[scname][1])
                if is_diag:
                    W("scalar", dve, bd.R[scname + "m"][1])
                wF("scalar", f"p{par}")
                if kp == 0:
                    wF("scalar", f"lparts{qt % 2}")
                v = op("scalar", "activation", p_sb[par][:, 0:cols], scp[:, 0:cols],
                       AF.Exp, bias=bias_cols[:, qt:qt + 1], scale=INV_SQRT_HD,
                       accum_out=lparts[:, (qt % 2) * 16 + kp:(qt % 2) * 16 + kp + 1],
                       inc=act)
                sR(f"p{par}", act, v)
                sF(scname, act, v)
                fins, defer[:] = defer[:], []
                for f in fins:
                    f()
                if prev is not None:
                    emit_trav(*prev)
                prev = (qt, kp, npr, par)
            fins, defer[:] = defer[:], []
            for f in fins:
                f()
            emit_trav(*prev)
            for f in defer:
                f()
            W("gpsimd", dve, bd.R["avT_done"][1])
            v = bd.dma("gpsimd", dgp, out=ag_in_h[h][:, :], in_=avT_sb[:, :])
            sF("avT_sb", dgp, v)
            W("gpsimd", dgp, v)
            v = op("gpsimd", "collective_compute", "AllGather", ALU.bypass,
                   replica_groups=[[0, 1, 2, 3], [4, 5, 6, 7]],
                   ins=[ag_in_h[h].ap().opt()], outs=[ag_out_h[h].ap().opt()], inc=cc)
        sR("ag", cc, cc.n)

        PE_END_A = pe.n

        POP = [("po0", pbank(0)), ("po1", pbank(1))]

        W("sync", pe, PE_END_A)
        v = bd.dma("sync", dsy, out=wout_sb[:, :], in_=wout[:, :])
        sR("wout", dsy, v)

        for tt in range(4):
            def agdma(e, tt=tt, nprev=None):
                with e.register(f"g2_{tt}") as g2:
                    for ah in range(HPC):
                        e.wait_ge(dsy.h, nprev + ah * 16)
                        e.reg_load(g2, coff_sb[:1, :1])
                        e.reg_add(g2, g2, tt * 512)
                        e.dma_start(
                            out=attT[:, ah * 2048:(ah + 1) * 2048].rearrange(
                                "p (r t) -> p r t", t=512),
                            in_=bass.AP(ag_out_h[ah], g2, [[S, 128], [128 * S, 4], [1, 512]]),
                        ).then_inc(dsy.h, 16)
            W("sync", cc, bd.R["ag"][1])
            W("sync", dgp, bd.R["smallin"][1])
            wF("sync", "attT")
            _np = dsy.n
            bd.raw("sync", lambda e, f=agdma, n=_np: f(e, nprev=n))
            vA = dsy.inc(16 * HPC)
            sR("attT", dsy, vA)
            wF("sync", "xr_t")
            v = bd.dma("sync", dsy, out=xr_t[:, :],
                   in_=bass.AP(xr, tt * 512, [[NKC * TC, 128], [TC, NKC], [1, 512]]))
            sR("xr_t", dsy, v)
            W("tensor", dsy, bd.R["attT"][1])
            if tt == 0:
                W("tensor", dsy, bd.R["wout"][1])
            for m in range(NKC):
                par = m % 2
                bname, psum = POP[par]
                wF("tensor", bname)
                for kc in range(NKC):
                    kcg = (kc % 4) * 4 + kc // 4
                    v = op("tensor", "matmul", psum[:, :],
                           wout_sb[:, kcg * D + m * 128:kcg * D + m * 128 + 128],
                           attT[:, kc * 512:(kc + 1) * 512],
                           start=(kc == 0), stop=(kc == NKC - 1),
                           inc=(pe if kc == NKC - 1 else None))
                sR(bname, pe, v)
                if m == NKC - 1:
                    sF("attT", pe, v)
                W("vector", pe, v)
                W("vector", dsy, bd.R["xr_t"][1])
                v = op("vector", "scalar_tensor_tensor",
                       out=r1[:, m * TC + tt * 512:m * TC + tt * 512 + 512],
                       in0=psum[:, :], scalar=bo_f[:, m:m + 1],
                       in1=xr_t[:, m * 512:(m + 1) * 512],
                       op0=ALU.add, op1=ALU.add, inc=dve)
                sF(bname, dve, v)
                if m == NKC - 1:
                    sF("xr_t", dve, v)
                    if tt == 3:
                        sR("r1", dve, v)

        SSUM = ("ssum", pbank(4)[0:1, :])
        SSQ = ("ssq", pbank(5)[0:1, :])
        BCP = ("bcp", pbank(6))
        W("scalar", dve, bd.R["r1"][1])
        for tt in range(4):
            for kc in range(NKC):
                par = kc % 2
                W("tensor", dve, bd.R["r1"][1])
                if kc == 0:
                    wF("tensor", "ssum")
                    wF("tensor", "ssq")
                v = op("tensor", "matmul", SSUM[1][:, :], ones_c_bf[:, :],
                       r1[:, kc * TC + tt * 512:kc * TC + tt * 512 + 512],
                       start=(kc == 0), stop=(kc == NKC - 1),
                       inc=(pe if kc == NKC - 1 else None))
                wF("scalar", f"sqB{par}")
                v2 = op("scalar", "activation", sq_tmp[par][:, :],
                        r1[:, kc * TC + tt * 512:kc * TC + tt * 512 + 512],
                        AF.Square, bias=0.0, scale=1.0, inc=act)
                sR(f"sqB{par}", act, v2)
                W("tensor", act, v2)
                v = op("tensor", "matmul", SSQ[1][:, :], ones_c_bf[:, :], sq_tmp[par][:, :],
                       start=(kc == 0), stop=(kc == NKC - 1), inc=pe)
                sF(f"sqB{par}", pe, v)
            sR("stats", pe, v)
            W("vector", pe, bd.R["stats"][1])
            op("vector", "tensor_copy", out=srow[0:1, tt * 512:(tt + 1) * 512], in_=SSUM[1][:, :])
            v = op("vector", "tensor_copy", out=sqrow[0:1, tt * 512:(tt + 1) * 512],
                   in_=SSQ[1][:, :], inc=dve)
            sR("statrows", dve, v)
            sF("ssum", dve, v)
            sF("ssq", dve, v)
        op("vector", "tensor_scalar", out=mean_r[:, :], in0=srow[:, :],
           scalar1=1.0 / D, scalar2=None, op0=ALU.mult)
        op("vector", "tensor_tensor", out=mrstd_r[:, :], in0=mean_r[:, :], in1=mean_r[:, :],
           op=ALU.mult)
        v = op("vector", "scalar_tensor_tensor", out=rstd_r[:, :], in0=sqrow[:, :],
               scalar=1.0 / D, in1=mrstd_r[:, :], op0=ALU.mult, op1=ALU.subtract, inc=dve)
        W("scalar", dve, v)
        op("scalar", "activation", rstd_r[:, :], rstd_r[:, :], AF.Ln, bias=eps5[0:1, :], scale=1.0)
        v = op("scalar", "activation", rstd_r[:, :], rstd_r[:, :], AF.Exp, bias=0.0,
               scale=-0.5, inc=act)
        sR("rstd_r", act, v)
        W("vector", act, v)
        v = op("vector", "tensor_tensor", out=mrstd_r[:, :], in0=mean_r[:, :], in1=rstd_r[:, :],
               op=ALU.mult, inc=dve)
        sR("mrstd_r", dve, v)
        for tt in range(4):
            W("tensor", dve, bd.R["mrstd_r"][1])
            wF("tensor", "bcp")
            v = op("tensor", "matmul", BCP[1][:, :], ones_r_f[:, :],
                   rstd_r[0:1, tt * 512:(tt + 1) * 512], start=True, stop=True, inc=pe)
            W("vector", pe, v)
            v = op("vector", "tensor_copy", out=rstd_b[:, tt * 512:(tt + 1) * 512],
                   in_=BCP[1][:, :], inc=dve)
            sF("bcp", dve, v)
            wF("tensor", "bcp")
            v = op("tensor", "matmul", BCP[1][:, :], ones_r_f[:, :],
                   mrstd_r[0:1, tt * 512:(tt + 1) * 512], start=True, stop=True, inc=pe)
            W("vector", pe, v)
            v = op("vector", "tensor_copy", out=mrstd_b[:, tt * 512:(tt + 1) * 512],
                   in_=BCP[1][:, :], inc=dve)
            sF("bcp", dve, v)
            if tt == 3:
                sR("bcast1", dve, v)

        LGP = [(f"lg{tt}", pbank(tt)[0:8, :]) for tt in range(4)]
        for tt in range(4):
            for kc in range(NKC):
                par = kc % 2
                W("vector", dve, bd.R["bcast1"][1])
                op("vector", "tensor_tensor", out=lnt[:, :],
                   in0=r1[:, kc * TC + tt * 512:kc * TC + tt * 512 + 512],
                   in1=rstd_b[:, tt * 512:(tt + 1) * 512], op=ALU.mult)
                wF("vector", f"x1s{par}")
                wF("vector", f"x1sD{par}")
                v = op("vector", "tensor_tensor", out=x1s[par][:, :], in0=lnt[:, :],
                       in1=mrstd_b[:, tt * 512:(tt + 1) * 512], op=ALU.subtract, inc=dve)
                sR(f"x1s{par}", dve, v)
                W("scalar", dve, v)
                v = bd.dma("scalar", dsc,
                       out=bass.AP(x1d, kc * 128 * TC + tt * 512, [[TC, 128], [1, 512]]),
                       in_=x1s[par][:, :])
                sF(f"x1sD{par}", dsc, v)
                vx1 = v
                W("tensor", dve, bd.R[f"x1s{par}"][1])
                if tt == 0 and kc == 0:
                    W("tensor", pool, bd.R["wg_lo"][1])
                    wF("tensor", f"lgb{tt}")
                op("tensor", "matmul", LGP[tt][1][:, :], wg_hi[:, kc * 8:(kc + 1) * 8],
                   x1s[par][:, :], start=(kc == 0), stop=False)
                v = op("tensor", "matmul", LGP[tt][1][:, :], wg_lo[:, kc * 8:(kc + 1) * 8],
                       x1s[par][:, :], start=False, stop=(kc == NKC - 1), inc=pe)
                sF(f"x1s{par}", pe, v)
                if kc == NKC - 1:
                    sR(LGP[tt][0], pe, v)
        sR("x1d_all", dsc, vx1)

        TRP = ("trp", pbank(4))
        CMP_ = ("cmp2", pa[0:8, 0:2048])
        for tt in range(4):
            W("vector", pe, bd.R[LGP[tt][0]][1])
            v = op("vector", "tensor_copy", out=lg_sb[:, tt * 512:(tt + 1) * 512],
                   in_=LGP[tt][1][:, :], inc=dve)
            if tt == 3:
                sR("lg", dve, v)
        W("tensor", dve, bd.R["lg"][1])
        wF("tensor", "trp")
        for ti in range(NKC):
            v = op("tensor", "transpose", TRP[1][:, ti * 8:(ti + 1) * 8],
                   lg_sb[0:8, ti * 128:(ti + 1) * 128], ident_f[0:8, 0:8],
                   inc=(pe if ti == NKC - 1 else None))
        sR("trp", pe, v)
        W("vector", pe, v)
        v = op("vector", "tensor_copy", out=ltok[:, :], in_=TRP[1][:, 0:NKC * 8], inc=dve)
        sF("trp", dve, v)
        for ti in range(NKC):
            lt = ltok[:, ti * 8:(ti + 1) * 8]
            op("vector", "reduce_max", out=m1c[:, :], in_=lt, axis=AX.X)
            op("vector", "reduce_max", out=nm1c[:, :], in_=lt, axis=AX.X, negate=True)
            op("vector", "tensor_scalar", out=msk8[:, :], in0=lt, scalar1=m1c[:, :],
               scalar2=None, op0=ALU.is_ge)
            op("vector", "scalar_tensor_tensor", out=msk8[:, :], in0=msk8[:, :],
               scalar=-1e30, in1=lt, op0=ALU.mult, op1=ALU.add)
            v = op("vector", "reduce_max", out=m2c[:, :], in_=msk8[:, :], axis=AX.X, inc=dve)
            W("scalar", dve, v)
            op("scalar", "activation", e1t[:, :], lt, AF.Exp, bias=nm1c[:, :], scale=1.0)
            v = op("scalar", "activation", e2c[:, :], m2c[:, :], AF.Exp, bias=nm1c[:, :],
                   scale=1.0, inc=act)
            W("vector", act, v)
            op("vector", "tensor_scalar", out=e2c[:, :], in0=e2c[:, :], scalar1=1.0,
               scalar2=None, op0=ALU.add)
            op("vector", "reciprocal", out=dinvc[:, :], in_=e2c[:, :])
            op("vector", "tensor_scalar", out=msk8[:, :], in0=lt, scalar1=m2c[:, :],
               scalar2=None, op0=ALU.is_ge)
            v = op("vector", "scalar_tensor_tensor", out=ltok[:, ti * 8:(ti + 1) * 8],
                   in0=e1t[:, :], scalar=dinvc[:, :], in1=msk8[:, :],
                   op0=ALU.mult, op1=ALU.mult, inc=dve)
            sR(f"cmb{ti}", dve, v)
        sR("cmb_all", dve, bd.R[f"cmb{NKC - 1}"][1])

        if "dbg_x1" in dbg:
            W("gpsimd", dsc, bd.R["x1d_all"][1])
            bd.dma("gpsimd", dgp, out=dbg["dbg_x1"][:, :, :], in_=x1d[:, :, :])

        TRP2 = ("trp2", pbank(7)[:, 0:8])
        TMJP = ("tmjp", pbank(4, 3).bitcast(BF16)[:, 0:2176])
        W("gpsimd", dve, bd.R["cmb_all"][1])
        op("gpsimd", "memset", LT_bf[:, :], 0.0)
        op("gpsimd", "affine_select", out=LT_bf[:, :], in_=LT_bf[:, :],
           compare_op=ALU.is_ge, fill=1.0, base=0, pattern=[[-1, 128]], channel_multiplier=1)
        v = op("gpsimd", "memset", ONES_bf[:, :], 1.0, inc=pool)
        sR("s1consts", pool, v)
        W("vector", dve, bd.R["cmb_all"][1])
        op("vector", "tensor_scalar", out=mask_f[:, :], in0=ltok[:, :], scalar1=0.0,
           scalar2=None, op0=ALU.is_gt)
        v = op("vector", "tensor_copy", out=mask_b[:, :], in_=mask_f[:, :], inc=dve)
        sR("maskb", dve, v)
        for ti in range(NKC):
            W("tensor", dve, bd.R["maskb"][1])
            W("tensor", pool, bd.R["s1consts"][1])
            wF("tensor", "trp2")
            nmm = 0
            for tj in range(ti + 1):
                nmm += 1
                v = op("tensor", "matmul", TRP2[1][:, :],
                       LT_bf[:, :] if tj == ti else ONES_bf[:, :],
                       mask_b[:, tj * 8:(tj + 1) * 8],
                       start=(nmm == 1), stop=(nmm == ti + 1),
                       inc=(pe if nmm == ti + 1 else None))
            sR("trp2R", pe, v)
            W("vector", pe, v)
            op("vector", "scalar_tensor_tensor", out=cmb_sb[:, 0:8],
               in0=mask_f[:, ti * 8:(ti + 1) * 8], scalar=-4096.0, in1=TRP2[1][:, :],
               op0=ALU.mult, op1=ALU.add)
            op("vector", "tensor_scalar", out=cmb_sb[:, 0:8], in0=cmb_sb[:, 0:8],
               scalar1=4096.0, scalar2=None, op0=ALU.add)
            v = op("vector", "tensor_copy", out=ridx[:, ti * 8:(ti + 1) * 8],
                   in_=cmb_sb[:, 0:8], inc=dve)
            sF("trp2", dve, v)
            if ti == NKC - 1:
                sR("ridx", dve, v)
        W("vector", dve, bd.R["cmb_all"][1])
        v = op("vector", "memset", tmj_sb[:, :], 0.0, inc=dve)
        sR("tmjpad", dve, v)
        W("gpsimd", dve, v)
        for e in range(E):
            for st in range(8):
                bd.dma("gpsimd", dgp, out=xg_d[e][st * 128:(st + 1) * 128, :],
                       in_=tmj_sb[:, :])
        sF("tmj_sb", dgp, dgp.n)
        for ti in range(NKC):
            W("sync", dsc, bd.R["x1d_all"][1])
            wF("sync", "fm_sb")
            v = bd.dma("sync", dsy, out=fm_sb[:, :].rearrange("p (k t) -> p k t", t=128),
                       in_=bass.AP(x1d, ti * 128, [[TC, 128], [128 * TC, NKC], [1, 128]]))
            sR("fm_sb", dsy, v)
            W("tensor", dsy, v)
            wF("tensor", "tmjp")
            for kc in range(NKC):
                v = op("tensor", "transpose", TMJP[1][:, kc * 128:(kc + 1) * 128],
                       fm_sb[:, kc * 128:(kc + 1) * 128], ident_bf[:, :],
                       inc=(pe if kc == NKC - 1 else None))
            sR("tmjpR", pe, v)
            sF("fm_sb", pe, v)
            W("vector", pe, v)
            wF("vector", "tmj_sb")
            op("vector", "tensor_copy", out=tmj_sb[:, 0:2048], in_=TMJP[1][:, 0:2048])
            op("vector", "tensor_copy",
               out=bass.AP(tmj_sb, 2048, [[2176, 128], [2, 8]]),
               in_=ltok[:, ti * 8:(ti + 1) * 8])
            v = op("vector", "tensor_tensor",
                   out=bass.AP(tmj_sb, 2049, [[2176, 128], [2, 8]]),
                   in0=ltok[:, ti * 8:(ti + 1) * 8],
                   in1=bass.AP(tmj_sb, 2048, [[2176, 128], [2, 8]]), op=ALU.subtract,
                   inc=dve)
            sR("tmjR", dve, v)
            sF("tmjp", dve, v)
            W("gpsimd", dve, v)
            W("gpsimd", dve, bd.R["ridx"][1])
            v = bd.dma("gpsimd", dgp, out=x1t[ti * 128:(ti + 1) * 128, :], in_=tmj_sb[:, :])
            for e in range(E):
                wv = dgp.inc(16)
                bd.plan["gpsimd"].append(
                    lambda eng, e=e, ti=ti, n=dgp.n - 16: (
                        eng.wait_ge(dgp.h, n),
                        eng.indirect_dma_start(
                            out=xg_d[e][:, :],
                            out_offset=bass.IndirectOffsetOnAxis(
                                ap=ridx[:, ti * 8 + e:ti * 8 + e + 1], axis=0),
                            in_=tmj_sb[:, :], in_offset=None,
                            bounds_check=_H.rb, oob_is_err=False,
                        ).then_inc(dgp.h, 16)))
            v = dgp.n
            sF("tmj_sb", dgp, v)
            if ti == NKC - 1:
                sR("scat_done", dgp, v)

        GUP = [(f"gu{i}", pbank(i)) for i in range(4)]
        CMBP = ("cmbp", pbank(7)[:, 0:512])
        gu_rot = [0]
        nstage = [0]
        nslice = [0]
        QWQ = [("sync", dsy), ("gpsimd", dgp)]

        def stream_weight(src_halves):
            par = nslice[0] % 2
            nslice[0] += 1
            for half, hap in enumerate(src_halves):
                s3 = nstage[0] % 3
                qeng, qsem = QWQ[nstage[0] % 2]
                nstage[0] += 1
                wF(qeng, f"wst{s3}")
                v = bd.dma(qeng, qsem, out=wstage[s3][:, :], in_=hap)
                sR(f"wst{s3}", qsem, v)
                bd.wait("vector", qsem, v)
                if half == 0:
                    wF("vector", f"whi{par}")
                    wF("vector", f"wlo{par}")
                op("vector", "tensor_copy", out=w_hi[par][:, half * 1024:(half + 1) * 1024],
                   in_=wstage[s3][:, :], inc=dve)
                vlo = op("vector", "tensor_tensor",
                         out=w_lo[par][:, half * 1024:(half + 1) * 1024],
                         in0=wstage[s3][:, :],
                         in1=w_hi[par][:, half * 1024:(half + 1) * 1024],
                         op=ALU.subtract, inc=dve)
                sF(f"wst{s3}", dve, vlo)
            return (par, vlo)

        def gate_up_part(wm, psrc_is_g, jp, xsrc, with_cmb):
            par_w, vlo = wm
            for tts in range(2):
                rot = gu_rot[0] % 4
                gu_rot[0] += 1
                bname, psum = GUP[rot]
                wF("tensor", bname)
                W("tensor", dve, vlo)
                nmm = 0
                for kc in range(NKC):
                    acol = kc * 1024 + tts * 512
                    for wt in (w_hi[par_w], w_lo[par_w]):
                        nmm += 1
                        v = op("tensor", "matmul", psum[:, :],
                               wt[:, kc * 128:(kc + 1) * 128], xsrc[:, acol:acol + 512],
                               start=(nmm == 1), stop=(nmm == NKC * 2),
                               inc=(pe if nmm == NKC * 2 else None))
                sR(bname, pe, v)
                if tts == 1:
                    sF(f"whi{par_w}", pe, v)
                    sF(f"wlo{par_w}", pe, v)
                if psrc_is_g:
                    W("scalar", pe, v)
                    wF("scalar", f"silu{tts}")
                    v = op("scalar", "activation", silu_g[:, tts * 512:(tts + 1) * 512],
                           psum[:, :], AF.Silu, bias=0.0, scale=1.0, inc=act)
                    sR(f"siluR{tts}", act, v)
                    sF(bname, act, v)
                else:
                    W("vector", pe, v)
                    W("vector", act, bd.R[f"siluR{tts}"][1])
                    if with_cmb:
                        W("vector", dve, bd.R["cmbR"][1])
                        op("vector", "tensor_tensor", out=act_f[:, tts * 512:(tts + 1) * 512],
                           in0=psum[:, :], in1=cmb_sb[:, tts * 512:(tts + 1) * 512],
                           op=ALU.mult)
                        v = op("vector", "tensor_tensor",
                               out=act_f[:, tts * 512:(tts + 1) * 512],
                               in0=act_f[:, tts * 512:(tts + 1) * 512],
                               in1=silu_g[:, tts * 512:(tts + 1) * 512], op=ALU.mult, inc=dve)
                    else:
                        v = op("vector", "tensor_tensor",
                               out=act_f[:, tts * 512:(tts + 1) * 512], in0=psum[:, :],
                               in1=silu_g[:, tts * 512:(tts + 1) * 512], op=ALU.mult, inc=dve)
                    sF(bname, dve, v)
                    sF(f"silu{tts}", dve, v)
                    acol = jp * 1024 + tts * 512
                    wF("vector", "act_hi")
                    v = op("vector", "tensor_copy", out=act_hi[:, acol:acol + 512],
                           in_=act_f[:, tts * 512:(tts + 1) * 512], inc=dve)
                    sR(f"ahi{jp}_{tts}", dve, v)
                    W("gpsimd", dve, v)
                    wF("gpsimd", "act_lo")
                    v = op("gpsimd", "tensor_tensor", out=act_lo[:, acol:acol + 512],
                           in0=act_f[:, tts * 512:(tts + 1) * 512],
                           in1=act_hi[:, acol:acol + 512], op=ALU.subtract, inc=pool)
                    sR(f"alo{jp}_{tts}", pool, v)

        def down_tok(wm, dslc, dst, drow0, accum):
            par, vlo = wm
            for st in range(8):
                rot = gu_rot[0] % 4
                gu_rot[0] += 1
                bname, psum = GUP[rot]
                wF("tensor", bname)
                W("tensor", dve, vlo)
                nmm = 0
                for ic in range(8):
                    scol = ic * 1024 + st * 128
                    wcol = ic * 512
                    for at, wt in ((act_hi, w_hi[par]), (act_lo, w_hi[par]),
                                   (act_hi, w_lo[par])):
                        nmm += 1
                        v = op("tensor", "matmul", psum[:, :],
                               at[:, scol:scol + 128], wt[:, wcol:wcol + 512],
                               start=(nmm == 1), stop=(nmm == 24),
                               inc=(pe if nmm == 24 else None))
                sR(bname, pe, v)
                if st == 7:
                    sF(f"whi{par}", pe, v)
                    sF(f"wlo{par}", pe, v)
                W("vector", pe, v)
                epar = st % 2
                wF("vector", f"est{epar}")
                v = op("vector", "tensor_copy", out=est[epar][:, :], in_=psum[:, :], inc=dve)
                sF(bname, dve, v)
                if accum:
                    W("gpsimd", dve, v)
                    if st == 0:
                        W("gpsimd", dsc, dsc.n)
                    v = bd.dma("gpsimd", dgp,
                               out=dst[drow0 + st * 128:drow0 + (st + 1) * 128,
                                       dslc * 512:(dslc + 1) * 512],
                               in_=est[epar][:, :], accum_op=ALU.add)
                    sF(f"est{epar}", dgp, v)
                else:
                    W("scalar", dve, v)
                    v = bd.dma("scalar", dsc,
                               out=dst[drow0 + st * 128:drow0 + (st + 1) * 128,
                                       dslc * 512:(dslc + 1) * 512],
                               in_=est[epar][:, :])
                    sF(f"est{epar}", dsc, v)

        for half in range(SW):
            wF("sync", "x1_sw")
            W("sync", dsc, bd.R["x1d_all"][1])
            W("sync", dve, bd.R["ridx"][1])
            W("sync", dve, bd.R["tmjR"][1])
            if half > 0:
                W("sync", pe, pe.n)
            v = bd.dma("sync", dsy,
                       out=x1_sw[:, :].rearrange("p (k t) -> p k t", t=TSW),
                       in_=bass.AP(x1d, half * TSW, [[TC, 128], [128 * TC, NKC], [1, TSW]]))
            sR("x1_sw", dsy, v)
            W("tensor", dsy, v)
            for ig in range(4):
                for jp in range(8):
                    ms = ig * 8 + jp
                    wmg = stream_weight([wsgu[ms, :, 0:1024], wsgu[ms, :, 1024:2048]])
                    gate_up_part(wmg, True, jp, x1_sw, False)
                    ms = 32 + ig * 8 + jp
                    wmu = stream_weight([wsgu[ms, :, 0:1024], wsgu[ms, :, 1024:2048]])
                    gate_up_part(wmu, False, jp, x1_sw, False)
                for jp in range(8):
                    for tts in range(2):
                        W("tensor", dve, bd.R[f"ahi{jp}_{tts}"][1])
                        W("tensor", pool, bd.R[f"alo{jp}_{tts}"][1])
                for dslc in range(4):
                    base = ((ig * 4 + dslc) * 8) * 128 * 512
                    wmd = stream_weight([
                        bass.AP(wsdn, base + h * 2 * 65536,
                                [[512, 128], [65536, 2], [1, 512]])
                        for h in range(4)])
                    down_tok(wmd, dslc, facc_d, half * TSW, accum=(ig > 0))
                sF("act_hi", pe, pe.n)
                sF("act_lo", pe, pe.n)
                sF("silu0", pe, pe.n)
                sF("silu1", pe, pe.n)
            if half == SW - 1:
                sF("x1_sw_done", dve, dve.n)

        for e in range(E):
            W("sync", dgp, bd.R["scat_done"][1])
            for st in range(8):
                wF("sync", "tmj_sb")
                v = bd.dma("sync", dsy, out=tmj_sb[:, :],
                           in_=xg_d[e][st * 128:(st + 1) * 128, :])
                sR("tmjL", dsy, v)
                W("tensor", dsy, v)
                wF("tensor", "tmjp")
                for ch in range(17):
                    v = op("tensor", "transpose", TMJP[1][:, ch * 128:(ch + 1) * 128],
                           tmj_sb[:, ch * 128:(ch + 1) * 128], ident_bf[:, :],
                           inc=(pe if ch == 16 else None))
                sR("tmjpR", pe, v)
                sF("tmj_sb", pe, v)
                W("vector", pe, v)
                wF("vector", f"xgcol{st}")
                v = op("vector", "tensor_copy",
                       out=bass.AP(xg_sb, st * 128, [[17 * 1024, 128], [1024, 17], [1, 128]]),
                       in_=TMJP[1][:, :].rearrange("p (c t) -> p c t", t=128), inc=dve)
                sR(f"xgR{st}", dve, v)
                sF("tmjp", dve, v)
            W("gpsimd", dve, bd.R[f"xgR7"][1])
            wF("gpsimd", "cslot")
            v = bd.dma("gpsimd", dgp, out=cslot[:, :],
                       in_=xg_sb[2 * e:2 * e + 2, 16 * 1024:17 * 1024])
            sR("cslotR", dgp, v)
            for tts in range(2):
                W("tensor", dgp, bd.R["cslotR"][1])
                W("tensor", pool, bd.R["s1consts"][1])
                wF("tensor", "cmbp")
                v = op("tensor", "matmul", CMBP[1][:, :], ONES_bf[0:2, 0:128],
                       cslot[:, tts * 512:(tts + 1) * 512], start=True, stop=True, inc=pe)
                sR("cmbpR", pe, v)
                W("vector", pe, v)
                wF("vector", "cmb_sb")
                v = op("vector", "tensor_copy", out=cmb_sb[:, tts * 512:(tts + 1) * 512],
                       in_=CMBP[1][:, :], inc=dve)
                sF("cmbp", dve, v)
                if tts == 1:
                    sR("cmbR", dve, v)
                    sF("cslot", dve, v)
            for jp in range(8):
                wmg = stream_weight([wgu[e, jp, :, 0:1024], wgu[e, jp, :, 1024:2048]])
                gate_up_part(wmg, True, jp, xg_sb, True)
                wmu = stream_weight([wgu[e, jp + 8, :, 0:1024], wgu[e, jp + 8, :, 1024:2048]])
                gate_up_part(wmu, False, jp, xg_sb, True)
            for jp in range(8):
                for tts in range(2):
                    W("tensor", dve, bd.R[f"ahi{jp}_{tts}"][1])
                    W("tensor", pool, bd.R[f"alo{jp}_{tts}"][1])
            for dslc in range(4):
                base = ((e * 4 + dslc) * 8) * 128 * 512
                wmd = stream_weight([
                    bass.AP(wdn, base + h * 2 * 65536,
                            [[512, 128], [65536, 2], [1, 512]])
                    for h in range(4)])
                down_tok(wmd, dslc, eo_d[e], 0, accum=False)
            sF("act_hi", pe, pe.n)
            sF("act_lo", pe, pe.n)
            sF("silu0", pe, pe.n)
            sF("silu1", pe, pe.n)
            sF("cmb_sb", pe, pe.n)
            for st in range(8):
                sF(f"xgcol{st}", pe, pe.n)

        EO_DONE = dsc.n

        for ti in range(NKC):
            par = ti % 2
            wF("gpsimd", f"facc{par}")
            W("gpsimd", dsc, EO_DONE)
            v = bd.dma("gpsimd", dgp, out=facc_t[par][:, :],
                       in_=facc_d[ti * 128:(ti + 1) * 128, :])
            for e in range(E):
                bd.plan["gpsimd"].append(
                    lambda eng, e=e, ti=ti, par=par, n=dgp.n: (
                        eng.wait_ge(dgp.h, n),
                        eng.indirect_dma_start(
                            out=facc_t[par][:, :], out_offset=None,
                            in_=eo_d[e][:, :],
                            in_offset=bass.IndirectOffsetOnAxis(
                                ap=ridx[:, ti * 8 + e:ti * 8 + e + 1], axis=0),
                            bounds_check=_H.rb, oob_is_err=False,
                            compute_op=ALU.add,
                        ).then_inc(dgp.h, 16)))
                dgp.inc(16)
            vfa = dgp.n
            sR(f"faccR{par}", dgp, vfa)
            wF("sync", "x1row")
            v = bd.dma("sync", dsy, out=x1row[:, :], in_=x1t[ti * 128:(ti + 1) * 128, 0:2048])
            sR("x1rowR", dsy, v)
            W("vector", dgp, vfa)
            W("vector", dsy, v)
            v = op("vector", "tensor_tensor", out=facc_t[par][:, :], in0=x1row[:, :],
                   in1=facc_t[par][:, :], op=ALU.add, inc=dve)
            sR(f"r2R{par}", dve, v)
            sF("x1row", dve, v)
            op("vector", "reduce_sum", out=s1c[:, :], in_=facc_t[par][:, :], axis=AX.X)
            W("scalar", dve, v)
            wF("scalar", "sq_t")
            v2 = op("scalar", "activation", sq_t[:, :], facc_t[par][:, :], AF.Square,
                    bias=0.0, scale=1.0, inc=act)
            W("vector", act, v2)
            v3 = op("vector", "reduce_sum", out=s2c[:, :], in_=sq_t[:, :], axis=AX.X, inc=dve)
            sF("sq_t", dve, v3)
            op("vector", "tensor_scalar", out=mean_c[:, :], in0=s1c[:, :],
               scalar1=1.0 / D, scalar2=None, op0=ALU.mult)
            op("vector", "tensor_tensor", out=rstd_c[:, :], in0=mean_c[:, :],
               in1=mean_c[:, :], op=ALU.mult)
            v = op("vector", "scalar_tensor_tensor", out=rstd_c[:, :], in0=s2c[:, :],
                   scalar=1.0 / D, in1=rstd_c[:, :], op0=ALU.mult, op1=ALU.subtract,
                   inc=dve)
            W("scalar", dve, v)
            op("scalar", "activation", rstd_c[:, :], rstd_c[:, :], AF.Ln,
               bias=eps5[:, :], scale=1.0)
            v = op("scalar", "activation", rstd_c[:, :], rstd_c[:, :], AF.Exp, bias=0.0,
                   scale=-0.5, inc=act)
            W("vector", act, v)
            v = op("vector", "tensor_scalar", out=facc_t[par][:, :], in0=facc_t[par][:, :],
                   scalar1=mean_c[:, :], scalar2=rstd_c[:, :],
                   op0=ALU.subtract, op1=ALU.mult, inc=dve)
            sR(f"outR{par}", dve, v)
            W("scalar", dve, v)
            v = bd.dma("scalar", dsc, out=out_d[ti * 128:(ti + 1) * 128, :],
                       in_=facc_t[par][:, :])
            sF(f"facc{par}", dsc, v)

        W("scalar", dsc, dsc.n)
        W("sync", dsy, dsy.n)
        W("gpsimd", dgp, dgp.n)

        with nc.Block() as block:
            @block.tensor
            def _(eng):
                for f in bd.plan["tensor"]:
                    f(eng)

            @block.vector
            def _(eng):
                for f in bd.plan["vector"]:
                    f(eng)

            @block.scalar
            def _(eng):
                for f in bd.plan["scalar"]:
                    f(eng)

            @block.gpsimd
            def _(eng):
                for f in bd.plan["gpsimd"]:
                    f(eng)

            @block.sync
            def _(eng):
                for f in bd.plan["sync"]:
                    f(eng)

    return nc



def _layout_inputs(inputs):
    x = np.asarray(inputs["L0_attn_x"])
    Wqkv = np.asarray(inputs["L0_attn_W_qkv"])
    bqkv = np.asarray(inputs["L0_attn_b_qkv"])
    Wout = np.asarray(inputs["L0_attn_W_out"])
    bout = np.asarray(inputs["L0_attn_b_out"])
    wg = np.asarray(inputs["L0_ffn_w_gate"])
    wgu = np.asarray(inputs["L0_ffn_w_gate_up"])
    wdn = np.asarray(inputs["L0_ffn_w_down"])
    wsgu = np.asarray(inputs["L0_ffn_w_sh_gate_up"])
    wsdn = np.asarray(inputs["L0_ffn_w_sh_down"])

    wout_h = np.ascontiguousarray(
        Wout.reshape(16, 128, 2048).transpose(1, 0, 2).reshape(128, 16 * 2048))
    bout_h = np.ascontiguousarray(bout.reshape(16, 128).T)
    wg_h = np.ascontiguousarray(
        wg.reshape(16, 128, 8).transpose(1, 0, 2).reshape(128, 128))
    wgu_h = np.ascontiguousarray(
        wgu.reshape(8, 16, 128, 16, 128).transpose(0, 3, 2, 1, 4).reshape(8, 16, 128, 2048))
    wdn_h = np.ascontiguousarray(
        wdn.reshape(8, 8, 128, 4, 512).transpose(0, 3, 1, 2, 4))
    wsgu_h = np.ascontiguousarray(
        wsgu.reshape(16, 128, 64, 128).transpose(2, 1, 0, 3).reshape(64, 128, 2048))
    wsdn_h = np.ascontiguousarray(
        wsdn.reshape(4, 8, 128, 4, 512).transpose(0, 3, 1, 2, 4))

    Wq5 = Wqkv.reshape(16, 128, 3, 16, 128)
    bq3 = bqkv.reshape(3, 16, 128)

    in_maps = []
    for c in range(8):
        b, hg = c // 4, c % 4
        s = c % 4
        xa_h = np.ascontiguousarray(
            x[b].reshape(NTA, TTA, 16, 128).transpose(0, 3, 2, 1).reshape(NTA, 128, 16 * TTA))
        xr_h = np.ascontiguousarray(
            x[b, s * TC:(s + 1) * TC].reshape(TC, 16, 128).transpose(2, 1, 0).reshape(128, 16 * TC))
        wqkv_h = np.ascontiguousarray(
            Wq5[:, :, :, hg * 4:(hg + 1) * 4, :].transpose(1, 0, 3, 2, 4).reshape(128, 16 * 1536))
        bq_h = np.ascontiguousarray(
            bq3[:, hg * 4:(hg + 1) * 4, :].transpose(2, 1, 0).reshape(128, 12))
        in_maps.append({
            "xa": xa_h, "xr": xr_h, "wqkv": wqkv_h, "bqkv": bq_h,
            "wout": wout_h, "bout": bout_h, "wg": wg_h,
            "wgu": wgu_h, "wdn": wdn_h, "wsgu": wsgu_h, "wsdn": wsdn_h,
            "coff": np.array([[s * TC]], dtype=np.int32),
        })
    return in_maps


def kernel(**inputs):
    in_maps = _layout_inputs(inputs)
    nc = build_nc(DEBUG_TAPS)
    res = run_bass_kernel_spmd(nc, in_maps, core_ids=list(range(8)))
    out = np.empty((B, S, D), np.float32)
    for c in range(8):
        b, s = c // 4, c % 4
        out[b, s * TC:(s + 1) * TC, :] = res.results[c]["out"]
    if DEBUG_TAPS:
        kernel.last_results = res
    return out
